# version 02
# baseline (speedup 1.0000x reference)
import math
import numpy as np

import concourse.bass as bass
import concourse.mybir as mybir
from concourse.bass_utils import run_bass_kernel_spmd
from concourse.alu_op_type import AluOpType as OP

F32 = mybir.dt.float32
BF16 = mybir.dt.bfloat16
I32 = mybir.dt.int32
FP8E4 = mybir.dt.float8e4
AF = mybir.ActivationFunctionType
AX = mybir.AxisListType.X

S = 2048
D = 2048
DFF = 8192
NH = 16
HD = 128
NKV = 4
KVD = NKV * HD
FQ = D + 2 * KVD
NT = S // 128
ND = D // 128
NF = DFF // 128
LN2C = float(np.float32(math.log(2.0)))
RLN2 = float(np.float32(1.0 / math.log(2.0)))
LN6 = float(np.float32(math.log(6.0)))
LN64 = float(np.float32(math.log(64.0)))
L10K64 = float(-math.log(10000.0) / 64.0)
EPS_LN = 1e-5
EPS_ACT = 1e-5
EPS_W = 1e-8
EPS_I8 = 1e-5
F32EPS = float(np.finfo(np.float32).eps)
ISQHD = float(np.float32(1.0 / math.sqrt(HD)))
PI2_HI = float(np.float32(6.2831854820251465))
PI2_LO = float(np.float32(-1.7484556000744487e-07))
HALF_PI = float(np.float32(math.pi / 2))
I2PI = float(np.float32(1.0 / (2 * math.pi)))

ENGS = ("sync", "scalar", "vector", "tensor", "gpsimd")
DEV_TAIL = "none"
DEV_DBG_SERIAL = False


class Thunk:
    __slots__ = ("op", "a", "k", "inc")

    def __init__(self, op, a, k):
        self.op, self.a, self.k, self.inc = op, a, k, None

    def then_inc(self, sem, n):
        self.inc = (sem, n)
        return self


class EngQ:
    def __init__(self):
        self.L = []

    def __getattr__(self, op):
        if op.startswith("_"):
            raise AttributeError(op)

        def call(*a, **k):
            t = Thunk(op, a, k)
            self.L.append(t)
            return t
        return call

    def replay(self, e):
        drainable = {"tensor_scalar", "tensor_tensor", "scalar_tensor_tensor",
                     "tensor_copy", "tensor_reduce", "activation", "reciprocal",
                     "bn_aggr", "select", "tensor_tensor_reduce"}

        def is_col_ap(v):
            try:
                return hasattr(v, "free_size") and v.free_size() <= 8
            except Exception:
                return False

        for t in self.L:
            if t.op in drainable:
                ins = list(t.a) + [v for kk, v in t.k.items() if kk != "out"]
                if any(is_col_ap(v) for v in ins):
                    e.drain()
            r = getattr(e, t.op)(*t.a, **t.k)
            if t.inc is not None:
                r.then_inc(*t.inc)


def build():
    nc = bass.Bass("TRN2", target_bir_lowering=False)
    dt = nc.dram_tensor

    x_in = dt("x", [S, D], F32, kind="ExternalInput").ap()
    out_h = dt("out", [S, D], F32, kind="ExternalOutput").ap()

    WqT = dt("WqT", [D, D], F32, kind="ExternalInput").ap()
    WkT = dt("WkT", [D, D], F32, kind="ExternalInput").ap()
    WvT = dt("WvT", [D, D], F32, kind="ExternalInput").ap()
    WoT = dt("WoT", [D, D], F32, kind="ExternalInput").ap()
    bq_h = dt("bq", [1, D], F32, kind="ExternalInput").ap()
    bk_h = dt("bk", [1, D], F32, kind="ExternalInput").ap()
    bv_h = dt("bv", [1, D], F32, kind="ExternalInput").ap()
    bo_h = dt("bo", [1, D], F32, kind="ExternalInput").ap()
    W1T = dt("W1T", [D, DFF], F32, kind="ExternalInput").ap()
    W2T = dt("W2T", [DFF, D], F32, kind="ExternalInput").ap()
    b1_h = dt("b1", [1, DFF], F32, kind="ExternalInput").ap()
    b2_h = dt("b2", [1, D], F32, kind="ExternalInput").ap()
    WfT = dt("WfT", [D, FQ], F32, kind="ExternalInput").ap()
    Wf_r = dt("Wf_r", [FQ, D], F32, kind="ExternalInput").ap()
    Wo1T = dt("Wo1T", [D, D], F32, kind="ExternalInput").ap()
    Wo1_r = dt("Wo1_r", [D, D], F32, kind="ExternalInput").ap()
    qgain_h = dt("qgain", [1, NH], F32, kind="ExternalInput").ap()
    Wg_r = dt("Wg_r", [DFF, D], F32, kind="ExternalInput").ap()
    Wu_r = dt("Wu_r", [DFF, D], F32, kind="ExternalInput").ap()
    Wd_r = dt("Wd_r", [D, DFF], F32, kind="ExternalInput").ap()

    X1 = dt("X1", [S, D], F32, kind="Internal").ap()
    GSP = dt("GSP", [S, DFF], F32, kind="Internal").ap()
    FFN0 = dt("FFN0", [S, D], F32, kind="Internal").ap()
    ATT1 = dt("ATT1", [S, D], F32, kind="Internal").ap()
    YSP = dt("YSP", [S, D], BF16, kind="Internal").ap()
    QINT = dt("QINT", [S, DFF], BF16, kind="Internal").ap()
    QTs = dt("QTs", [D, S], BF16, kind="Internal").ap()
    KTs = dt("KTs", [D, S], BF16, kind="Internal").ap()
    VTs = dt("VTs", [D, S], BF16, kind="Internal").ap()
    W2Q = dt("W2Q", [DFF, D], BF16, kind="Internal").ap()
    WFQ = dt("WFQ", [D, FQ], BF16, kind="Internal").ap()
    WOQ = dt("WOQ", [D, D], BF16, kind="Internal").ap()
    WGQ = dt("WGQ", [DFF, D], BF16, kind="Internal").ap()
    WUQ = dt("WUQ", [DFF, D], BF16, kind="Internal").ap()
    WDQ = dt("WDQ", [D, DFF], BF16, kind="Internal").ap()

    ctxs = []

    def sbt(name, shape, dtype):
        cm = nc.sbuf_tensor(name, list(shape), dtype)
        t = cm.__enter__()
        ctxs.append(cm)
        return t

    def pstt(name, shape, dtype=F32):
        cm = nc.psum_tensor(name, list(shape), dtype)
        t = cm.__enter__()
        ctxs.append(cm)
        return t

    AR1 = sbt("AR1", [128, 32768], BF16)
    AR2 = sbt("AR2", [128, 16384], BF16)
    WR = sbt("WR", [128, 10240], BF16)
    wldA = sbt("wldA", [128, 2048], F32)
    wldB = sbt("wldB", [128, 2048], F32)
    tmpA = sbt("tmpA", [128, 2048], F32)
    tmpB = sbt("tmpB", [128, 2048], F32)
    tmpHH = sbt("tmpHH", [128, 4096], BF16)
    tmpF8 = sbt("tmpF8", [128, 2048], FP8E4)
    vext = sbt("vext", [128, NT * 132], BF16)
    colA = sbt("colA", [128, 64], F32)
    colB = sbt("colB", [128, 64], F32)
    colC = sbt("colC", [128, 64], F32)
    colD = sbt("colD", [128, 64], F32)
    colE = sbt("colE", [128, 64], F32)
    colI = sbt("colI", [128, 64], I32)
    pidx = sbt("pidx", [128, 1], F32)
    stat = sbt("stat", [128, NT * 8], F32)
    gsum = sbt("gsum", [128, NT], F32)
    gsq = sbt("gsq", [128, NT], F32)
    sqscr = sbt("sqscr", [128, 512], F32)
    gmax = sbt("gmax", [128, NT], F32)
    gmin = sbt("gmin", [128, NT], F32)
    invf = sbt("invf", [128, 64], F32)
    ropeC = sbt("ropeC", [128, NT * 64], BF16)
    ropeS = sbt("ropeS", [128, NT * 64], BF16)
    srow = sbt("srow", [1, S], BF16)
    srow2 = sbt("srow2", [1, S], BF16)
    rbufA = sbt("rbufA", [1, 512], BF16)
    rbufB = sbt("rbufB", [1, 512], BF16)
    rbufR = sbt("rbufR", [1, 128], F32)
    onesM = sbt("onesM", [1, 128], F32)
    onesMb = sbt("onesMb", [1, 128], BF16)
    ones_r = sbt("ones_r", [1, 512], BF16)
    onesc_f = sbt("onesc_f", [128, 1], F32)
    eyeb = sbt("eyeb", [128, 128], BF16)
    eyef = sbt("eyef", [128, 128], F32)
    bdmask = sbt("bdmask", [128, 128], BF16)
    dgmask = sbt("dgmask", [128, 4 * 512], BF16)
    gainb = sbt("gainb", [128, NH], F32)
    scal = sbt("scal", [128, 16], F32)
    xtile = sbt("xtile", [128, 2048], F32)
    wsA = sbt("wsA", [128, FQ], BF16)
    wsB = sbt("wsB", [128, D], BF16)

    psum = [pstt(f"ps{i}", [128, 512], F32) for i in range(6)]
    psT = pstt("psT", [128, 512], BF16)
    psM = pstt("psM", [128, 512], F32)

    sems = {}
    for name in ["qS", "qA", "qG", "pe", "vec", "sc", "gp", "ph", "evd"]:
        sems[name] = nc.semaphore(name).__enter__()

    cnt = {n: 0 for n in sems}
    Q = {e: EngQ() for e in ENGS}
    qS, qA, qV, qT, qG = (Q[e] for e in ENGS)

    def inc(name, v=1):
        cnt[name] += v
        return cnt[name]

    def dma(qeng, out, in_, transpose=False):
        sem_name = {"sync": "qS", "scalar": "qA", "gpsimd": "qG"}[qeng]
        Q[qeng].dma_start(out=out, in_=in_, transpose=transpose).then_inc(
            sems[sem_name], 16)
        return inc(sem_name, 16)

    def wait(eng, sem_name, val=None):
        Q[eng].wait_ge(sems[sem_name], cnt[sem_name] if val is None else val)

    def sig(eng, th, sem_name="vec" if True else None, n=1):
        th.then_inc(sems[sem_name], n)
        return inc(sem_name, n)

    bar_n = [0]

    def barrier():
        bar_n[0] += 1
        tgt = 5 * bar_n[0]
        for e in ENGS:
            if e == "sync":
                wait(e, "qS")
            elif e == "scalar":
                wait(e, "qA")
                Q[e].drain()
            elif e == "gpsimd":
                wait(e, "qG")
                Q[e].drain()
            else:
                Q[e].drain()
            Q[e].sem_inc(sems["ph"], 1)
            Q[e].wait_ge(sems["ph"], tgt)

    bankfree = [("ph", 0)] * 8

    def bank_wait(i):
        sname, v = bankfree[i]
        Q["tensor"].wait_ge(sems[sname], v)

    def bank_mark(i, sem_name):
        bankfree[i] = (sem_name, cnt[sem_name])

    def ar1(dd, t0, w):
        return AR1[:, dd * S + t0: dd * S + t0 + w]

    IH = tmpHH[:].bitcast(I32)

    qV.memset(onesM[:], 1.0)
    qV.memset(onesMb[:], 1.0)
    qV.memset(ones_r[:], 1.0)
    qV.memset(onesc_f[:], 1.0)
    qV.memset(gmax[:], -3.0e38)
    qV.memset(gmin[:], 3.0e38)
    qV.memset(gsum[:], 0.0)
    qV.memset(gsq[:], 0.0)
    qV.memset(scal[:, 10:11], LN2C)
    qV.memset(scal[:, 11:12], -LN2C)
    qV.memset(scal[:, 12:13], LN6)
    qV.memset(scal[:, 13:14], LN64)
    qV.memset(scal[:, 14:15], -LN64)
    for _vc in range(NT):
        qV.memset(vext[:, _vc * 132 + 128:_vc * 132 + 129], 1.0)
    sig("vector", qV.drain(), "vec")

    qG.iota(out=pidx[:], pattern=[[0, 1]], base=0, channel_multiplier=1,
            allow_small_or_imprecise_dtypes=True)
    qG.iota(out=tmpB[:, 0:128], pattern=[[1, 128]], base=0, channel_multiplier=0,
            allow_small_or_imprecise_dtypes=True)
    qG.iota(out=tmpB[:, 512:1024], pattern=[[1, 512]], base=0, channel_multiplier=-1,
            allow_small_or_imprecise_dtypes=True)
    qG.iota(out=colC[0:1, 0:64], pattern=[[1, 64]], base=0, channel_multiplier=0,
            allow_small_or_imprecise_dtypes=True)
    qG.memset(eyeb[:], 1.0)
    qG.memset(eyef[:], 1.0)
    qG.drain()
    qG.affine_select(out=eyeb[:], in_=eyeb[:], pattern=[[1, 128]],
                     compare_op=OP.is_equal, fill=0.0, base=0, channel_multiplier=-1)
    qG.affine_select(out=eyef[:], in_=eyef[:], pattern=[[1, 128]],
                     compare_op=OP.is_equal, fill=0.0, base=0, channel_multiplier=-1)
    sig("gpsimd", qG.drain(), "gp")

    wait("vector", "gp")
    qV.tensor_scalar(out=colB[:, 0:1], in0=pidx[:], scalar1=1.0 / 16.0,
                     scalar2=-0.499, op0=OP.mult, op1=OP.add)
    qV.tensor_copy(out=colI[:, 0:1], in_=colB[:, 0:1])
    qV.tensor_copy(out=colB[:, 1:2], in_=colI[:, 0:1])
    qV.tensor_scalar(out=tmpA[:, 0:128], in0=tmpB[:, 0:128], scalar1=1.0 / 16.0,
                     scalar2=-0.499, op0=OP.mult, op1=OP.add)
    qV.tensor_copy(out=IH[:, 0:128], in_=tmpA[:, 0:128])
    qV.tensor_copy(out=tmpA[:, 128:256], in_=IH[:, 0:128])
    qV.tensor_tensor(out=bdmask[:], in0=tmpA[:, 128:256],
                     in1=colB[:, 1:2].broadcast_to((128, 128)), op=OP.is_equal)
    for a in range(4):
        qV.tensor_scalar(out=dgmask[:, a * 512:(a + 1) * 512], in0=tmpB[:, 512:1024],
                         scalar1=float(128 * a), scalar2=None, op0=OP.is_ge)
    sig("vector", qV.drain(), "vec")

    wait("tensor", "vec")
    bank_wait(7)
    qT.matmul(out=psM[:, 0:64], lhsT=onesM[:], rhs=colC[0:1, 0:64], start=True, stop=True)
    sig("tensor", qT.drain(), "pe")
    wait("scalar", "pe")
    qA.activation(out=invf[:], in_=psM[:, 0:64], func=AF.Exp, scale=L10K64)
    sc_invf = sig("scalar", qA.drain(), "sc")
    bank_mark(7, "sc")
    dma("sync", rbufR[0:1, 0:16], qgain_h[:, 0:NH])
    wait("tensor", "qS")
    bank_wait(7)
    qT.matmul(out=psM[:, 64:64 + NH], lhsT=onesM[:], rhs=rbufR[0:1, 0:16], start=True, stop=True)
    sig("tensor", qT.drain(), "pe")
    wait("vector", "pe")
    qV.tensor_copy(out=gainb[:], in_=psM[:, 64:64 + NH])
    v_g = sig("vector", qV.drain(), "vec")
    bank_mark(7, "vec")

    wait("vector", "sc", sc_invf)
    for tt in range(NT):
        qV.tensor_scalar(out=colD[:, 0:1], in0=pidx[:], scalar1=float(tt * 128),
                         scalar2=None, op0=OP.add)
        qV.drain()
        qV.tensor_scalar(out=colA[:, 0:64], in0=invf[:], scalar1=colD[:, 0:1],
                         scalar2=None, op0=OP.mult)
        qV.tensor_scalar(out=colE[:, 0:64], in0=colA[:, 0:64], scalar1=I2PI,
                         scalar2=None, op0=OP.mult)
        qV.tensor_copy(out=colI[:, 0:64], in_=colE[:, 0:64])
        qV.scalar_tensor_tensor(out=colE[:, 0:64], in0=colI[:, 0:64], scalar=-PI2_HI,
                                in1=colA[:, 0:64], op0=OP.mult, op1=OP.add)
        qV.scalar_tensor_tensor(out=colD[:, 0:64], in0=colI[:, 0:64], scalar=-PI2_LO,
                                in1=colE[:, 0:64], op0=OP.mult, op1=OP.add)
        qV.tensor_scalar(out=colB[:, 0:64], in0=colA[:, 0:64], scalar1=HALF_PI,
                         scalar2=None, op0=OP.add)
        qV.tensor_scalar(out=colE[:, 0:64], in0=colB[:, 0:64], scalar1=I2PI,
                         scalar2=None, op0=OP.mult)
        qV.tensor_copy(out=colI[:, 0:64], in_=colE[:, 0:64])
        qV.scalar_tensor_tensor(out=colE[:, 0:64], in0=colI[:, 0:64], scalar=-PI2_HI,
                                in1=colB[:, 0:64], op0=OP.mult, op1=OP.add)
        qV.scalar_tensor_tensor(out=colB[:, 0:64], in0=colI[:, 0:64], scalar=-PI2_LO,
                                in1=colE[:, 0:64], op0=OP.mult, op1=OP.add)
        v_r = sig("vector", qV.drain(), "vec")
        wait("scalar", "vec", v_r)
        qA.activation(out=ropeS[:, tt * 64:(tt + 1) * 64], in_=colD[:, 0:64], func=AF.Sin)
        qA.activation(out=ropeC[:, tt * 64:(tt + 1) * 64], in_=colB[:, 0:64], func=AF.Sin)
        s_r = sig("scalar", qA.drain(), "sc")
        wait("vector", "sc", s_r)

    qV.memset(colA[:, 2:4], 0.0)
    qV.drain()

    def stream_stat(src, n_rt, n_ch, acc_col):
        toks = []
        wait("sync", "vec")
        for i in range(n_rt * n_ch):
            rr, cc = divmod(i, n_ch)
            buf = (wldA, wldB)[i % 2]
            if len(toks) >= 2:
                wait("sync", "vec", toks[-2])
            tok = dma("sync", buf[:], src[rr * 128:(rr + 1) * 128, cc * 2048:(cc + 1) * 2048])
            wait("vector", "qS", tok)
            qV.tensor_reduce(out=colB[:, 8:9], in_=buf[:], axis=AX, op=OP.add,
                             apply_absolute_value=True)
            qV.tensor_tensor(out=acc_col, in0=acc_col, in1=colB[:, 8:9], op=OP.add)
            toks.append(sig("vector", qV.drain(), "vec"))

    stream_stat(W1T, ND, 4, colA[:, 2:3])
    stream_stat(W2T, NF, 1, colA[:, 3:4])
    v_now = cnt["vec"]
    wait("tensor", "vec", v_now)
    bank_wait(7)
    qT.matmul(out=psM[0:1, 128:129], lhsT=colA[:, 2:3], rhs=onesc_f[:], start=True, stop=True)
    qT.matmul(out=psM[0:1, 129:130], lhsT=colA[:, 3:4], rhs=onesc_f[:], start=True, stop=True)
    sig("tensor", qT.drain(), "pe")
    wait("vector", "pe")
    qV.tensor_scalar(out=scal[0:1, 8:9], in0=psM[0:1, 128:129],
                     scalar1=1.0 / (D * DFF), scalar2=EPS_W, op0=OP.mult, op1=OP.max)
    qV.tensor_scalar(out=scal[0:1, 9:10], in0=psM[0:1, 129:130],
                     scalar1=1.0 / (D * DFF), scalar2=EPS_W, op0=OP.mult, op1=OP.max)
    v_now = sig("vector", qV.drain(), "vec")
    wait("tensor", "vec", v_now)
    qT.matmul(out=psM[:, 130:131], lhsT=onesM[:], rhs=scal[0:1, 8:9], start=True, stop=True)
    qT.matmul(out=psM[:, 131:132], lhsT=onesM[:], rhs=scal[0:1, 9:10], start=True, stop=True)
    sig("tensor", qT.drain(), "pe")
    wait("vector", "pe")
    qV.tensor_copy(out=scal[:, 0:1], in_=psM[:, 130:131])
    qV.tensor_copy(out=scal[:, 2:3], in_=psM[:, 131:132])
    qV.drain()
    qV.reciprocal(out=scal[:, 1:2], in_=scal[:, 0:1])
    qV.reciprocal(out=scal[:, 3:4], in_=scal[:, 2:3])
    sig("vector", qV.drain(), "vec")
    bank_mark(7, "vec")

    g_toks = []
    for rt in range(NF):
        buf = (wldA, wldB)[rt % 2]
        if rt >= 2:
            wait("sync", "vec", v_w2q[rt - 2])
        tok = dma("sync", buf[:], W2T[rt * 128:(rt + 1) * 128, :])
        wait("vector", "qS", tok)
        if rt >= 2:
            wait("vector", "qG", g_toks[rt - 2])
        ob = WR[:, (rt % 2) * 2048:(rt % 2) * 2048 + 2048]
        qV.tensor_scalar(out=tmpA[:], in0=buf[:], scalar1=scal[:, 3:4], scalar2=None,
                         op0=OP.mult)
        qV.tensor_copy(out=IH[:], in_=tmpA[:])
        qV.tensor_scalar(out=ob, in0=IH[:], scalar1=1.0, scalar2=-1.0,
                         op0=OP.min, op1=OP.max)
        vtok = sig("vector", qV.drain(), "vec")
        if rt == 0:
            v_w2q = []
        v_w2q.append(vtok)
        wait("gpsimd", "vec", vtok)
        g_toks.append(dma("gpsimd", W2Q[rt * 128:(rt + 1) * 128, :], ob))

    barrier()

    for tt in range(NT):
        tok = dma("sync", xtile[:], x_in[tt * 128:(tt + 1) * 128, :])
        wait("vector", "qS", tok)
        for c in range(4):
            qV.bn_stats(out=tmpB[:, c * 6:c * 6 + 6], in_=xtile[:, c * 512:(c + 1) * 512])
        qV.drain()
        qV.bn_aggr(out=colA[:, 0:2], in_=tmpB[:, 0:24])
        qV.drain()
        qV.tensor_scalar(out=colB[:, 0:1], in0=colA[:, 1:2], scalar1=EPS_LN,
                         scalar2=None, op0=OP.add)
        qV.drain()
        qV.reciprocal(out=colB[:, 1:2], in_=colB[:, 0:1])
        v1 = sig("vector", qV.drain(), "vec")
        wait("scalar", "vec", v1)
        qA.activation(out=colB[:, 2:3], in_=colB[:, 1:2], func=AF.Sqrt)
        s1 = sig("scalar", qA.drain(), "sc")
        wait("vector", "sc", s1)
        qV.tensor_scalar(out=tmpA[:], in0=xtile[:], scalar1=colA[:, 0:1],
                         scalar2=colB[:, 2:3], op0=OP.subtract, op1=OP.mult)
        for c in range(4):
            qV.bn_stats(out=tmpB[:, c * 6:c * 6 + 6], in_=tmpA[:, c * 512:(c + 1) * 512])
        qV.tensor_reduce(out=colD[:, 0:1], in_=tmpA[:], axis=AX, op=OP.max)
        qV.tensor_reduce(out=colD[:, 1:2], in_=tmpA[:], axis=AX, op=OP.min)
        qV.drain()
        qV.bn_aggr(out=colC[:, 0:2], in_=tmpB[:, 0:24])
        qV.drain()
        qV.tensor_scalar(out=colB[:, 0:1], in0=colC[:, 1:2], scalar1=EPS_LN,
                         scalar2=None, op0=OP.add)
        qV.drain()
        qV.reciprocal(out=colB[:, 1:2], in_=colB[:, 0:1])
        v2 = sig("vector", qV.drain(), "vec")
        wait("scalar", "vec", v2)
        qA.activation(out=colC[:, 2:3], in_=colB[:, 1:2], func=AF.Sqrt)
        s2 = sig("scalar", qA.drain(), "sc")
        wait("vector", "sc", s2)
        qV.tensor_tensor(out=colE[:, 0:1], in0=colD[:, 0:1], in1=colC[:, 0:1], op=OP.subtract)
        qV.tensor_tensor(out=colE[:, 1:2], in0=colC[:, 0:1], in1=colD[:, 1:2], op=OP.subtract)
        qV.tensor_tensor(out=colE[:, 2:3], in0=colE[:, 0:1], in1=colE[:, 1:2], op=OP.max)
        qV.tensor_tensor(out=colE[:, 3:4], in0=colE[:, 2:3], in1=colC[:, 2:3], op=OP.mult)
        qV.tensor_scalar(out=colE[:, 4:5], in0=colE[:, 3:4], scalar1=EPS_ACT,
                         scalar2=None, op0=OP.max)
        qV.drain()
        qV.reciprocal(out=colE[:, 5:6], in_=colE[:, 4:5])
        qV.tensor_tensor(out=colD[:, 2:3], in0=colE[:, 4:5], in1=scal[:, 0:1], op=OP.mult)
        qV.tensor_scalar(out=stat[:, tt * 8:tt * 8 + 1], in0=colD[:, 2:3],
                         scalar1=1.0 / 127.0, scalar2=None, op0=OP.mult)
        qV.tensor_tensor(out=colD[:, 3:4], in0=colE[:, 5:6], in1=scal[:, 1:2], op=OP.mult)
        qV.tensor_scalar(out=colD[:, 4:5], in0=colD[:, 3:4], scalar1=127.0,
                         scalar2=None, op0=OP.mult)
        qV.tensor_tensor(out=colE[:, 6:7], in0=colE[:, 5:6], in1=colC[:, 2:3], op=OP.mult)
        qV.tensor_scalar(out=colE[:, 7:8], in0=colE[:, 6:7], scalar1=127.0,
                         scalar2=None, op0=OP.mult)
        qV.drain()
        qV.tensor_scalar(out=tmpB[:], in0=tmpA[:], scalar1=colC[:, 0:1],
                         scalar2=colE[:, 7:8], op0=OP.subtract, op1=OP.mult)
        qV.tensor_copy(out=IH[:], in_=tmpB[:])
        qV.tensor_scalar(out=WR[:, 0:2048], in0=IH[:], scalar1=127.0, scalar2=-128.0,
                         op0=OP.min, op1=OP.max)
        v3 = sig("vector", qV.drain(), "vec")
        wait("tensor", "vec", v3)
        bank_wait(7)
        qT.transpose(out=psM[0:1, 0:128], in_=colD[:, 4:5], identity=eyef[:])
        p1 = sig("tensor", qT.drain(), "pe")
        wait("vector", "pe", p1)
        qV.tensor_copy(out=srow[0:1, tt * 128:(tt + 1) * 128], in_=psM[0:1, 0:128])
        sig("vector", qV.drain(), "vec")
        bank_mark(7, "vec")
        wait("sync", "vec")
        for c in range(ND):
            dma("sync", AR1[:, c * S + tt * 128: c * S + (tt + 1) * 128],
                WR[:, c * 128:(c + 1) * 128], transpose=True)

    barrier()

    for fc in range(DFF // 512):
        pe_prev = cnt["pe"]
        wait("sync", "pe", pe_prev)
        b1tok = dma("gpsimd", rbufA[:], b1_h[:, fc * 512:(fc + 1) * 512])
        for ddc in range(ND):
            buf = (wldA, wldB)[ddc % 2]
            if ddc >= 2:
                wait("sync", "vec", rq_toks[ddc - 2])
            tok = dma("sync", buf[:, 0:512],
                      W1T[ddc * 128:(ddc + 1) * 128, fc * 512:(fc + 1) * 512])
            wait("vector", "qS", tok)
            if ddc == 0:
                wait("vector", "pe", pe_prev)
                rq_toks = []
            qV.tensor_scalar(out=tmpA[:, 0:512], in0=buf[:, 0:512], scalar1=scal[:, 1:2],
                             scalar2=None, op0=OP.mult)
            qV.tensor_copy(out=IH[:, 0:512], in_=tmpA[:, 0:512])
            qV.tensor_scalar(out=WR[:, ddc * 512:(ddc + 1) * 512], in0=IH[:, 0:512],
                             scalar1=1.0, scalar2=-1.0, op0=OP.min, op1=OP.max)
            rq_toks.append(sig("vector", qV.drain(), "vec"))
        v_rhs = cnt["vec"]
        wait("tensor", "vec", v_rhs)
        wait("tensor", "qG", b1tok)
        for tt in range(NT):
            b = tt % 4
            bank_wait(b)
            for ddc in range(ND):
                qT.matmul(out=psum[b][:], lhsT=AR1[:, ddc * S + tt * 128: ddc * S + (tt + 1) * 128],
                          rhs=WR[:, ddc * 512:(ddc + 1) * 512],
                          start=(ddc == 0), stop=False)
            mm = qT.matmul(out=psum[b][:], lhsT=srow[0:1, tt * 128:(tt + 1) * 128],
                           rhs=rbufA[:], start=False, stop=True)
            ptok = sig("tensor", mm, "pe")
            wait("scalar", "pe", ptok)
            gslot = tmpB[:, (tt % 4) * 512:(tt % 4) * 512 + 512]
            qA.activation(out=gslot, in_=psum[b][:], func=AF.Gelu,
                          scale=stat[:, tt * 8:tt * 8 + 1])
            stok = sig("scalar", qA.drain(), "sc")
            bank_mark(b, "sc")
            wait("vector", "sc", stok)
            qV.tensor_reduce(out=colA[:, 8:9], in_=gslot, axis=AX, op=OP.max)
            qV.tensor_reduce(out=colA[:, 9:10], in_=gslot, axis=AX, op=OP.min)
            qV.tensor_scalar(out=sqscr[:], in0=gslot, scalar1=1.0, scalar2=0.0,
                             op0=OP.mult, op1=OP.add, accum_out=colA[:, 10:11])
            qV.scalar_tensor_tensor(out=sqscr[:], in0=gslot, scalar=1.0, in1=gslot,
                                    op0=OP.mult, op1=OP.mult, accum_out=colA[:, 11:12])
            qV.drain()
            qV.tensor_tensor(out=gmax[:, tt:tt + 1], in0=gmax[:, tt:tt + 1],
                             in1=colA[:, 8:9], op=OP.max)
            qV.tensor_tensor(out=gmin[:, tt:tt + 1], in0=gmin[:, tt:tt + 1],
                             in1=colA[:, 9:10], op=OP.min)
            qV.tensor_tensor(out=gsum[:, tt:tt + 1], in0=gsum[:, tt:tt + 1],
                             in1=colA[:, 10:11], op=OP.add)
            qV.tensor_tensor(out=gsq[:, tt:tt + 1], in0=gsq[:, tt:tt + 1],
                             in1=colA[:, 11:12], op=OP.add)
            vtok = sig("vector", qV.drain(), "vec")
            wait("gpsimd", "vec", vtok)
            dma("gpsimd", GSP[tt * 128:(tt + 1) * 128, fc * 512:(fc + 1) * 512], gslot)
            if tt >= 3:
                wait("scalar", "qG")

    barrier()

    for tt in range(NT):
        qV.tensor_scalar(out=colA[:, 0:1], in0=gsum[:, tt:tt + 1], scalar1=1.0 / DFF,
                         scalar2=None, op0=OP.mult)
        qV.tensor_scalar(out=colA[:, 2:3], in0=gsq[:, tt:tt + 1], scalar1=1.0 / DFF,
                         scalar2=None, op0=OP.mult)
        qV.drain()
        qV.tensor_tensor(out=colA[:, 3:4], in0=colA[:, 0:1], in1=colA[:, 0:1], op=OP.mult)
        qV.tensor_tensor(out=colA[:, 1:2], in0=colA[:, 2:3], in1=colA[:, 3:4],
                         op=OP.subtract)
        qV.tensor_scalar(out=colB[:, 0:1], in0=colA[:, 1:2], scalar1=EPS_LN,
                         scalar2=None, op0=OP.add)
        qV.drain()
        qV.reciprocal(out=colB[:, 1:2], in_=colB[:, 0:1])
        v1 = sig("vector", qV.drain(), "vec")
        wait("scalar", "vec", v1)
        qA.activation(out=colB[:, 2:3], in_=colB[:, 1:2], func=AF.Sqrt)
        s1 = sig("scalar", qA.drain(), "sc")
        wait("vector", "sc", s1)
        qV.tensor_tensor(out=colE[:, 0:1], in0=gmax[:, tt:tt + 1], in1=colA[:, 0:1],
                         op=OP.subtract)
        qV.tensor_tensor(out=colE[:, 1:2], in0=colA[:, 0:1], in1=gmin[:, tt:tt + 1],
                         op=OP.subtract)
        qV.tensor_tensor(out=colE[:, 2:3], in0=colE[:, 0:1], in1=colE[:, 1:2], op=OP.max)
        qV.tensor_tensor(out=colE[:, 3:4], in0=colE[:, 2:3], in1=colB[:, 2:3], op=OP.mult)
        qV.tensor_scalar(out=colE[:, 4:5], in0=colE[:, 3:4], scalar1=EPS_ACT,
                         scalar2=None, op0=OP.max)
        qV.drain()
        qV.reciprocal(out=colE[:, 5:6], in_=colE[:, 4:5])
        qV.drain()
        qV.tensor_scalar(out=colE[:, 6:7], in0=colE[:, 5:6], scalar1=127.0,
                         scalar2=None, op0=OP.mult)
        qV.tensor_tensor(out=stat[:, tt * 8 + 1:tt * 8 + 2], in0=colE[:, 6:7],
                         in1=colB[:, 2:3], op=OP.mult)
        qV.tensor_copy(out=stat[:, tt * 8 + 2:tt * 8 + 3], in_=colA[:, 0:1])
        qV.tensor_tensor(out=colD[:, 0:1], in0=colE[:, 4:5], in1=scal[:, 2:3], op=OP.mult)
        qV.tensor_scalar(out=stat[:, tt * 8 + 3:tt * 8 + 4], in0=colD[:, 0:1],
                         scalar1=1.0 / 127.0, scalar2=None, op0=OP.mult)
        qV.tensor_tensor(out=colD[:, 1:2], in0=colE[:, 6:7], in1=scal[:, 3:4], op=OP.mult)
        v2 = sig("vector", qV.drain(), "vec")
        wait("tensor", "vec", v2)
        bank_wait(7)
        qT.transpose(out=psM[0:1, 0:128], in_=colD[:, 1:2], identity=eyef[:])
        p1 = sig("tensor", qT.drain(), "pe")
        wait("vector", "pe", p1)
        qV.tensor_copy(out=srow2[0:1, tt * 128:(tt + 1) * 128], in_=psM[0:1, 0:128])
        sig("vector", qV.drain(), "vec")
        bank_mark(7, "vec")

    barrier()

    WSTG = WR[:, 8192:10240]
    prev_vq = [0, 0]
    for grp in range(8):
        pe_guard = cnt["pe"]
        vq_toks = list(prev_vq)
        for ti in range(2):
            tt = grp * 2 + ti
            for c in range(4):
                i_lin = ti * 4 + c
                buf = (wldA, wldB)[i_lin % 2]
                wait("sync", "vec", vq_toks[i_lin])
                tok = dma("sync", buf[:], GSP[tt * 128:(tt + 1) * 128, c * 2048:(c + 1) * 2048])
                wait("vector", "qS", tok)
                if ti == 0 and c == 0:
                    wait("vector", "pe", pe_guard)
                wait("vector", "qS")
                qV.tensor_scalar(out=tmpA[:], in0=buf[:], scalar1=stat[:, tt * 8 + 2:tt * 8 + 3],
                                 scalar2=stat[:, tt * 8 + 1:tt * 8 + 2], op0=OP.subtract, op1=OP.mult)
                qV.tensor_copy(out=IH[:], in_=tmpA[:])
                qV.tensor_scalar(out=WSTG[:], in0=IH[:], scalar1=127.0, scalar2=-128.0,
                                 op0=OP.min, op1=OP.max)
                vtok = sig("vector", qV.drain(), "vec")
                vq_toks.append(vtok)
                if i_lin == 6:
                    prev_vq[0] = vtok
                if i_lin == 7:
                    prev_vq[1] = vtok
                wait("sync", "vec", vtok)
                for j in range(16):
                    ddg = c * 16 + j
                    dma("sync", AR2[:, ddg * 256 + ti * 128: ddg * 256 + (ti + 1) * 128],
                        WSTG[:, j * 128:(j + 1) * 128], transpose=True)
        t_group_ready = cnt["qS"]
        if DEV_TAIL == "ar2g" and grp == 5:
            wait("gpsimd", "qS", t_group_ready)
            for rr in range(16):
                dma("gpsimd", out_h[rr * 128:(rr + 1) * 128, 0:1024],
                    AR2[:, rr * 1024:(rr + 1) * 1024])
            wait("gpsimd", "qG")
            wait("sync", "qG")
            wait("vector", "qG")
            wait("tensor", "qG")
        for foc in range(4):
            wait("gpsimd", "pe")
            b2tok = dma("gpsimd", rbufA[:], b2_h[:, foc * 512:(foc + 1) * 512])
            wait("tensor", "qS", t_group_ready)
            wait("tensor", "qG", b2tok)
            rtoks = []
            for ddg in range(NF):
                slot = WR[:, (ddg % 2) * 512:(ddg % 2) * 512 + 512]
                wait("sync", "pe")
                rtoks.append(dma("sync", slot, W2Q[ddg * 128:(ddg + 1) * 128,
                                                   foc * 512:(foc + 1) * 512]))
                wait("tensor", "qS", rtoks[-1])
                for ti in range(2):
                    if ddg == 0:
                        bank_wait(ti)
                    mmr = qT.matmul(out=psum[ti][:], lhsT=AR2[:, ddg * 256 + ti * 128: ddg * 256 + (ti + 1) * 128],
                                    rhs=slot, start=(ddg == 0), stop=False)
                    if ti == 1:
                        sig("tensor", mmr, "pe")
            for ti in range(2):
                tt = grp * 2 + ti
                mm = qT.matmul(out=psum[ti][:], lhsT=srow2[0:1, tt * 128:(tt + 1) * 128],
                               rhs=rbufA[:], start=False, stop=True)
                ptok = sig("tensor", mm, "pe")
                wait("vector", "pe", ptok)
                oslot = tmpB[:, (ti % 4) * 512:(ti % 4) * 512 + 512]
                qV.tensor_scalar(out=oslot, in0=psum[ti][:], scalar1=stat[:, tt * 8 + 3:tt * 8 + 4],
                                 scalar2=None, op0=OP.mult)
                vtok = sig("vector", qV.drain(), "vec")
                bank_mark(ti, "vec")
                wait("gpsimd", "vec", vtok)
                dma("gpsimd", FFN0[tt * 128:(tt + 1) * 128, foc * 512:(foc + 1) * 512], oslot)
            wait("vector", "qG")

    barrier()

    for tt in range(NT):
        tok = dma("sync", xtile[:], x_in[tt * 128:(tt + 1) * 128, :])
        wait("vector", "qS", tok)
        if tt > 0:
            wait("vector", "qS")
        for c in range(4):
            qV.bn_stats(out=tmpB[:, c * 6:c * 6 + 6], in_=xtile[:, c * 512:(c + 1) * 512])
        qV.drain()
        qV.bn_aggr(out=colA[:, 0:2], in_=tmpB[:, 0:24])
        qV.tensor_scalar(out=colB[:, 0:1], in0=colA[:, 1:2], scalar1=EPS_LN,
                         scalar2=None, op0=OP.add)
        qV.reciprocal(out=colB[:, 1:2], in_=colB[:, 0:1])
        v1 = sig("vector", qV.drain(), "vec")
        wait("scalar", "vec", v1)
        qA.activation(out=colB[:, 2:3], in_=colB[:, 1:2], func=AF.Sqrt)
        s1 = sig("scalar", qA.drain(), "sc")
        wait("vector", "sc", s1)
        qV.tensor_scalar(out=WSTG[:], in0=xtile[:], scalar1=colA[:, 0:1],
                         scalar2=colB[:, 2:3], op0=OP.subtract, op1=OP.mult)
        vtok = sig("vector", qV.drain(), "vec")
        wait("sync", "vec", vtok)
        for c in range(ND):
            dma("sync", AR1[:, c * S + tt * 128: c * S + (tt + 1) * 128],
                WSTG[:, c * 128:(c + 1) * 128], transpose=True)

    barrier()

    for (WT_h, b_h, dst) in ((WqT, bq_h, QTs), (WkT, bk_h, KTs), (WvT, bv_h, VTs)):
        for fc in range(4):
            btok = dma("gpsimd", rbufA[:], b_h[:, fc * 512:(fc + 1) * 512])
            for tc in range(4):
                for dd in range(ND):
                    slot = WR[:, (dd % 2) * 512:(dd % 2) * 512 + 512]
                    wait("gpsimd", "pe")
                    wtok = dma("gpsimd", slot,
                               WT_h[dd * 128:(dd + 1) * 128, fc * 512:(fc + 1) * 512])
                    wait("tensor", "qG", wtok)
                    for fi in range(4):
                        if dd == 0:
                            bank_wait(fi)
                        mmr = qT.matmul(out=psum[fi][:],
                                        lhsT=slot[:, fi * 128:(fi + 1) * 128],
                                        rhs=AR1[:, dd * S + tc * 512: dd * S + (tc + 1) * 512],
                                        start=(dd == 0), stop=False)
                        if fi == 3:
                            sig("tensor", mmr, "pe")
                wait("tensor", "qG", btok)
                for fi in range(4):
                    mm = qT.matmul(out=psum[fi][:], lhsT=rbufA[0:1, fi * 128:(fi + 1) * 128],
                                   rhs=ones_r[:], start=False, stop=True)
                    ptok = sig("tensor", mm, "pe")
                    wait("vector", "pe", ptok)
                    if not (fc == 0 and tc == 0 and fi == 0):
                        wait("vector", "qG")
                    oslot = tmpHH[:, (fi % 2) * 512:(fi % 2) * 512 + 512]
                    qV.tensor_copy(out=oslot, in_=psum[fi][:])
                    vtok = sig("vector", qV.drain(), "vec")
                    bank_mark(fi, "vec")
                    wait("gpsimd", "vec", vtok)
                    dma("gpsimd", dst[(fc * 4 + fi) * 128:(fc * 4 + fi + 1) * 128,
                                      tc * 512:(tc + 1) * 512], oslot)

    barrier()

    yeR = AR2[:, 12288:16384].rearrange("p (h t) -> p t h", h=16)
    qR = AR2[:, 0:4096].rearrange("p (h t) -> p t h", h=16)
    kR = AR2[:, 4096:8192].rearrange("p (h t) -> p t h", h=16)
    vR = AR2[:, 8192:12288].rearrange("p (h t) -> p t h", h=16)
    for e8 in range(8):
        pe_guard = cnt["pe"]
        v_guard = cnt["vec"]
        wait("sync", "pe", pe_guard)
        wait("sync", "vec", v_guard)
        for ft in range(16):
            dma("sync", AR2[:, 0 + ft * 256:0 + (ft + 1) * 256],
                QTs[ft * 128:(ft + 1) * 128, e8 * 256:(e8 + 1) * 256])
            dma("sync", AR2[:, 4096 + ft * 256:4096 + (ft + 1) * 256],
                KTs[ft * 128:(ft + 1) * 128, e8 * 256:(e8 + 1) * 256])
            dma("sync", AR2[:, 8192 + ft * 256:8192 + (ft + 1) * 256],
                VTs[ft * 128:(ft + 1) * 128, e8 * 256:(e8 + 1) * 256])
        t_loaded = cnt["qS"]
        wait("tensor", "qS", t_loaded)
        wait("vector", "qS", t_loaded)
        for g in range(32):
            t0 = g * 8
            bs = g % 2
            bc = 2 + (g % 2)
            qAP = qR[:, t0:t0 + 8, :]
            kAP = kR[:, t0:t0 + 8, :]
            vAP = vR[:, t0:t0 + 8, :]
            sl = (g % 2) * 1024
            E_s = tmpHH[:, sl + 0:sl + 128]
            A_s = tmpHH[:, sl + 128:sl + 256]
            AT_s = tmpHH[:, sl + 256:sl + 384]
            VG_s = tmpHH[:, sl + 384:sl + 512]
            CG_s = tmpHH[:, sl + 512:sl + 640]
            KG_s = tmpHH[:, sl + 640:sl + 768]
            VGT_s = tmpHH[:, sl + 768:sl + 896]
            QG_s = tmpHH[:, sl + 896:sl + 1024]
            qV.tensor_copy(out=KG_s, in_=kAP)
            qV.tensor_reduce(out=colB[:, 0:8], in_=vAP, axis=AX, op=OP.add)
            qV.drain()
            qV.tensor_copy(out=VGT_s.rearrange("p (t h) -> p t h", h=16),
                           in_=colB[:, 0:8].unsqueeze(2).broadcast_to((128, 8, 16)))
            qV.tensor_copy(out=QG_s, in_=qAP)
            vk = sig("vector", qV.drain(), "vec")
            wait("tensor", "vec", vk)
            bank_wait(bs)
            mm = qT.matmul(out=psum[bs][:, 0:128], lhsT=QG_s, rhs=KG_s, start=True, stop=True)
            p1 = sig("tensor", mm, "pe")
            wait("scalar", "pe", p1)
            qA.activation(out=E_s, in_=psum[bs][:, 0:128], func=AF.Exp, scale=ISQHD)
            s1 = sig("scalar", qA.drain(), "sc")
            bank_mark(bs, "sc")
            wait("vector", "sc", s1)
            qV.tensor_tensor(out=A_s, in0=E_s, in1=bdmask[:], op=OP.mult)
            qV.drain()
            qV.tensor_reduce(out=colA[:, 0:1], in_=A_s, axis=AX, op=OP.add)
            qV.reciprocal(out=colA[:, 1:2], in_=colA[:, 0:1])
            qV.tensor_scalar(out=A_s, in0=A_s, scalar1=colA[:, 1:2], scalar2=None,
                             op0=OP.mult)
            v1 = sig("vector", qV.drain(), "vec")
            wait("tensor", "vec", v1)
            bank_wait(6)
            qT.transpose(out=psT[:, 0:128], in_=A_s, identity=eyeb[:])
            qT.transpose(out=psT[:, 128:256], in_=VGT_s, identity=eyeb[:])
            p2 = sig("tensor", qT.drain(), "pe")
            wait("vector", "pe", p2)
            qV.tensor_copy(out=AT_s, in_=psT[:, 0:128])
            qV.tensor_copy(out=VG_s, in_=psT[:, 128:256])
            v2 = sig("vector", qV.drain(), "vec")
            bank_mark(6, "vec")
            wait("tensor", "vec", v2)
            bank_wait(bc)
            mm = qT.matmul(out=psum[bc][:, 0:128], lhsT=AT_s, rhs=VG_s, start=True, stop=True)
            p3 = sig("tensor", mm, "pe")
            wait("vector", "pe", p3)
            qV.tensor_copy(out=CG_s, in_=psum[bc][:, 0:128])
            v3 = sig("vector", qV.drain(), "vec")
            bank_mark(bc, "vec")
            wait("tensor", "vec", v3)
            bank_wait(6)
            qT.transpose(out=psT[:, 256:384], in_=CG_s, identity=eyeb[:])
            p4 = sig("tensor", qT.drain(), "pe")
            wait("vector", "pe", p4)
            qV.tensor_copy(out=yeR[:, t0:t0 + 8, :], in_=psT[:, 256:384])
            sig("vector", qV.drain(), "vec")
            bank_mark(6, "vec")
            if DEV_TAIL == "grp0" and e8 == 0 and g == 0:
                wait("gpsimd", "vec")
                dma("gpsimd", out_h[0:128, 0:128], QG_s)
                dma("gpsimd", out_h[0:128, 128:256], KG_s)
                dma("gpsimd", out_h[0:128, 256:384], E_s)
                dma("gpsimd", out_h[0:128, 384:512], tmpHH[:, sl + 128:sl + 256])
                dma("gpsimd", out_h[0:128, 512:640], tmpHH[:, sl + 384:sl + 512])
                dma("gpsimd", out_h[0:128, 640:768], tmpHH[:, sl + 512:sl + 640])
                dma("gpsimd", out_h[128:256, 0:2048], AR2[:, 12288:14336])
                dma("gpsimd", out_h[384:512, 0:2048], AR2[:, 0:2048])
                wait("gpsimd", "qG")
                for _e in ("sync", "vector", "tensor", "scalar"):
                    wait(_e, "qG")
        v_ye = cnt["vec"]
        for tt2 in range(2):
            rtt = e8 * 256 + tt2 * 128
            for foc in range(4):
                botok = dma("gpsimd", rbufB[:], bo_h[:, foc * 512:(foc + 1) * 512])
                xtok = dma("sync", wldA[:, 0:512], x_in[rtt:rtt + 128, foc * 512:(foc + 1) * 512])
                ftok = dma("sync", wldA[:, 512:1024], FFN0[rtt:rtt + 128, foc * 512:(foc + 1) * 512])
                for ff in range(16):
                    slot = WR[:, (ff % 2) * 512:(ff % 2) * 512 + 512]
                    wait("gpsimd", "pe")
                    wtok = dma("gpsimd", slot,
                               WoT[ff * 128:(ff + 1) * 128, foc * 512:(foc + 1) * 512])
                    wait("tensor", "qG", wtok)
                    if ff == 0:
                        wait("tensor", "vec", v_ye)
                        bank_wait(4 + tt2 % 2)
                    mmr = qT.matmul(out=psum[4 + tt2 % 2][:],
                                    lhsT=AR2[:, 12288 + ff * 256 + tt2 * 128: 12288 + ff * 256 + tt2 * 128 + 128],
                                    rhs=slot, start=(ff == 0), stop=False)
                    sig("tensor", mmr, "pe")
                wait("tensor", "qG", botok)
                mm = qT.matmul(out=psum[4 + tt2 % 2][:], lhsT=onesMb[:], rhs=rbufB[:],
                               start=False, stop=True)
                ptok = sig("tensor", mm, "pe")
                wait("vector", "pe", ptok)
                wait("vector", "qS", ftok)
                oslot = tmpB[:, (foc % 4) * 512:(foc % 4) * 512 + 512]
                qV.tensor_tensor(out=oslot, in0=psum[4 + tt2 % 2][:], in1=wldA[:, 0:512],
                                 op=OP.add)
                qV.tensor_tensor(out=oslot, in0=oslot, in1=wldA[:, 512:1024], op=OP.add)
                vtok = sig("vector", qV.drain(), "vec")
                bank_mark(4 + tt2 % 2, "vec")
                wait("gpsimd", "vec", vtok)
                dma("gpsimd", X1[rtt:rtt + 128, foc * 512:(foc + 1) * 512], oslot)
                wait("sync", "qG")

    barrier()

    def row_stats_bcast(src_r, n_rt, dst_ws, iws_region):
        for rt in range(n_rt):
            buf = (wldA, wldB)[rt % 2]
            wait("sync", "vec")
            tok = dma("sync", buf[:], src_r[rt * 128:(rt + 1) * 128, :])
            wait("vector", "qS", tok)
            qV.tensor_reduce(out=colA[:, 0:1], in_=buf[:], axis=AX, op=OP.max,
                             apply_absolute_value=True)
            qV.tensor_scalar(out=colA[:, 1:2], in0=colA[:, 0:1], scalar1=1.0 / 127.0,
                             scalar2=EPS_I8, op0=OP.mult, op1=OP.max)
            qV.reciprocal(out=colA[:, 2:3], in_=colA[:, 1:2])
            v1 = sig("vector", qV.drain(), "vec")
            wait("tensor", "vec", v1)
            bank_wait(7)
            qT.transpose(out=psM[0:1, 0:128], in_=colA[:, 1:2], identity=eyef[:])
            qT.transpose(out=psM[0:1, 128:256], in_=colA[:, 2:3], identity=eyef[:])
            p1 = sig("tensor", qT.drain(), "pe")
            wait("vector", "pe", p1)
            qV.tensor_copy(out=rbufR[0:1, 0:128], in_=psM[0:1, 0:128])
            qV.tensor_copy(out=rbufB[0:1, 0:128].bitcast(F32), in_=psM[0:1, 128:256]) \
                if False else qV.tensor_copy(out=colB[0:1, 0:64], in_=psM[0:1, 128:192])
            qV.tensor_copy(out=colC[0:1, 0:64], in_=psM[0:1, 192:256])
            v2 = sig("vector", qV.drain(), "vec")
            wait("tensor", "vec", v2)
            qT.matmul(out=psM[:, 256:384], lhsT=onesM[:], rhs=rbufR[0:1, 0:128],
                      start=True, stop=True)
            qT.matmul(out=psM[:, 384:448], lhsT=onesM[:], rhs=colB[0:1, 0:64],
                      start=True, stop=True)
            qT.matmul(out=psM[:, 448:512], lhsT=onesM[:], rhs=colC[0:1, 0:64],
                      start=True, stop=True)
            p2 = sig("tensor", qT.drain(), "pe")
            wait("vector", "pe", p2)
            qV.tensor_copy(out=dst_ws[:, rt * 128:(rt + 1) * 128], in_=psM[:, 256:384])
            qV.tensor_copy(out=iws_region[:, rt * 128:rt * 128 + 64], in_=psM[:, 384:448])
            qV.tensor_copy(out=iws_region[:, rt * 128 + 64:(rt + 1) * 128], in_=psM[:, 448:512])
            sig("vector", qV.drain(), "vec")
            bank_mark(7, "vec")

    iwsF = AR2[:, 0:FQ]
    iwsO = AR2[:, FQ:FQ + D]
    row_stats_bcast(Wf_r, FQ // 128, wsA, iwsF)
    row_stats_bcast(Wo1_r, ND, wsB, iwsO)

    def quant_int8_T(srcT, n_dd, width, iws_region, dstQ):
        for dd in range(n_dd):
            for c in range((width + 2047) // 2048):
                w = min(2048, width - c * 2048)
                buf = (wldA, wldB)[(dd + c) % 2]
                wait("sync", "vec")
                wait("sync", "qG")
                tok = dma("sync", buf[:, 0:w],
                          srcT[dd * 128:(dd + 1) * 128, c * 2048:c * 2048 + w])
                wait("vector", "qS", tok)
                qV.tensor_tensor(out=tmpA[:, 0:w], in0=buf[:, 0:w],
                                 in1=iws_region[:, c * 2048:c * 2048 + w], op=OP.mult)
                qV.tensor_copy(out=IH[:, 0:w], in_=tmpA[:, 0:w])
                qV.tensor_scalar(out=WSTG[:, 0:w], in0=IH[:, 0:w], scalar1=127.0,
                                 scalar2=-127.0, op0=OP.min, op1=OP.max)
                vtok = sig("vector", qV.drain(), "vec")
                wait("gpsimd", "vec", vtok)
                dma("gpsimd", dstQ[dd * 128:(dd + 1) * 128, c * 2048:c * 2048 + w],
                    WSTG[:, 0:w])
                wait("vector", "qG")

    quant_int8_T(WfT, ND, FQ, iwsF, WFQ)
    quant_int8_T(Wo1T, ND, D, iwsO, WOQ)

    barrier()

    def quant_mx_w(src_r, n_rt, width, dstQ):
        nch = width // 2048
        for rti in range(n_rt * nch):
            rt, ch = divmod(rti, nch)
            buf = (wldA, wldB)[rti % 2]
            wait("sync", "vec")
            wait("sync", "qG")
            tok = dma("sync", buf[:],
                      src_r[rt * 128:(rt + 1) * 128, ch * 2048:(ch + 1) * 2048])
            wait("vector", "qS", tok)
            qV.tensor_reduce(out=colA[:, 0:64],
                             in_=buf[:].rearrange("p (b k) -> p b k", k=32),
                             axis=AX, op=OP.max, apply_absolute_value=True)
            v1 = sig("vector", qV.drain(), "vec")
            wait("scalar", "vec", v1)
            qA.activation(out=colB[:, 0:64], in_=colA[:, 0:64], func=AF.Ln)
            s1 = sig("scalar", qA.drain(), "sc")
            wait("vector", "sc", s1)
            qV.tensor_scalar(out=colB[:, 0:64], in0=colB[:, 0:64], scalar1=RLN2,
                             scalar2=None, op0=OP.mult)
            qV.tensor_copy(out=colI[:, 0:64], in_=colB[:, 0:64])
            qV.tensor_tensor(out=colC[:, 0:64], in0=colI[:, 0:64], in1=colB[:, 0:64],
                             op=OP.is_gt)
            qV.tensor_tensor(out=colD[:, 0:64], in0=colI[:, 0:64], in1=colC[:, 0:64],
                             op=OP.subtract)
            qV.tensor_scalar(out=colD[:, 0:64], in0=colD[:, 0:64], scalar1=-2.0,
                             scalar2=127.0, op0=OP.add, op1=OP.min)
            qV.tensor_scalar(out=colD[:, 0:64], in0=colD[:, 0:64], scalar1=-127.0,
                             scalar2=None, op0=OP.max)
            qV.tensor_tensor(out=tmpA[:], in0=buf[:], in1=buf[:], op=OP.mult)
            v2 = sig("vector", qV.drain(), "vec")
            wait("scalar", "vec", v2)
            qA.activation(out=tmpB[:], in_=tmpA[:], func=AF.Ln)
            s2 = sig("scalar", qA.drain(), "sc")
            wait("vector", "sc", s2)
            qV.tensor_scalar(out=tmpB[:], in0=tmpB[:], scalar1=RLN2 * 0.5,
                             scalar2=-0.5, op0=OP.mult, op1=OP.add)
            qV.tensor_copy(out=IH[:], in_=tmpB[:])
            qV.tensor_tensor(out=tmpB[:].rearrange("p (b k) -> p b k", k=32),
                             in0=IH[:].rearrange("p (b k) -> p b k", k=32),
                             in1=colD[:, 0:64].unsqueeze(2).broadcast_to((128, 64, 32)),
                             op=OP.max)
            v3 = sig("vector", qV.drain(), "vec")
            wait("scalar", "vec", v3)
            qA.activation(out=tmpA[:], in_=tmpB[:], func=AF.Exp, scale=-LN2C, bias=scal[:, 10:11])
            qA.activation(out=colC[:, 0:64], in_=colD[:, 0:64], func=AF.Exp,
                          scale=LN2C, bias=scal[:, 12:13])
            s3 = sig("scalar", qA.drain(), "sc")
            wait("vector", "sc", s3)
            qV.tensor_tensor(out=tmpA[:], in0=buf[:], in1=tmpA[:], op=OP.mult)
            qV.tensor_copy(out=IH[:], in_=tmpA[:])
            v4 = sig("vector", qV.drain(), "vec")
            wait("scalar", "vec", v4)
            qA.activation(out=tmpB[:], in_=tmpB[:], func=AF.Exp, scale=LN2C, bias=scal[:, 11:12])
            s4 = sig("scalar", qA.drain(), "sc")
            wait("vector", "sc", s4)
            qV.tensor_scalar(out=colB[:, 0:64], in0=colC[:, 0:64], scalar1=-1.0,
                             scalar2=None, op0=OP.mult)
            qV.tensor_tensor(out=tmpA[:], in0=IH[:], in1=tmpB[:], op=OP.mult)
            qV.tensor_tensor(out=tmpA[:].rearrange("p (b k) -> p b k", k=32),
                             in0=tmpA[:].rearrange("p (b k) -> p b k", k=32),
                             in1=colC[:, 0:64].unsqueeze(2).broadcast_to((128, 64, 32)),
                             op=OP.min)
            qV.tensor_tensor(out=WSTG[:].rearrange("p (b k) -> p b k", k=32),
                             in0=tmpA[:].rearrange("p (b k) -> p b k", k=32),
                             in1=colB[:, 0:64].unsqueeze(2).broadcast_to((128, 64, 32)),
                             op=OP.max)
            vtok = sig("vector", qV.drain(), "vec")
            wait("gpsimd", "vec", vtok)
            dma("gpsimd", dstQ[rt * 128:(rt + 1) * 128, ch * 2048:(ch + 1) * 2048],
                WSTG[:])
            wait("vector", "qG")

    quant_mx_w(Wg_r, NF, D, WGQ)
    quant_mx_w(Wu_r, NF, D, WUQ)
    quant_mx_w(Wd_r, ND, DFF, WDQ)

    barrier()

    def mx_act_quant(src_f32, w, dst_bf16, nblk, dbg=None):
        kb = 32
        qV.tensor_reduce(out=colA[:, 0:nblk],
                         in_=src_f32.rearrange("p (b k) -> p b k", k=kb),
                         axis=AX, op=OP.max, apply_absolute_value=True)
        v1 = sig("vector", qV.drain(), "vec")
        wait("scalar", "vec", v1)
        qA.activation(out=colB[:, 0:nblk], in_=colA[:, 0:nblk], func=AF.Ln)
        s1 = sig("scalar", qA.drain(), "sc")
        wait("vector", "sc", s1)
        if dbg is not None:
            qV.tensor_copy(out=wsB[:, dbg:dbg + 8], in_=colA[:, 0:nblk])
            qV.tensor_copy(out=wsB[:, dbg + 8:dbg + 16], in_=colB[:, 0:nblk])
        qV.tensor_scalar(out=colB[:, 0:nblk], in0=colB[:, 0:nblk], scalar1=RLN2,
                         scalar2=None, op0=OP.mult)
        qV.tensor_copy(out=colI[:, 0:nblk], in_=colB[:, 0:nblk])
        qV.tensor_tensor(out=colC[:, 0:nblk], in0=colI[:, 0:nblk], in1=colB[:, 0:nblk],
                         op=OP.is_gt)
        qV.tensor_tensor(out=colD[:, 0:nblk], in0=colI[:, 0:nblk], in1=colC[:, 0:nblk],
                         op=OP.subtract)
        qV.tensor_scalar(out=colD[:, 0:nblk], in0=colD[:, 0:nblk], scalar1=-2.0,
                         scalar2=127.0, op0=OP.add, op1=OP.min)
        qV.tensor_scalar(out=colD[:, 0:nblk], in0=colD[:, 0:nblk], scalar1=-127.0,
                         scalar2=None, op0=OP.max)
        if dbg is not None:
            qV.drain()
            qV.tensor_copy(out=wsB[:, dbg + 16:dbg + 24], in_=colB[:, 0:nblk])
            qV.tensor_copy(out=wsB[:, dbg + 24:dbg + 32], in_=colI[:, 0:nblk])
            qV.tensor_copy(out=wsB[:, dbg + 32:dbg + 40], in_=colD[:, 0:nblk])
        v2 = sig("vector", qV.drain(), "vec")
        wait("scalar", "vec", v2)
        qA.activation(out=colB[:, 0:nblk], in_=colD[:, 0:nblk], func=AF.Exp,
                      scale=-LN2C, bias=scal[:, 14:15])
        qA.activation(out=colC[:, 0:nblk], in_=colD[:, 0:nblk], func=AF.Exp,
                      scale=LN2C, bias=scal[:, 13:14])
        s2 = sig("scalar", qA.drain(), "sc")
        wait("vector", "sc", s2)
        if dbg is not None:
            qV.tensor_copy(out=wsB[:, dbg + 40:dbg + 48], in_=colB[:, 0:nblk])
            qV.tensor_copy(out=wsB[:, dbg + 48:dbg + 56], in_=colC[:, 0:nblk])
        qV.tensor_tensor(out=tmpA[:, 0:w].rearrange("p (b k) -> p b k", k=kb),
                         in0=src_f32.rearrange("p (b k) -> p b k", k=kb),
                         in1=colB[:, 0:nblk].unsqueeze(2).broadcast_to((128, nblk, kb)),
                         op=OP.mult)
        qV.tensor_scalar(out=tmpA[:, 0:w], in0=tmpA[:, 0:w], scalar1=7.5 / 64.0,
                         scalar2=-7.5 / 64.0, op0=OP.min, op1=OP.max)
        qV.tensor_copy(out=tmpF8[:, 0:w], in_=tmpA[:, 0:w])
        qV.drain()
        qV.tensor_tensor(out=dst_bf16.rearrange("p (b k) -> p b k", k=kb),
                         in0=tmpF8[:, 0:w].rearrange("p (b k) -> p b k", k=kb),
                         in1=colC[:, 0:nblk].unsqueeze(2).broadcast_to((128, nblk, kb)),
                         op=OP.mult)
        return sig("vector", qV.drain(), "vec")

    for half in range(2):
        pe_g = cnt["pe"]
        for th in range(8):
            tt = half * 8 + th
            wait("sync", "vec")
            tok = dma("sync", xtile[:], X1[tt * 128:(tt + 1) * 128, :])
            wait("vector", "qS", tok)
            if th == 0:
                wait("vector", "pe", pe_g)
            for c in range(4):
                qV.bn_stats(out=tmpB[:, c * 6:c * 6 + 6], in_=xtile[:, c * 512:(c + 1) * 512])
            qV.drain()
            qV.bn_aggr(out=colE[:, 0:2], in_=tmpB[:, 0:24])
            qV.tensor_scalar(out=colE[:, 2:3], in0=colE[:, 1:2], scalar1=EPS_LN,
                             scalar2=None, op0=OP.add)
            qV.reciprocal(out=colE[:, 3:4], in_=colE[:, 2:3])
            v1 = sig("vector", qV.drain(), "vec")
            wait("scalar", "vec", v1)
            qA.activation(out=colE[:, 4:5], in_=colE[:, 3:4], func=AF.Sqrt)
            s1 = sig("scalar", qA.drain(), "sc")
            wait("vector", "sc", s1)
            qV.tensor_scalar(out=tmpB[:], in0=xtile[:], scalar1=colE[:, 0:1],
                             scalar2=colE[:, 4:5], op0=OP.subtract, op1=OP.mult)
            qV.drain()
            vtok = mx_act_quant(tmpB[:], 2048, WSTG[:], 64)
            wait("sync", "vec", vtok)
            for c in range(ND):
                dma("sync", AR2[:, c * 1024 + th * 128: c * 1024 + (th + 1) * 128],
                    WSTG[:, c * 128:(c + 1) * 128], transpose=True)
        t_half = cnt["qS"]
        wait("tensor", "qS", t_half)
        for fc in range(32):
            pe_now = cnt["pe"]
            wait("sync", "pe", pe_now)
            for dd in range(ND):
                for c4 in range(2):
                    dma("sync", WR[:, dd * 256 + c4 * 128: dd * 256 + (c4 + 1) * 128],
                        WGQ[fc * 256 + c4 * 128: fc * 256 + (c4 + 1) * 128,
                            dd * 128:(dd + 1) * 128], transpose=True)
                    dma("sync", WR[:, 4096 + dd * 256 + c4 * 128: 4096 + dd * 256 + (c4 + 1) * 128],
                        WUQ[fc * 256 + c4 * 128: fc * 256 + (c4 + 1) * 128,
                            dd * 128:(dd + 1) * 128], transpose=True)
            rhs_tok = cnt["qS"]
            wait("tensor", "qS", rhs_tok)
            for th in range(8):
                tt = half * 8 + th
                bg, bu = th % 2, 2 + th % 2
                if DEV_DBG_SERIAL:
                    bg, bu = 0, 2
                bank_wait(bg)
                bank_wait(bu)
                for dd in range(ND):
                    qT.matmul(out=psum[bg][:, 0:256],
                              lhsT=AR2[:, dd * 1024 + th * 128: dd * 1024 + (th + 1) * 128],
                              rhs=WR[:, dd * 256:(dd + 1) * 256],
                              start=(dd == 0), stop=(dd == ND - 1))
                    mmr = qT.matmul(out=psum[bu][:, 0:256],
                                    lhsT=AR2[:, dd * 1024 + th * 128: dd * 1024 + (th + 1) * 128],
                                    rhs=WR[:, 4096 + dd * 256: 4096 + (dd + 1) * 256],
                                    start=(dd == 0), stop=(dd == ND - 1))
                ptok = sig("tensor", mmr, "pe")
                wait("scalar", "pe", ptok)
                sgs = tmpA[:, 512:768] if DEV_DBG_SERIAL else \
                    tmpA[:, 512 + (th % 2) * 256: 512 + (th % 2) * 256 + 256]
                qA.activation(out=sgs, in_=psum[bg][:, 0:256], func=AF.Silu)
                stok = sig("scalar", qA.drain(), "sc")
                bank_mark(bg, "sc")
                wait("vector", "sc", stok)
                wait("vector", "qG")
                qV.tensor_tensor(out=tmpB[:, 0:256], in0=sgs, in1=psum[bu][:, 0:256],
                                 op=OP.mult)
                if DEV_TAIL == "gup" and half == 0 and fc == 0 and th < 2:
                    qV.tensor_copy(out=wsA[:, th * 1024: th * 1024 + 256],
                                   in_=psum[bg][:, 0:256])
                    qV.tensor_copy(out=wsA[:, th * 1024 + 256: th * 1024 + 512],
                                   in_=psum[bu][:, 0:256])
                    qV.tensor_copy(out=wsA[:, th * 1024 + 512: th * 1024 + 640],
                                   in_=AR2[:, th * 128: th * 128 + 128])
                    qV.tensor_copy(out=wsA[:, th * 1024 + 640: th * 1024 + 896],
                                   in_=WR[:, 0:256])
                vv = sig("vector", qV.drain(), "vec")
                bank_mark(bu, "vec")
                _dbg = 64 * th if (DEV_TAIL == "gup" and half == 0 and fc == 0
                                   and th < 2) else None
                vtok = mx_act_quant(tmpB[:, 0:256], 256, WSTG[:, 0:256], 8, dbg=_dbg)
                if DEV_TAIL == "gup" and half == 0 and fc == 0 and th < 2:
                    qV.tensor_copy(out=wsA[:, 2048 + th * 256: 2048 + th * 256 + 256],
                                   in_=WSTG[:, 0:256])
                    qV.tensor_copy(out=wsA[:, 2560 + th * 256: 2560 + th * 256 + 256],
                                   in_=tmpB[:, 0:256])
                wait("gpsimd", "vec", vtok)
                dma("gpsimd", QINT[tt * 128:(tt + 1) * 128, fc * 256:(fc + 1) * 256],
                    WSTG[:, 0:256])

    if DEV_TAIL == "qx":
        wait("gpsimd", "qS")
        for c in range(ND):
            dma("gpsimd", GSP[c * 128:(c + 1) * 128, 0:1024], AR2[:, c * 1024:(c + 1) * 1024])
        wait("gpsimd", "qG")
    elif DEV_TAIL == "gup":
        wait("gpsimd", "vec")
        dma("gpsimd", GSP[0:128, 0:2048], wsA[:, 0:2048])
        dma("gpsimd", GSP[128:256, 0:1024], wsA[:, 2048:3072])
        dma("gpsimd", GSP[128:256, 1024:1152], wsB[:, 0:128])
        wait("gpsimd", "qG")

    barrier()

    for tt in range(NT):
        wait("sync", "vec")
        tok = dma("sync", xtile[:], X1[tt * 128:(tt + 1) * 128, :])
        wait("vector", "qS", tok)
        for c in range(4):
            qV.bn_stats(out=tmpB[:, c * 6:c * 6 + 6], in_=xtile[:, c * 512:(c + 1) * 512])
        qV.drain()
        qV.bn_aggr(out=colE[:, 0:2], in_=tmpB[:, 0:24])
        qV.tensor_scalar(out=colE[:, 2:3], in0=colE[:, 1:2], scalar1=EPS_LN,
                         scalar2=None, op0=OP.add)
        qV.reciprocal(out=colE[:, 3:4], in_=colE[:, 2:3])
        v1 = sig("vector", qV.drain(), "vec")
        wait("scalar", "vec", v1)
        qA.activation(out=colE[:, 4:5], in_=colE[:, 3:4], func=AF.Sqrt)
        s1 = sig("scalar", qA.drain(), "sc")
        wait("vector", "sc", s1)
        qV.tensor_reduce(out=colD[:, 0:1], in_=xtile[:], axis=AX, op=OP.max)
        qV.tensor_reduce(out=colD[:, 1:2], in_=xtile[:], axis=AX, op=OP.min)
        qV.drain()
        qV.tensor_tensor(out=colD[:, 2:3], in0=colD[:, 0:1], in1=colE[:, 0:1], op=OP.subtract)
        qV.tensor_tensor(out=colD[:, 3:4], in0=colE[:, 0:1], in1=colD[:, 1:2], op=OP.subtract)
        qV.tensor_tensor(out=colD[:, 4:5], in0=colD[:, 2:3], in1=colD[:, 3:4], op=OP.max)
        qV.tensor_tensor(out=colD[:, 5:6], in0=colD[:, 4:5], in1=colE[:, 4:5], op=OP.mult)
        qV.tensor_scalar(out=stat[:, tt * 8 + 4:tt * 8 + 5], in0=colD[:, 5:6],
                         scalar1=1.0 / 127.0, scalar2=EPS_I8, op0=OP.mult, op1=OP.max)
        qV.drain()
        qV.reciprocal(out=colD[:, 6:7], in_=stat[:, tt * 8 + 4:tt * 8 + 5])
        qV.tensor_tensor(out=colD[:, 7:8], in0=colE[:, 4:5], in1=colD[:, 6:7], op=OP.mult)
        qV.drain()
        qV.tensor_scalar(out=tmpB[:], in0=xtile[:], scalar1=colE[:, 0:1],
                         scalar2=colD[:, 7:8], op0=OP.subtract, op1=OP.mult)
        qV.tensor_copy(out=IH[:], in_=tmpB[:])
        qV.tensor_scalar(out=WSTG[:], in0=IH[:], scalar1=127.0, scalar2=-127.0,
                         op0=OP.min, op1=OP.max)
        vtok = sig("vector", qV.drain(), "vec")
        wait("sync", "vec", vtok)
        for c in range(ND):
            dma("sync", AR1[:, c * S + tt * 128: c * S + (tt + 1) * 128],
                WSTG[:, c * 128:(c + 1) * 128], transpose=True)

    barrier()

    ebuf = WR[:, 0:8192]
    kT_g = AR2[:, 0:2048]
    qT_h = AR2[:, 2048:4096]
    WFS = AR2[:, 4096:4096 + 16 * 128]

    def l1_proj(fslice, dst_kind, rms_rope, gain_col, out_cb):
        wait("sync", "pe")
        for dd in range(ND):
            dma("sync", WFS[:, dd * 128:(dd + 1) * 128],
                WFQ[dd * 128:(dd + 1) * 128, fslice:fslice + 128])
        wtok = cnt["qS"]
        wait("tensor", "qS", wtok)
        for ttg in range(4):
            for ti in range(4):
                tt = ttg * 4 + ti
                if ti == 0:
                    bank_wait(4)
                for dd in range(ND):
                    mmr = qT.matmul(out=psum[4][:, ti * 128:(ti + 1) * 128],
                                    lhsT=AR1[:, dd * S + tt * 128: dd * S + (tt + 1) * 128],
                                    rhs=WFS[:, dd * 128:(dd + 1) * 128],
                                    start=(dd == 0), stop=(dd == ND - 1))
            ptok = sig("tensor", mmr, "pe")
            wait("vector", "pe", ptok)
            for ti in range(4):
                tt = ttg * 4 + ti
                pslot = psum[4][:, ti * 128:(ti + 1) * 128]
                fslot = tmpA[:, (ti % 2) * 1024:(ti % 2) * 1024 + 128]
                qV.scalar_tensor_tensor(out=fslot, in0=pslot,
                                        scalar=stat[:, tt * 8 + 4:tt * 8 + 5],
                                        in1=wsA[:, fslice:fslice + 128],
                                        op0=OP.mult, op1=OP.mult)
                if rms_rope:
                    qV.drain()
                    qV.scalar_tensor_tensor(out=tmpB[:, 0:128], in0=fslot, scalar=1.0,
                                            in1=fslot, op0=OP.mult, op1=OP.mult,
                                            accum_out=colA[:, 0:1])
                    qV.tensor_scalar(out=colA[:, 1:2], in0=colA[:, 0:1],
                                     scalar1=1.0 / 128.0, scalar2=F32EPS,
                                     op0=OP.mult, op1=OP.add)
                    qV.reciprocal(out=colA[:, 2:3], in_=colA[:, 1:2])
                    vv = sig("vector", qV.drain(), "vec")
                    wait("scalar", "vec", vv)
                    qA.activation(out=colA[:, 3:4], in_=colA[:, 2:3], func=AF.Sqrt)
                    ss = sig("scalar", qA.drain(), "sc")
                    wait("vector", "sc", ss)
                    if gain_col is not None:
                        qV.tensor_tensor(out=colA[:, 4:5], in0=colA[:, 3:4],
                                         in1=gain_col, op=OP.mult)
                    else:
                        qV.tensor_copy(out=colA[:, 4:5], in_=colA[:, 3:4])
                    qV.drain()
                    qV.tensor_scalar(out=fslot, in0=fslot, scalar1=colA[:, 4:5],
                                     scalar2=None, op0=OP.mult)
                    cosb = ropeC[:, tt * 64:(tt + 1) * 64]
                    sinb = ropeS[:, tt * 64:(tt + 1) * 64]
                    qV.tensor_tensor(out=tmpB[:, 0:64], in0=fslot[:, 0:64], in1=cosb, op=OP.mult)
                    qV.tensor_tensor(out=tmpB[:, 64:128], in0=fslot[:, 64:128], in1=sinb, op=OP.mult)
                    qV.tensor_tensor(out=tmpB[:, 128:192], in0=fslot[:, 0:64], in1=sinb, op=OP.mult)
                    qV.tensor_tensor(out=tmpB[:, 192:256], in0=fslot[:, 64:128], in1=cosb, op=OP.mult)
                    qV.tensor_tensor(out=WSTG[:, 0:64], in0=tmpB[:, 0:64],
                                     in1=tmpB[:, 64:128], op=OP.add)
                    qV.tensor_tensor(out=WSTG[:, 64:128], in0=tmpB[:, 192:256],
                                     in1=tmpB[:, 128:192], op=OP.subtract)
                else:
                    qV.tensor_copy(out=WSTG[:, 0:128], in_=fslot)
                vtok = sig("vector", qV.drain(), "vec")
                out_cb(tt, WSTG[:, 0:128], vtok)
            bank_mark(4, "vec")

    kv_gain = None
    for kv in range(NKV):
        def k_cb(tt, slot, vtok):
            wait("sync", "vec", vtok)
            dma("sync", kT_g[:, tt * 128:(tt + 1) * 128], slot, transpose=True)
        l1_proj(D + kv * 128, "k", True, None, k_cb)
        def v_cb(tt, slot, vtok):
            wait("gpsimd", "vec", vtok)
            g1 = dma("gpsimd", YSP[tt * 128:(tt + 1) * 128, 0:128], slot)
            wait("vector", "qG", g1)
        def v_cb2(tt, slot, vtok):
            qV.tensor_copy(out=vext[:, tt * 132:tt * 132 + 128], in_=slot)
            sig("vector", qV.drain(), "vec")
        l1_proj(D + KVD + kv * 128, "v", False, None, v_cb2)
        for qh in range(kv * 4, kv * 4 + 4):
            def q_cb(tt, slot, vtok):
                wait("sync", "vec", vtok)
                dma("sync", qT_h[:, tt * 128:(tt + 1) * 128], slot, transpose=True)
            l1_proj(qh * 128, "q", True, gainb[:, qh:qh + 1], q_cb)
            wait("tensor", "qS")
            wait("tensor", "vec")
            for tqc in range(4):
                ntk = 4 * tqc + 4
                sc_base = cnt["sc"]
                for tkt in range(ntk):
                    b = tkt % 2
                    bank_wait(b)
                    mm = qT.matmul(out=psum[b][:], lhsT=kT_g[:, tkt * 128:(tkt + 1) * 128],
                                   rhs=qT_h[:, tqc * 512:(tqc + 1) * 512],
                                   start=True, stop=True)
                    ptok = sig("tensor", mm, "pe")
                    wait("scalar", "pe", ptok)
                    qA.activation(out=ebuf[:, tkt * 512:(tkt + 1) * 512], in_=psum[b][:],
                                  func=AF.Exp, scale=ISQHD)
                    stok = sig("scalar", qA.drain(), "sc")
                    bank_mark(b, "sc")
                    if tkt >= 4 * tqc:
                        a = tkt - 4 * tqc
                        wait("vector", "sc", stok)
                        qV.tensor_tensor(out=ebuf[:, tkt * 512:(tkt + 1) * 512],
                                         in0=ebuf[:, tkt * 512:(tkt + 1) * 512],
                                         in1=dgmask[:, a * 512:(a + 1) * 512], op=OP.mult)
                        sig("vector", qV.drain(), "vec")
                wait("tensor", "sc")
                wait("tensor", "vec")
                for ti in range(4):
                    Tg = tqc * 4 + ti
                    b2 = 2 + ti % 2
                    bank_wait(b2)
                    for tkt in range(Tg + 1):
                        mm = qT.matmul(out=psum[b2][:, 0:129],
                                       lhsT=ebuf[:, tkt * 512 + ti * 128: tkt * 512 + (ti + 1) * 128],
                                       rhs=vext[:, tkt * 132:tkt * 132 + 129],
                                       start=(tkt == 0), stop=(tkt == Tg))
                    ptok = sig("tensor", mm, "pe")
                    wait("vector", "pe", ptok)
                    qV.reciprocal(out=colA[:, 8:9], in_=psum[b2][:, 128:129])
                    qV.drain()
                    yslot = tmpA[:, 1536 + (ti % 2) * 128: 1536 + (ti % 2) * 128 + 128]
                    qV.tensor_scalar(out=yslot, in0=psum[b2][:, 0:128],
                                     scalar1=colA[:, 8:9], scalar2=None, op0=OP.mult)
                    vtok = sig("vector", qV.drain(), "vec")
                    bank_mark(b2, "vec")
                    wait("gpsimd", "vec", vtok)
                    g1 = dma("gpsimd", YSP[Tg * 128:(Tg + 1) * 128, qh * 128:(qh + 1) * 128],
                             yslot.bitcast(F32) if False else yslot)
                    wait("vector", "qG", g1)

    barrier()

    YB = WR[:, 0:4096]
    for tt in range(NT):
        wait("sync", "vec")
        tok = dma("sync", YB[:, 0:2048], YSP[tt * 128:(tt + 1) * 128, :])
        wait("vector", "qS", tok)
        qV.tensor_reduce(out=colA[:, 0:1], in_=YB[:, 0:2048], axis=AX, op=OP.max,
                         apply_absolute_value=True)
        qV.tensor_scalar(out=stat[:, tt * 8 + 5:tt * 8 + 6], in0=colA[:, 0:1],
                         scalar1=1.0 / 127.0, scalar2=EPS_I8, op0=OP.mult, op1=OP.max)
        qV.drain()
        qV.reciprocal(out=colA[:, 1:2], in_=stat[:, tt * 8 + 5:tt * 8 + 6])
        qV.drain()
        qV.tensor_scalar(out=tmpB[:], in0=YB[:, 0:2048], scalar1=colA[:, 1:2],
                         scalar2=None, op0=OP.mult)
        qV.tensor_copy(out=IH[:], in_=tmpB[:])
        qV.tensor_scalar(out=WSTG[:], in0=IH[:], scalar1=127.0, scalar2=-127.0,
                         op0=OP.min, op1=OP.max)
        vtok = sig("vector", qV.drain(), "vec")
        wait("sync", "vec", vtok)
        for c in range(ND):
            dma("sync", AR1[:, c * S + tt * 128: c * S + (tt + 1) * 128],
                WSTG[:, c * 128:(c + 1) * 128], transpose=True)

    for foc in range(4):
        for ttg in range(4):
            wait("sync", "pe")
            rtoks = []
            for dd in range(ND):
                slot = WR[:, 4096 + (dd % 2) * 512: 4096 + (dd % 2) * 512 + 512]
                wait("sync", "pe")
                rtoks.append(dma("sync", slot, WOQ[dd * 128:(dd + 1) * 128,
                                                   foc * 512:(foc + 1) * 512]))
                wait("tensor", "qS", rtoks[-1])
                for ti in range(4):
                    tt = ttg * 4 + ti
                    if dd == 0:
                        bank_wait(ti)
                    mmr = qT.matmul(out=psum[ti][:],
                                    lhsT=AR1[:, dd * S + tt * 128: dd * S + (tt + 1) * 128],
                                    rhs=slot, start=(dd == 0), stop=(dd == ND - 1))
                    if ti == 3:
                        sig("tensor", mmr, "pe")
            for ti in range(4):
                tt = ttg * 4 + ti
                wait("vector", "pe")
                oslot = tmpB[:, (ti % 4) * 512:(ti % 4) * 512 + 512]
                qV.scalar_tensor_tensor(out=oslot, in0=psum[ti][:],
                                        scalar=stat[:, tt * 8 + 5:tt * 8 + 6],
                                        in1=wsB[:, foc * 512:(foc + 1) * 512],
                                        op0=OP.mult, op1=OP.mult)
                vtok = sig("vector", qV.drain(), "vec")
                bank_mark(ti, "vec")
                wait("gpsimd", "vec", vtok)
                dma("gpsimd", ATT1[tt * 128:(tt + 1) * 128, foc * 512:(foc + 1) * 512], oslot)
            wait("vector", "qG")

    barrier()

    prev_q = [0] * 4
    for grp in range(4):
        pe_g = cnt["pe"]
        for ti in range(4):
            tt = grp * 4 + ti
            wait("sync", "vec")
            wait("sync", "pe", pe_g)
            tok = dma("sync", WR[:, 0:8192], QINT[tt * 128:(tt + 1) * 128, :])
            wait("sync", "qS", tok)
            for ddg in range(NF):
                dma("sync", AR1[:, ddg * 512 + ti * 128: ddg * 512 + (ti + 1) * 128],
                    WR[:, ddg * 128:(ddg + 1) * 128], transpose=True)
            sig("vector", qV.nop(), "vec")
        t_grp = cnt["qS"]
        for foc in range(4):
            wait("sync", "qS", t_grp)
            for ddg in range(NF):
                slot = WR[:, 8192 + (ddg % 2) * 512: 8192 + (ddg % 2) * 512 + 512]
                wait("sync", "pe")
                for c4 in range(4):
                    dma("sync", slot[:, c4 * 128:(c4 + 1) * 128],
                        WDQ[foc * 512 + c4 * 128: foc * 512 + (c4 + 1) * 128,
                            ddg * 128:(ddg + 1) * 128], transpose=True)
                wait("tensor", "qS")
                for ti in range(4):
                    if ddg == 0:
                        bank_wait(ti)
                    mmr = qT.matmul(out=psum[ti][:],
                                    lhsT=AR1[:, ddg * 512 + ti * 128: ddg * 512 + (ti + 1) * 128],
                                    rhs=slot, start=(ddg == 0), stop=(ddg == NF - 1))
                    if ti == 3:
                        sig("tensor", mmr, "pe")
            for ti in range(4):
                tt = grp * 4 + ti
                xtok = dma("sync", wldA[:, 0:512],
                           X1[tt * 128:(tt + 1) * 128, foc * 512:(foc + 1) * 512])
                atok = dma("sync", wldA[:, 512:1024],
                           ATT1[tt * 128:(tt + 1) * 128, foc * 512:(foc + 1) * 512])
                wait("vector", "pe")
                wait("vector", "qS", atok)
                oslot = tmpB[:, (ti % 4) * 512:(ti % 4) * 512 + 512]
                qV.tensor_tensor(out=oslot, in0=psum[ti][:], in1=wldA[:, 0:512], op=OP.add)
                qV.tensor_tensor(out=oslot, in0=oslot, in1=wldA[:, 512:1024], op=OP.add)
                vtok = sig("vector", qV.drain(), "vec")
                bank_mark(ti, "vec")
                wait("gpsimd", "vec", vtok)
                dma("gpsimd", out_h[tt * 128:(tt + 1) * 128, foc * 512:(foc + 1) * 512],
                    oslot)
                wait("sync", "qG")
            wait("vector", "qG")

    wait("gpsimd", "qG")


    src_map = {"x": x_in, "gsp": GSP, "x1": X1, "ffn0": FFN0, "att1": ATT1,
               "qts": None}
    if DEV_TAIL == "gsp":
        for tt in range(NT):
            dma("sync", out_h[tt * 128:(tt + 1) * 128, :],
                GSP[tt * 128:(tt + 1) * 128, 0:2048])
    elif DEV_TAIL in ("x1", "ffn0", "att1"):
        t = src_map[DEV_TAIL]
        for tt in range(NT):
            dma("sync", out_h[tt * 128:(tt + 1) * 128, :], t[tt * 128:(tt + 1) * 128, :])
    elif DEV_TAIL in ("qts", "kts", "vts"):
        _src = {"qts": QTs, "kts": KTs, "vts": VTs}[DEV_TAIL]
        for tt in range(NT):
            dma("gpsimd", out_h[tt * 128:(tt + 1) * 128, 0:2048],
                _src[tt * 128:(tt + 1) * 128, 0:2048])
        wait("gpsimd", "qG")
    elif DEV_TAIL in ("wgq", "wuq", "wdq"):
        _srcq = {"wgq": WGQ, "wuq": WUQ, "wdq": WDQ}[DEV_TAIL]
        for tt in range(NT):
            dma("gpsimd", out_h[tt * 128:(tt + 1) * 128, :], _srcq[tt * 128:(tt + 1) * 128, 0:2048])
        wait("gpsimd", "qG")
    elif DEV_TAIL == "qx":
        for tt in range(NT):
            dma("gpsimd", out_h[tt * 128:(tt + 1) * 128, 0:1024],
                GSP[tt * 128:(tt + 1) * 128, 0:1024])
        wait("gpsimd", "qG")
    elif DEV_TAIL == "gup":
        dma("gpsimd", out_h[0:128, 0:2048], GSP[0:128, 0:2048])
        dma("gpsimd", out_h[128:256, 0:1024], GSP[128:256, 0:1024])
        dma("gpsimd", out_h[128:256, 1024:1152], GSP[128:256, 1024:1152])
        wait("gpsimd", "qG")
    elif DEV_TAIL == "qint":
        for tt in range(NT):
            dma("gpsimd", out_h[tt * 128:(tt + 1) * 128, :], QINT[tt * 128:(tt + 1) * 128, 0:2048])
        wait("gpsimd", "qG")
    elif DEV_TAIL == "w2q":
        for tt in range(NT):
            dma("gpsimd", out_h[tt * 128:(tt + 1) * 128, :], W2Q[tt * 128:(tt + 1) * 128, :])
        wait("gpsimd", "qG")
    elif DEV_TAIL == "xq2b":
        for tt in range(8):
            wait("sync", "vec")
            wait("sync", "qG")
            tok = dma("sync", wldA[:], GSP[tt * 128:(tt + 1) * 128, 0:2048])
            wait("vector", "qS", tok)
            qV.tensor_scalar(out=tmpA[:], in0=wldA[:], scalar1=stat[:, tt * 8 + 2:tt * 8 + 3],
                             scalar2=stat[:, tt * 8 + 1:tt * 8 + 2], op0=OP.subtract, op1=OP.mult)
            qV.tensor_copy(out=IH[:], in_=tmpA[:])
            qV.tensor_scalar(out=tmpB[:], in0=IH[:], scalar1=127.0, scalar2=-128.0,
                             op0=OP.min, op1=OP.max)
            vtok = sig("vector", qV.drain(), "vec")
            wait("gpsimd", "vec", vtok)
            g1 = dma("gpsimd", out_h[tt * 128:(tt + 1) * 128, :], tmpB[:])
            g2 = dma("gpsimd", out_h[1024 + tt * 128:1024 + (tt + 1) * 128, :], wldA[:])
            wait("vector", "qG", g2)
    elif DEV_TAIL == "xq2":
        for tt in range(NT):
            wait("sync", "vec")
            wait("sync", "qG")
            tok = dma("sync", wldA[:], GSP[tt * 128:(tt + 1) * 128, 0:2048])
            wait("vector", "qS", tok)
            qV.tensor_scalar(out=tmpA[:], in0=wldA[:], scalar1=stat[:, tt * 8 + 2:tt * 8 + 3],
                             scalar2=stat[:, tt * 8 + 1:tt * 8 + 2], op0=OP.subtract, op1=OP.mult)
            qV.tensor_copy(out=IH[:], in_=tmpA[:])
            qV.tensor_scalar(out=tmpB[:], in0=IH[:], scalar1=127.0, scalar2=-128.0,
                             op0=OP.min, op1=OP.max)
            vtok = sig("vector", qV.drain(), "vec")
            wait("gpsimd", "vec", vtok)
            g2 = dma("gpsimd", out_h[tt * 128:(tt + 1) * 128, :], tmpB[:])
            wait("vector", "qG", g2)
    elif DEV_TAIL == "xq1t":
        for ddx in range(ND):
            dma("gpsimd", out_h[ddx * 128:(ddx + 1) * 128, :], AR1[:, ddx * S:(ddx + 1) * S])
        wait("gpsimd", "qG")
    elif DEV_TAIL == "srow":
        dma("gpsimd", out_h[0:1, 0:S], srow[:])
        dma("gpsimd", out_h[2:3, 0:S], srow2[:])
        dma("gpsimd", out_h[4:132, 0:NT * 8], stat[:])
        dma("gpsimd", out_h[140:268, 0:NT], gmax[:])
        dma("gpsimd", out_h[270:398, 0:NT], gmin[:])
        wait("gpsimd", "qG")
    elif DEV_TAIL == "ye":
        dma("gpsimd", out_h[0:128, 0:2048], AR2[:, 12288:14336])
        dma("gpsimd", out_h[128:256, 0:2048], AR2[:, 14336:16384])
        wait("gpsimd", "qG")
    elif DEV_TAIL in ("none", "ar2g", "grp0"):
        pass
    else:
        for tt in range(NT):
            dma("sync", out_h[tt * 128:(tt + 1) * 128, :], x_in[tt * 128:(tt + 1) * 128, :])
    wait("sync", "qS")

    blk = nc.Block().__enter__()
    for ename in ENGS:
        getattr(blk, ename)(lambda e, _q=Q[ename]: _q.replay(e))
    blk.__exit__(None, None, None)
    for cm in reversed(ctxs):
        cm.__exit__(None, None, None)
    return nc


def _stage(inputs):
    A = {k: np.ascontiguousarray(v) for k, v in inputs.items()}
    wf = np.concatenate([A["L1_attn_w_q"], A["L1_attn_w_k"], A["L1_attn_w_v"]], axis=0)
    common = dict(
        WqT=np.ascontiguousarray(A["L0_attn_Wq"].T), bq=A["L0_attn_bq"].reshape(1, D),
        WkT=np.ascontiguousarray(A["L0_attn_Wk"].T), bk=A["L0_attn_bk"].reshape(1, D),
        WvT=np.ascontiguousarray(A["L0_attn_Wv"].T), bv=A["L0_attn_bv"].reshape(1, D),
        WoT=np.ascontiguousarray(A["L0_attn_Wo"].T), bo=A["L0_attn_bo"].reshape(1, D),
        W1T=np.ascontiguousarray(A["L0_ffn_w1"].T), b1=A["L0_ffn_b1"].reshape(1, DFF),
        W2T=np.ascontiguousarray(A["L0_ffn_w2"].T), b2=A["L0_ffn_b2"].reshape(1, D),
        WfT=np.ascontiguousarray(wf.T), Wf_r=wf,
        Wo1T=np.ascontiguousarray(A["L1_attn_w_o"].T), Wo1_r=A["L1_attn_w_o"],
        qgain=A["L1_attn_q_gain"].reshape(1, NH),
        Wg_r=A["L1_ffn_w_gate"], Wu_r=A["L1_ffn_w_up"], Wd_r=A["L1_ffn_w_down"],
    )
    xs = A["L0_attn_x"]
    return [dict(common, x=np.ascontiguousarray(xs[i])) for i in range(8)]


def kernel(**inputs):
    nc = build()
    in_maps = _stage(inputs)
    res = run_bass_kernel_spmd(nc, in_maps, core_ids=list(range(8)))
    out = np.stack([np.asarray(r["out"]) for r in res.results], axis=0)
    return out.astype(np.float32)


# version 03
# speedup vs baseline: 1.2672x; 1.2672x over previous
import math
import numpy as np

import concourse.bass as bass
import concourse.mybir as mybir
from concourse.bass_utils import run_bass_kernel_spmd
from concourse.alu_op_type import AluOpType as OP

F32 = mybir.dt.float32
BF16 = mybir.dt.bfloat16
I32 = mybir.dt.int32
FP8E4 = mybir.dt.float8e4
AF = mybir.ActivationFunctionType
AX = mybir.AxisListType.X

S = 2048
D = 2048
DFF = 8192
NH = 16
HD = 128
NKV = 4
KVD = NKV * HD
FQ = D + 2 * KVD
NT = S // 128
ND = D // 128
NF = DFF // 128
LN2C = float(np.float32(math.log(2.0)))
RLN2 = float(np.float32(1.0 / math.log(2.0)))
LN6 = float(np.float32(math.log(6.0)))
LN64 = float(np.float32(math.log(64.0)))
L10K64 = float(-math.log(10000.0) / 64.0)
EPS_LN = 1e-5
EPS_ACT = 1e-5
EPS_W = 1e-8
EPS_I8 = 1e-5
F32EPS = float(np.finfo(np.float32).eps)
ISQHD = float(np.float32(1.0 / math.sqrt(HD)))
PI2_HI = float(np.float32(6.2831854820251465))
PI2_LO = float(np.float32(-1.7484556000744487e-07))
HALF_PI = float(np.float32(math.pi / 2))
I2PI = float(np.float32(1.0 / (2 * math.pi)))

ENGS = ("sync", "scalar", "vector", "tensor", "gpsimd")
DEV_TAIL = "none"
DEV_DBG_SERIAL = False


class Thunk:
    __slots__ = ("op", "a", "k", "inc")

    def __init__(self, op, a, k):
        self.op, self.a, self.k, self.inc = op, a, k, None

    def then_inc(self, sem, n):
        self.inc = (sem, n)
        return self


class EngQ:
    def __init__(self):
        self.L = []

    def __getattr__(self, op):
        if op.startswith("_"):
            raise AttributeError(op)

        def call(*a, **k):
            t = Thunk(op, a, k)
            self.L.append(t)
            return t
        return call

    def replay(self, e):
        drainable = {"tensor_scalar", "tensor_tensor", "scalar_tensor_tensor",
                     "tensor_copy", "tensor_reduce", "activation", "reciprocal",
                     "bn_aggr", "select", "tensor_tensor_reduce"}

        def is_col_ap(v):
            try:
                return hasattr(v, "free_size") and v.free_size() <= 8
            except Exception:
                return False

        for t in self.L:
            if t.op in drainable:
                ins = list(t.a) + [v for kk, v in t.k.items() if kk != "out"]
                if any(is_col_ap(v) for v in ins):
                    e.drain()
            r = getattr(e, t.op)(*t.a, **t.k)
            if t.inc is not None:
                r.then_inc(*t.inc)


def build():
    nc = bass.Bass("TRN2", target_bir_lowering=False)
    dt = nc.dram_tensor

    x_in = dt("x", [S, D], F32, kind="ExternalInput").ap()
    out_h = dt("out", [S, D], F32, kind="ExternalOutput").ap()

    WqT = dt("WqT", [D, D], F32, kind="ExternalInput").ap()
    WkT = dt("WkT", [D, D], F32, kind="ExternalInput").ap()
    WvT = dt("WvT", [D, D], F32, kind="ExternalInput").ap()
    WoT = dt("WoT", [D, D], F32, kind="ExternalInput").ap()
    bq_h = dt("bq", [1, D], F32, kind="ExternalInput").ap()
    bk_h = dt("bk", [1, D], F32, kind="ExternalInput").ap()
    bv_h = dt("bv", [1, D], F32, kind="ExternalInput").ap()
    bo_h = dt("bo", [1, D], F32, kind="ExternalInput").ap()
    W1T = dt("W1T", [D, DFF], F32, kind="ExternalInput").ap()
    W2T = dt("W2T", [DFF, D], F32, kind="ExternalInput").ap()
    b1_h = dt("b1", [1, DFF], F32, kind="ExternalInput").ap()
    b2_h = dt("b2", [1, D], F32, kind="ExternalInput").ap()
    WfT = dt("WfT", [D, FQ], F32, kind="ExternalInput").ap()
    Wf_r = dt("Wf_r", [FQ, D], F32, kind="ExternalInput").ap()
    Wo1T = dt("Wo1T", [D, D], F32, kind="ExternalInput").ap()
    Wo1_r = dt("Wo1_r", [D, D], F32, kind="ExternalInput").ap()
    qgain_h = dt("qgain", [1, NH], F32, kind="ExternalInput").ap()
    Wg_r = dt("Wg_r", [DFF, D], F32, kind="ExternalInput").ap()
    Wu_r = dt("Wu_r", [DFF, D], F32, kind="ExternalInput").ap()
    Wd_r = dt("Wd_r", [D, DFF], F32, kind="ExternalInput").ap()

    X1 = dt("X1", [S, D], F32, kind="Internal").ap()
    GSP = dt("GSP", [S, DFF], F32, kind="Internal").ap()
    FFN0 = dt("FFN0", [S, D], F32, kind="Internal").ap()
    ATT1 = dt("ATT1", [S, D], F32, kind="Internal").ap()
    YSP = dt("YSP", [S, D], BF16, kind="Internal").ap()
    QINT = dt("QINT", [S, DFF], BF16, kind="Internal").ap()
    QTs = dt("QTs", [D, S], BF16, kind="Internal").ap()
    KTs = dt("KTs", [D, S], BF16, kind="Internal").ap()
    VTs = dt("VTs", [D, S], BF16, kind="Internal").ap()
    W2Q = dt("W2Q", [DFF, D], BF16, kind="Internal").ap()
    WFQ = dt("WFQ", [D, FQ], BF16, kind="Internal").ap()
    WOQ = dt("WOQ", [D, D], BF16, kind="Internal").ap()
    WGQT = dt("WGQT", [D, DFF], BF16, kind="Internal").ap()
    WUQT = dt("WUQT", [D, DFF], BF16, kind="Internal").ap()
    WDQT = dt("WDQT", [DFF, D], BF16, kind="Internal").ap()

    ctxs = []

    def sbt(name, shape, dtype):
        cm = nc.sbuf_tensor(name, list(shape), dtype)
        t = cm.__enter__()
        ctxs.append(cm)
        return t

    def pstt(name, shape, dtype=F32):
        cm = nc.psum_tensor(name, list(shape), dtype)
        t = cm.__enter__()
        ctxs.append(cm)
        return t

    AR1 = sbt("AR1", [128, 32768], BF16)
    AR2 = sbt("AR2", [128, 16384], BF16)
    WR = sbt("WR", [128, 10240], BF16)
    wldA = sbt("wldA", [128, 2048], F32)
    wldB = sbt("wldB", [128, 2048], F32)
    tmpA = sbt("tmpA", [128, 2048], F32)
    tmpB = sbt("tmpB", [128, 2048], F32)
    tmpHH = sbt("tmpHH", [128, 4096], BF16)
    tmpF8 = sbt("tmpF8", [128, 2048], FP8E4)
    vext = sbt("vext", [128, NT * 132], BF16)
    colA = sbt("colA", [128, 64], F32)
    colB = sbt("colB", [128, 64], F32)
    colC = sbt("colC", [128, 64], F32)
    colD = sbt("colD", [128, 64], F32)
    colE = sbt("colE", [128, 64], F32)
    colI = sbt("colI", [128, 64], I32)
    pidx = sbt("pidx", [128, 1], F32)
    stat = sbt("stat", [128, NT * 8], F32)
    gsum = sbt("gsum", [128, NT], F32)
    gsq = sbt("gsq", [128, NT], F32)
    sqscr = sbt("sqscr", [128, 512], F32)
    gmax = sbt("gmax", [128, NT], F32)
    gmin = sbt("gmin", [128, NT], F32)
    invf = sbt("invf", [128, 64], F32)
    ropeC = sbt("ropeC", [128, NT * 64], BF16)
    ropeS = sbt("ropeS", [128, NT * 64], BF16)
    srow = sbt("srow", [1, S], BF16)
    srow2 = sbt("srow2", [1, S], BF16)
    rbufA = sbt("rbufA", [1, 512], BF16)
    rbufB = sbt("rbufB", [1, 512], BF16)
    rbufR = sbt("rbufR", [1, 128], F32)
    onesM = sbt("onesM", [1, 128], F32)
    onesMb = sbt("onesMb", [1, 128], BF16)
    ones_r = sbt("ones_r", [1, 512], BF16)
    onesc_f = sbt("onesc_f", [128, 1], F32)
    eyeb = sbt("eyeb", [128, 128], BF16)
    eyef = sbt("eyef", [128, 128], F32)
    bdmask = sbt("bdmask", [128, 128], BF16)
    dgmask = sbt("dgmask", [128, 4 * 512], BF16)
    gainb = sbt("gainb", [128, NH], F32)
    scal = sbt("scal", [128, 16], F32)
    xtile = sbt("xtile", [128, 2048], F32)
    wsA = sbt("wsA", [128, FQ], BF16)
    wsB = sbt("wsB", [128, D], BF16)

    psum = [pstt(f"ps{i}", [128, 512], F32) for i in range(6)]
    psT = pstt("psT", [128, 512], BF16)
    psM = pstt("psM", [128, 512], F32)

    sems = {}
    for name in ["qS", "qA", "qG", "pe", "vec", "sc", "gp", "ph", "evd"]:
        sems[name] = nc.semaphore(name).__enter__()

    cnt = {n: 0 for n in sems}
    Q = {e: EngQ() for e in ENGS}
    qS, qA, qV, qT, qG = (Q[e] for e in ENGS)

    def inc(name, v=1):
        cnt[name] += v
        return cnt[name]

    def dma(qeng, out, in_, transpose=False):
        sem_name = {"sync": "qS", "scalar": "qA", "gpsimd": "qG"}[qeng]
        Q[qeng].dma_start(out=out, in_=in_, transpose=transpose).then_inc(
            sems[sem_name], 16)
        return inc(sem_name, 16)

    def wait(eng, sem_name, val=None):
        Q[eng].wait_ge(sems[sem_name], cnt[sem_name] if val is None else val)

    def sig(eng, th, sem_name="vec" if True else None, n=1):
        th.then_inc(sems[sem_name], n)
        return inc(sem_name, n)

    bar_n = [0]

    def barrier():
        bar_n[0] += 1
        tgt = 5 * bar_n[0]
        for e in ENGS:
            if e == "sync":
                wait(e, "qS")
            elif e == "scalar":
                wait(e, "qA")
                Q[e].drain()
            elif e == "gpsimd":
                wait(e, "qG")
                Q[e].drain()
            else:
                Q[e].drain()
            Q[e].sem_inc(sems["ph"], 1)
            Q[e].wait_ge(sems["ph"], tgt)

    bankfree = [("ph", 0)] * 8

    def bank_wait(i):
        sname, v = bankfree[i]
        Q["tensor"].wait_ge(sems[sname], v)

    def bank_mark(i, sem_name):
        bankfree[i] = (sem_name, cnt[sem_name])

    def ar1(dd, t0, w):
        return AR1[:, dd * S + t0: dd * S + t0 + w]

    IH = tmpHH[:].bitcast(I32)

    qV.memset(onesM[:], 1.0)
    qV.memset(onesMb[:], 1.0)
    qV.memset(ones_r[:], 1.0)
    qV.memset(onesc_f[:], 1.0)
    qV.memset(gmax[:], -3.0e38)
    qV.memset(gmin[:], 3.0e38)
    qV.memset(gsum[:], 0.0)
    qV.memset(gsq[:], 0.0)
    qV.memset(scal[:, 10:11], LN2C)
    qV.memset(scal[:, 11:12], -LN2C)
    qV.memset(scal[:, 12:13], LN6)
    qV.memset(scal[:, 13:14], LN64)
    qV.memset(scal[:, 14:15], -LN64)
    for _vc in range(NT):
        qV.memset(vext[:, _vc * 132 + 128:_vc * 132 + 129], 1.0)
    sig("vector", qV.drain(), "vec")

    qG.iota(out=pidx[:], pattern=[[0, 1]], base=0, channel_multiplier=1,
            allow_small_or_imprecise_dtypes=True)
    qG.iota(out=tmpB[:, 0:128], pattern=[[1, 128]], base=0, channel_multiplier=0,
            allow_small_or_imprecise_dtypes=True)
    qG.iota(out=tmpB[:, 512:1024], pattern=[[1, 512]], base=0, channel_multiplier=-1,
            allow_small_or_imprecise_dtypes=True)
    qG.iota(out=colC[0:1, 0:64], pattern=[[1, 64]], base=0, channel_multiplier=0,
            allow_small_or_imprecise_dtypes=True)
    qG.memset(eyeb[:], 1.0)
    qG.memset(eyef[:], 1.0)
    qG.drain()
    qG.affine_select(out=eyeb[:], in_=eyeb[:], pattern=[[1, 128]],
                     compare_op=OP.is_equal, fill=0.0, base=0, channel_multiplier=-1)
    qG.affine_select(out=eyef[:], in_=eyef[:], pattern=[[1, 128]],
                     compare_op=OP.is_equal, fill=0.0, base=0, channel_multiplier=-1)
    sig("gpsimd", qG.drain(), "gp")

    wait("vector", "gp")
    qV.tensor_scalar(out=colB[:, 0:1], in0=pidx[:], scalar1=1.0 / 16.0,
                     scalar2=-0.499, op0=OP.mult, op1=OP.add)
    qV.tensor_copy(out=colI[:, 0:1], in_=colB[:, 0:1])
    qV.tensor_copy(out=colB[:, 1:2], in_=colI[:, 0:1])
    qV.tensor_scalar(out=tmpA[:, 0:128], in0=tmpB[:, 0:128], scalar1=1.0 / 16.0,
                     scalar2=-0.499, op0=OP.mult, op1=OP.add)
    qV.tensor_copy(out=IH[:, 0:128], in_=tmpA[:, 0:128])
    qV.tensor_copy(out=tmpA[:, 128:256], in_=IH[:, 0:128])
    qV.tensor_tensor(out=bdmask[:], in0=tmpA[:, 128:256],
                     in1=colB[:, 1:2].broadcast_to((128, 128)), op=OP.is_equal)
    for a in range(4):
        qV.tensor_scalar(out=dgmask[:, a * 512:(a + 1) * 512], in0=tmpB[:, 512:1024],
                         scalar1=float(128 * a), scalar2=None, op0=OP.is_ge)
    sig("vector", qV.drain(), "vec")

    wait("tensor", "vec")
    bank_wait(7)
    qT.matmul(out=psM[:, 0:64], lhsT=onesM[:], rhs=colC[0:1, 0:64], start=True, stop=True)
    sig("tensor", qT.drain(), "pe")
    wait("scalar", "pe")
    qA.activation(out=invf[:], in_=psM[:, 0:64], func=AF.Exp, scale=L10K64)
    sc_invf = sig("scalar", qA.drain(), "sc")
    bank_mark(7, "sc")
    dma("sync", rbufR[0:1, 0:16], qgain_h[:, 0:NH])
    wait("tensor", "qS")
    bank_wait(7)
    qT.matmul(out=psM[:, 64:64 + NH], lhsT=onesM[:], rhs=rbufR[0:1, 0:16], start=True, stop=True)
    sig("tensor", qT.drain(), "pe")
    wait("vector", "pe")
    qV.tensor_copy(out=gainb[:], in_=psM[:, 64:64 + NH])
    v_g = sig("vector", qV.drain(), "vec")
    bank_mark(7, "vec")

    wait("vector", "sc", sc_invf)
    for tt in range(NT):
        qV.tensor_scalar(out=colD[:, 0:1], in0=pidx[:], scalar1=float(tt * 128),
                         scalar2=None, op0=OP.add)
        qV.drain()
        qV.tensor_scalar(out=colA[:, 0:64], in0=invf[:], scalar1=colD[:, 0:1],
                         scalar2=None, op0=OP.mult)
        qV.tensor_scalar(out=colE[:, 0:64], in0=colA[:, 0:64], scalar1=I2PI,
                         scalar2=None, op0=OP.mult)
        qV.tensor_copy(out=colI[:, 0:64], in_=colE[:, 0:64])
        qV.scalar_tensor_tensor(out=colE[:, 0:64], in0=colI[:, 0:64], scalar=-PI2_HI,
                                in1=colA[:, 0:64], op0=OP.mult, op1=OP.add)
        qV.scalar_tensor_tensor(out=colD[:, 0:64], in0=colI[:, 0:64], scalar=-PI2_LO,
                                in1=colE[:, 0:64], op0=OP.mult, op1=OP.add)
        qV.tensor_scalar(out=colB[:, 0:64], in0=colA[:, 0:64], scalar1=HALF_PI,
                         scalar2=None, op0=OP.add)
        qV.tensor_scalar(out=colE[:, 0:64], in0=colB[:, 0:64], scalar1=I2PI,
                         scalar2=None, op0=OP.mult)
        qV.tensor_copy(out=colI[:, 0:64], in_=colE[:, 0:64])
        qV.scalar_tensor_tensor(out=colE[:, 0:64], in0=colI[:, 0:64], scalar=-PI2_HI,
                                in1=colB[:, 0:64], op0=OP.mult, op1=OP.add)
        qV.scalar_tensor_tensor(out=colB[:, 0:64], in0=colI[:, 0:64], scalar=-PI2_LO,
                                in1=colE[:, 0:64], op0=OP.mult, op1=OP.add)
        v_r = sig("vector", qV.drain(), "vec")
        wait("scalar", "vec", v_r)
        qA.activation(out=ropeS[:, tt * 64:(tt + 1) * 64], in_=colD[:, 0:64], func=AF.Sin)
        qA.activation(out=ropeC[:, tt * 64:(tt + 1) * 64], in_=colB[:, 0:64], func=AF.Sin)
        s_r = sig("scalar", qA.drain(), "sc")
        wait("vector", "sc", s_r)

    qV.memset(colA[:, 2:4], 0.0)
    qV.drain()

    def stream_stat(src, n_rt, n_ch, acc_col):
        toks = []
        wait("sync", "vec")
        for i in range(n_rt * n_ch):
            rr, cc = divmod(i, n_ch)
            buf = (wldA, wldB)[i % 2]
            if len(toks) >= 2:
                wait("sync", "vec", toks[-2])
            tok = dma("sync", buf[:], src[rr * 128:(rr + 1) * 128, cc * 2048:(cc + 1) * 2048])
            wait("vector", "qS", tok)
            qV.tensor_reduce(out=colB[:, 8:9], in_=buf[:], axis=AX, op=OP.add,
                             apply_absolute_value=True)
            qV.tensor_tensor(out=acc_col, in0=acc_col, in1=colB[:, 8:9], op=OP.add)
            toks.append(sig("vector", qV.drain(), "vec"))

    stream_stat(W1T, ND, 4, colA[:, 2:3])
    stream_stat(W2T, NF, 1, colA[:, 3:4])
    v_now = cnt["vec"]
    wait("tensor", "vec", v_now)
    bank_wait(7)
    qT.matmul(out=psM[0:1, 128:129], lhsT=colA[:, 2:3], rhs=onesc_f[:], start=True, stop=True)
    qT.matmul(out=psM[0:1, 129:130], lhsT=colA[:, 3:4], rhs=onesc_f[:], start=True, stop=True)
    sig("tensor", qT.drain(), "pe")
    wait("vector", "pe")
    qV.tensor_scalar(out=scal[0:1, 8:9], in0=psM[0:1, 128:129],
                     scalar1=1.0 / (D * DFF), scalar2=EPS_W, op0=OP.mult, op1=OP.max)
    qV.tensor_scalar(out=scal[0:1, 9:10], in0=psM[0:1, 129:130],
                     scalar1=1.0 / (D * DFF), scalar2=EPS_W, op0=OP.mult, op1=OP.max)
    v_now = sig("vector", qV.drain(), "vec")
    wait("tensor", "vec", v_now)
    qT.matmul(out=psM[:, 130:131], lhsT=onesM[:], rhs=scal[0:1, 8:9], start=True, stop=True)
    qT.matmul(out=psM[:, 131:132], lhsT=onesM[:], rhs=scal[0:1, 9:10], start=True, stop=True)
    sig("tensor", qT.drain(), "pe")
    wait("vector", "pe")
    qV.tensor_copy(out=scal[:, 0:1], in_=psM[:, 130:131])
    qV.tensor_copy(out=scal[:, 2:3], in_=psM[:, 131:132])
    qV.drain()
    qV.reciprocal(out=scal[:, 1:2], in_=scal[:, 0:1])
    qV.reciprocal(out=scal[:, 3:4], in_=scal[:, 2:3])
    sig("vector", qV.drain(), "vec")
    bank_mark(7, "vec")

    g_toks = []
    for rt in range(NF):
        buf = (wldA, wldB)[rt % 2]
        if rt >= 2:
            wait("sync", "vec", v_w2q[rt - 2])
        tok = dma("sync", buf[:], W2T[rt * 128:(rt + 1) * 128, :])
        wait("vector", "qS", tok)
        if rt >= 2:
            wait("vector", "qG", g_toks[rt - 2])
        ob = WR[:, (rt % 2) * 2048:(rt % 2) * 2048 + 2048]
        qV.tensor_scalar(out=tmpA[:], in0=buf[:], scalar1=scal[:, 3:4], scalar2=None,
                         op0=OP.mult)
        qV.tensor_copy(out=IH[:], in_=tmpA[:])
        qV.tensor_scalar(out=ob, in0=IH[:], scalar1=1.0, scalar2=-1.0,
                         op0=OP.min, op1=OP.max)
        vtok = sig("vector", qV.drain(), "vec")
        if rt == 0:
            v_w2q = []
        v_w2q.append(vtok)
        wait("gpsimd", "vec", vtok)
        g_toks.append(dma("gpsimd", W2Q[rt * 128:(rt + 1) * 128, :], ob))

    barrier()

    for tt in range(NT):
        tok = dma("sync", xtile[:], x_in[tt * 128:(tt + 1) * 128, :])
        wait("vector", "qS", tok)
        for c in range(4):
            qV.bn_stats(out=tmpB[:, c * 6:c * 6 + 6], in_=xtile[:, c * 512:(c + 1) * 512])
        qV.drain()
        qV.bn_aggr(out=colA[:, 0:2], in_=tmpB[:, 0:24])
        qV.drain()
        qV.tensor_scalar(out=colB[:, 0:1], in0=colA[:, 1:2], scalar1=EPS_LN,
                         scalar2=None, op0=OP.add)
        qV.drain()
        qV.reciprocal(out=colB[:, 1:2], in_=colB[:, 0:1])
        v1 = sig("vector", qV.drain(), "vec")
        wait("scalar", "vec", v1)
        qA.activation(out=colB[:, 2:3], in_=colB[:, 1:2], func=AF.Sqrt)
        s1 = sig("scalar", qA.drain(), "sc")
        wait("vector", "sc", s1)
        qV.tensor_scalar(out=tmpA[:], in0=xtile[:], scalar1=colA[:, 0:1],
                         scalar2=colB[:, 2:3], op0=OP.subtract, op1=OP.mult)
        for c in range(4):
            qV.bn_stats(out=tmpB[:, c * 6:c * 6 + 6], in_=tmpA[:, c * 512:(c + 1) * 512])
        qV.tensor_reduce(out=colD[:, 0:1], in_=tmpA[:], axis=AX, op=OP.max)
        qV.tensor_reduce(out=colD[:, 1:2], in_=tmpA[:], axis=AX, op=OP.min)
        qV.drain()
        qV.bn_aggr(out=colC[:, 0:2], in_=tmpB[:, 0:24])
        qV.drain()
        qV.tensor_scalar(out=colB[:, 0:1], in0=colC[:, 1:2], scalar1=EPS_LN,
                         scalar2=None, op0=OP.add)
        qV.drain()
        qV.reciprocal(out=colB[:, 1:2], in_=colB[:, 0:1])
        v2 = sig("vector", qV.drain(), "vec")
        wait("scalar", "vec", v2)
        qA.activation(out=colC[:, 2:3], in_=colB[:, 1:2], func=AF.Sqrt)
        s2 = sig("scalar", qA.drain(), "sc")
        wait("vector", "sc", s2)
        qV.tensor_tensor(out=colE[:, 0:1], in0=colD[:, 0:1], in1=colC[:, 0:1], op=OP.subtract)
        qV.tensor_tensor(out=colE[:, 1:2], in0=colC[:, 0:1], in1=colD[:, 1:2], op=OP.subtract)
        qV.tensor_tensor(out=colE[:, 2:3], in0=colE[:, 0:1], in1=colE[:, 1:2], op=OP.max)
        qV.tensor_tensor(out=colE[:, 3:4], in0=colE[:, 2:3], in1=colC[:, 2:3], op=OP.mult)
        qV.tensor_scalar(out=colE[:, 4:5], in0=colE[:, 3:4], scalar1=EPS_ACT,
                         scalar2=None, op0=OP.max)
        qV.drain()
        qV.reciprocal(out=colE[:, 5:6], in_=colE[:, 4:5])
        qV.tensor_tensor(out=colD[:, 2:3], in0=colE[:, 4:5], in1=scal[:, 0:1], op=OP.mult)
        qV.tensor_scalar(out=stat[:, tt * 8:tt * 8 + 1], in0=colD[:, 2:3],
                         scalar1=1.0 / 127.0, scalar2=None, op0=OP.mult)
        qV.tensor_tensor(out=colD[:, 3:4], in0=colE[:, 5:6], in1=scal[:, 1:2], op=OP.mult)
        qV.tensor_scalar(out=colD[:, 4:5], in0=colD[:, 3:4], scalar1=127.0,
                         scalar2=None, op0=OP.mult)
        qV.tensor_tensor(out=colE[:, 6:7], in0=colE[:, 5:6], in1=colC[:, 2:3], op=OP.mult)
        qV.tensor_scalar(out=colE[:, 7:8], in0=colE[:, 6:7], scalar1=127.0,
                         scalar2=None, op0=OP.mult)
        qV.drain()
        qV.tensor_scalar(out=tmpB[:], in0=tmpA[:], scalar1=colC[:, 0:1],
                         scalar2=colE[:, 7:8], op0=OP.subtract, op1=OP.mult)
        qV.tensor_copy(out=IH[:], in_=tmpB[:])
        qV.tensor_scalar(out=WR[:, 0:2048], in0=IH[:], scalar1=127.0, scalar2=-128.0,
                         op0=OP.min, op1=OP.max)
        v3 = sig("vector", qV.drain(), "vec")
        wait("tensor", "vec", v3)
        bank_wait(7)
        qT.transpose(out=psM[0:1, 0:128], in_=colD[:, 4:5], identity=eyef[:])
        p1 = sig("tensor", qT.drain(), "pe")
        wait("vector", "pe", p1)
        qV.tensor_copy(out=srow[0:1, tt * 128:(tt + 1) * 128], in_=psM[0:1, 0:128])
        sig("vector", qV.drain(), "vec")
        bank_mark(7, "vec")
        wait("sync", "vec")
        for c in range(ND):
            dma("sync", AR1[:, c * S + tt * 128: c * S + (tt + 1) * 128],
                WR[:, c * 128:(c + 1) * 128], transpose=True)

    barrier()

    for fc in range(DFF // 512):
        pe_prev = cnt["pe"]
        wait("sync", "pe", pe_prev)
        b1tok = dma("gpsimd", rbufA[:], b1_h[:, fc * 512:(fc + 1) * 512])
        for ddc in range(ND):
            buf = (wldA, wldB)[ddc % 2]
            if ddc >= 2:
                wait("sync", "vec", rq_toks[ddc - 2])
            tok = dma("sync", buf[:, 0:512],
                      W1T[ddc * 128:(ddc + 1) * 128, fc * 512:(fc + 1) * 512])
            wait("vector", "qS", tok)
            if ddc == 0:
                wait("vector", "pe", pe_prev)
                rq_toks = []
            qV.tensor_scalar(out=tmpA[:, 0:512], in0=buf[:, 0:512], scalar1=scal[:, 1:2],
                             scalar2=None, op0=OP.mult)
            qV.tensor_copy(out=IH[:, 0:512], in_=tmpA[:, 0:512])
            qV.tensor_scalar(out=WR[:, ddc * 512:(ddc + 1) * 512], in0=IH[:, 0:512],
                             scalar1=1.0, scalar2=-1.0, op0=OP.min, op1=OP.max)
            rq_toks.append(sig("vector", qV.drain(), "vec"))
        v_rhs = cnt["vec"]
        wait("tensor", "vec", v_rhs)
        wait("tensor", "qG", b1tok)
        for tt in range(NT):
            b = tt % 4
            bank_wait(b)
            for ddc in range(ND):
                qT.matmul(out=psum[b][:], lhsT=AR1[:, ddc * S + tt * 128: ddc * S + (tt + 1) * 128],
                          rhs=WR[:, ddc * 512:(ddc + 1) * 512],
                          start=(ddc == 0), stop=False)
            mm = qT.matmul(out=psum[b][:], lhsT=srow[0:1, tt * 128:(tt + 1) * 128],
                           rhs=rbufA[:], start=False, stop=True)
            ptok = sig("tensor", mm, "pe")
            wait("scalar", "pe", ptok)
            gslot = tmpB[:, (tt % 4) * 512:(tt % 4) * 512 + 512]
            qA.activation(out=gslot, in_=psum[b][:], func=AF.Gelu,
                          scale=stat[:, tt * 8:tt * 8 + 1])
            stok = sig("scalar", qA.drain(), "sc")
            bank_mark(b, "sc")
            wait("vector", "sc", stok)
            qV.tensor_reduce(out=colA[:, 8:9], in_=gslot, axis=AX, op=OP.max)
            qV.tensor_reduce(out=colA[:, 9:10], in_=gslot, axis=AX, op=OP.min)
            qV.tensor_scalar(out=sqscr[:], in0=gslot, scalar1=1.0, scalar2=0.0,
                             op0=OP.mult, op1=OP.add, accum_out=colA[:, 10:11])
            qV.scalar_tensor_tensor(out=sqscr[:], in0=gslot, scalar=1.0, in1=gslot,
                                    op0=OP.mult, op1=OP.mult, accum_out=colA[:, 11:12])
            qV.drain()
            qV.tensor_tensor(out=gmax[:, tt:tt + 1], in0=gmax[:, tt:tt + 1],
                             in1=colA[:, 8:9], op=OP.max)
            qV.tensor_tensor(out=gmin[:, tt:tt + 1], in0=gmin[:, tt:tt + 1],
                             in1=colA[:, 9:10], op=OP.min)
            qV.tensor_tensor(out=gsum[:, tt:tt + 1], in0=gsum[:, tt:tt + 1],
                             in1=colA[:, 10:11], op=OP.add)
            qV.tensor_tensor(out=gsq[:, tt:tt + 1], in0=gsq[:, tt:tt + 1],
                             in1=colA[:, 11:12], op=OP.add)
            vtok = sig("vector", qV.drain(), "vec")
            wait("gpsimd", "vec", vtok)
            dma("gpsimd", GSP[tt * 128:(tt + 1) * 128, fc * 512:(fc + 1) * 512], gslot)
            if tt >= 3:
                wait("scalar", "qG")

    barrier()

    for tt in range(NT):
        qV.tensor_scalar(out=colA[:, 0:1], in0=gsum[:, tt:tt + 1], scalar1=1.0 / DFF,
                         scalar2=None, op0=OP.mult)
        qV.tensor_scalar(out=colA[:, 2:3], in0=gsq[:, tt:tt + 1], scalar1=1.0 / DFF,
                         scalar2=None, op0=OP.mult)
        qV.drain()
        qV.tensor_tensor(out=colA[:, 3:4], in0=colA[:, 0:1], in1=colA[:, 0:1], op=OP.mult)
        qV.tensor_tensor(out=colA[:, 1:2], in0=colA[:, 2:3], in1=colA[:, 3:4],
                         op=OP.subtract)
        qV.tensor_scalar(out=colB[:, 0:1], in0=colA[:, 1:2], scalar1=EPS_LN,
                         scalar2=None, op0=OP.add)
        qV.drain()
        qV.reciprocal(out=colB[:, 1:2], in_=colB[:, 0:1])
        v1 = sig("vector", qV.drain(), "vec")
        wait("scalar", "vec", v1)
        qA.activation(out=colB[:, 2:3], in_=colB[:, 1:2], func=AF.Sqrt)
        s1 = sig("scalar", qA.drain(), "sc")
        wait("vector", "sc", s1)
        qV.tensor_tensor(out=colE[:, 0:1], in0=gmax[:, tt:tt + 1], in1=colA[:, 0:1],
                         op=OP.subtract)
        qV.tensor_tensor(out=colE[:, 1:2], in0=colA[:, 0:1], in1=gmin[:, tt:tt + 1],
                         op=OP.subtract)
        qV.tensor_tensor(out=colE[:, 2:3], in0=colE[:, 0:1], in1=colE[:, 1:2], op=OP.max)
        qV.tensor_tensor(out=colE[:, 3:4], in0=colE[:, 2:3], in1=colB[:, 2:3], op=OP.mult)
        qV.tensor_scalar(out=colE[:, 4:5], in0=colE[:, 3:4], scalar1=EPS_ACT,
                         scalar2=None, op0=OP.max)
        qV.drain()
        qV.reciprocal(out=colE[:, 5:6], in_=colE[:, 4:5])
        qV.drain()
        qV.tensor_scalar(out=colE[:, 6:7], in0=colE[:, 5:6], scalar1=127.0,
                         scalar2=None, op0=OP.mult)
        qV.tensor_tensor(out=stat[:, tt * 8 + 1:tt * 8 + 2], in0=colE[:, 6:7],
                         in1=colB[:, 2:3], op=OP.mult)
        qV.tensor_copy(out=stat[:, tt * 8 + 2:tt * 8 + 3], in_=colA[:, 0:1])
        qV.tensor_tensor(out=colD[:, 0:1], in0=colE[:, 4:5], in1=scal[:, 2:3], op=OP.mult)
        qV.tensor_scalar(out=stat[:, tt * 8 + 3:tt * 8 + 4], in0=colD[:, 0:1],
                         scalar1=1.0 / 127.0, scalar2=None, op0=OP.mult)
        qV.tensor_tensor(out=colD[:, 1:2], in0=colE[:, 6:7], in1=scal[:, 3:4], op=OP.mult)
        v2 = sig("vector", qV.drain(), "vec")
        wait("tensor", "vec", v2)
        bank_wait(7)
        qT.transpose(out=psM[0:1, 0:128], in_=colD[:, 1:2], identity=eyef[:])
        p1 = sig("tensor", qT.drain(), "pe")
        wait("vector", "pe", p1)
        qV.tensor_copy(out=srow2[0:1, tt * 128:(tt + 1) * 128], in_=psM[0:1, 0:128])
        sig("vector", qV.drain(), "vec")
        bank_mark(7, "vec")

    barrier()

    WSTG = WR[:, 8192:10240]
    prev_vq = [0, 0]
    for grp in range(8):
        pe_guard = cnt["pe"]
        vq_toks = list(prev_vq)
        for ti in range(2):
            tt = grp * 2 + ti
            for c in range(4):
                i_lin = ti * 4 + c
                buf = (wldA, wldB)[i_lin % 2]
                wait("sync", "vec", vq_toks[i_lin])
                tok = dma("sync", buf[:], GSP[tt * 128:(tt + 1) * 128, c * 2048:(c + 1) * 2048])
                wait("vector", "qS", tok)
                if ti == 0 and c == 0:
                    wait("vector", "pe", pe_guard)
                wait("vector", "qS")
                qV.tensor_scalar(out=tmpA[:], in0=buf[:], scalar1=stat[:, tt * 8 + 2:tt * 8 + 3],
                                 scalar2=stat[:, tt * 8 + 1:tt * 8 + 2], op0=OP.subtract, op1=OP.mult)
                qV.tensor_copy(out=IH[:], in_=tmpA[:])
                qV.tensor_scalar(out=WSTG[:], in0=IH[:], scalar1=127.0, scalar2=-128.0,
                                 op0=OP.min, op1=OP.max)
                vtok = sig("vector", qV.drain(), "vec")
                vq_toks.append(vtok)
                if i_lin == 6:
                    prev_vq[0] = vtok
                if i_lin == 7:
                    prev_vq[1] = vtok
                wait("sync", "vec", vtok)
                for j in range(16):
                    ddg = c * 16 + j
                    dma("sync", AR2[:, ddg * 256 + ti * 128: ddg * 256 + (ti + 1) * 128],
                        WSTG[:, j * 128:(j + 1) * 128], transpose=True)
        t_group_ready = cnt["qS"]
        if DEV_TAIL == "ar2g" and grp == 5:
            wait("gpsimd", "qS", t_group_ready)
            for rr in range(16):
                dma("gpsimd", out_h[rr * 128:(rr + 1) * 128, 0:1024],
                    AR2[:, rr * 1024:(rr + 1) * 1024])
            wait("gpsimd", "qG")
            wait("sync", "qG")
            wait("vector", "qG")
            wait("tensor", "qG")
        for foc in range(4):
            wait("gpsimd", "pe")
            b2tok = dma("gpsimd", rbufA[:], b2_h[:, foc * 512:(foc + 1) * 512])
            wait("tensor", "qS", t_group_ready)
            wait("tensor", "qG", b2tok)
            rtoks = []
            for ddg in range(NF):
                slot = WR[:, (ddg % 2) * 512:(ddg % 2) * 512 + 512]
                wait("sync", "pe")
                rtoks.append(dma("sync", slot, W2Q[ddg * 128:(ddg + 1) * 128,
                                                   foc * 512:(foc + 1) * 512]))
                wait("tensor", "qS", rtoks[-1])
                for ti in range(2):
                    if ddg == 0:
                        bank_wait(ti)
                    mmr = qT.matmul(out=psum[ti][:], lhsT=AR2[:, ddg * 256 + ti * 128: ddg * 256 + (ti + 1) * 128],
                                    rhs=slot, start=(ddg == 0), stop=False)
                    if ti == 1:
                        sig("tensor", mmr, "pe")
            for ti in range(2):
                tt = grp * 2 + ti
                mm = qT.matmul(out=psum[ti][:], lhsT=srow2[0:1, tt * 128:(tt + 1) * 128],
                               rhs=rbufA[:], start=False, stop=True)
                ptok = sig("tensor", mm, "pe")
                wait("vector", "pe", ptok)
                oslot = tmpB[:, (ti % 4) * 512:(ti % 4) * 512 + 512]
                qV.tensor_scalar(out=oslot, in0=psum[ti][:], scalar1=stat[:, tt * 8 + 3:tt * 8 + 4],
                                 scalar2=None, op0=OP.mult)
                vtok = sig("vector", qV.drain(), "vec")
                bank_mark(ti, "vec")
                wait("gpsimd", "vec", vtok)
                dma("gpsimd", FFN0[tt * 128:(tt + 1) * 128, foc * 512:(foc + 1) * 512], oslot)
            wait("vector", "qG")

    barrier()

    for tt in range(NT):
        tok = dma("sync", xtile[:], x_in[tt * 128:(tt + 1) * 128, :])
        wait("vector", "qS", tok)
        if tt > 0:
            wait("vector", "qS")
        for c in range(4):
            qV.bn_stats(out=tmpB[:, c * 6:c * 6 + 6], in_=xtile[:, c * 512:(c + 1) * 512])
        qV.drain()
        qV.bn_aggr(out=colA[:, 0:2], in_=tmpB[:, 0:24])
        qV.tensor_scalar(out=colB[:, 0:1], in0=colA[:, 1:2], scalar1=EPS_LN,
                         scalar2=None, op0=OP.add)
        qV.reciprocal(out=colB[:, 1:2], in_=colB[:, 0:1])
        v1 = sig("vector", qV.drain(), "vec")
        wait("scalar", "vec", v1)
        qA.activation(out=colB[:, 2:3], in_=colB[:, 1:2], func=AF.Sqrt)
        s1 = sig("scalar", qA.drain(), "sc")
        wait("vector", "sc", s1)
        qV.tensor_scalar(out=WSTG[:], in0=xtile[:], scalar1=colA[:, 0:1],
                         scalar2=colB[:, 2:3], op0=OP.subtract, op1=OP.mult)
        vtok = sig("vector", qV.drain(), "vec")
        wait("sync", "vec", vtok)
        for c in range(ND):
            dma("sync", AR1[:, c * S + tt * 128: c * S + (tt + 1) * 128],
                WSTG[:, c * 128:(c + 1) * 128], transpose=True)

    barrier()

    for (WT_h, b_h, dst) in ((WqT, bq_h, QTs), (WkT, bk_h, KTs), (WvT, bv_h, VTs)):
        for fc in range(4):
            btok = dma("gpsimd", rbufA[:], b_h[:, fc * 512:(fc + 1) * 512])
            for tc in range(4):
                for dd in range(ND):
                    slot = WR[:, (dd % 2) * 512:(dd % 2) * 512 + 512]
                    wait("gpsimd", "pe")
                    wtok = dma("gpsimd", slot,
                               WT_h[dd * 128:(dd + 1) * 128, fc * 512:(fc + 1) * 512])
                    wait("tensor", "qG", wtok)
                    for fi in range(4):
                        if dd == 0:
                            bank_wait(fi)
                        mmr = qT.matmul(out=psum[fi][:],
                                        lhsT=slot[:, fi * 128:(fi + 1) * 128],
                                        rhs=AR1[:, dd * S + tc * 512: dd * S + (tc + 1) * 512],
                                        start=(dd == 0), stop=False)
                        if fi == 3:
                            sig("tensor", mmr, "pe")
                wait("tensor", "qG", btok)
                for fi in range(4):
                    mm = qT.matmul(out=psum[fi][:], lhsT=rbufA[0:1, fi * 128:(fi + 1) * 128],
                                   rhs=ones_r[:], start=False, stop=True)
                    ptok = sig("tensor", mm, "pe")
                    wait("vector", "pe", ptok)
                    if not (fc == 0 and tc == 0 and fi == 0):
                        wait("vector", "qG")
                    oslot = tmpHH[:, (fi % 2) * 512:(fi % 2) * 512 + 512]
                    qV.tensor_copy(out=oslot, in_=psum[fi][:])
                    vtok = sig("vector", qV.drain(), "vec")
                    bank_mark(fi, "vec")
                    wait("gpsimd", "vec", vtok)
                    dma("gpsimd", dst[(fc * 4 + fi) * 128:(fc * 4 + fi + 1) * 128,
                                      tc * 512:(tc + 1) * 512], oslot)

    barrier()

    yeR = AR2[:, 12288:16384].rearrange("p (h t) -> p t h", h=16)
    qR = AR2[:, 0:4096].rearrange("p (h t) -> p t h", h=16)
    kR = AR2[:, 4096:8192].rearrange("p (h t) -> p t h", h=16)
    vR = AR2[:, 8192:12288].rearrange("p (h t) -> p t h", h=16)
    for e8 in range(8):
        pe_guard = cnt["pe"]
        v_guard = cnt["vec"]
        wait("sync", "pe", pe_guard)
        wait("sync", "vec", v_guard)
        for ft in range(16):
            dma("sync", AR2[:, 0 + ft * 256:0 + (ft + 1) * 256],
                QTs[ft * 128:(ft + 1) * 128, e8 * 256:(e8 + 1) * 256])
            dma("sync", AR2[:, 4096 + ft * 256:4096 + (ft + 1) * 256],
                KTs[ft * 128:(ft + 1) * 128, e8 * 256:(e8 + 1) * 256])
            dma("sync", AR2[:, 8192 + ft * 256:8192 + (ft + 1) * 256],
                VTs[ft * 128:(ft + 1) * 128, e8 * 256:(e8 + 1) * 256])
        t_loaded = cnt["qS"]
        wait("tensor", "qS", t_loaded)
        wait("vector", "qS", t_loaded)
        for g in range(32):
            t0 = g * 8
            bs = g % 2
            bc = 2 + (g % 2)
            qAP = qR[:, t0:t0 + 8, :]
            kAP = kR[:, t0:t0 + 8, :]
            vAP = vR[:, t0:t0 + 8, :]
            sl = (g % 2) * 1024
            E_s = tmpHH[:, sl + 0:sl + 128]
            A_s = tmpHH[:, sl + 128:sl + 256]
            AT_s = tmpHH[:, sl + 256:sl + 384]
            VG_s = tmpHH[:, sl + 384:sl + 512]
            CG_s = tmpHH[:, sl + 512:sl + 640]
            KG_s = tmpHH[:, sl + 640:sl + 768]
            VGT_s = tmpHH[:, sl + 768:sl + 896]
            QG_s = tmpHH[:, sl + 896:sl + 1024]
            qV.tensor_copy(out=KG_s, in_=kAP)
            qV.tensor_reduce(out=colB[:, 0:8], in_=vAP, axis=AX, op=OP.add)
            qV.drain()
            qV.tensor_copy(out=VGT_s.rearrange("p (t h) -> p t h", h=16),
                           in_=colB[:, 0:8].unsqueeze(2).broadcast_to((128, 8, 16)))
            qV.tensor_copy(out=QG_s, in_=qAP)
            vk = sig("vector", qV.drain(), "vec")
            wait("tensor", "vec", vk)
            bank_wait(bs)
            mm = qT.matmul(out=psum[bs][:, 0:128], lhsT=QG_s, rhs=KG_s, start=True, stop=True)
            p1 = sig("tensor", mm, "pe")
            wait("scalar", "pe", p1)
            qA.activation(out=E_s, in_=psum[bs][:, 0:128], func=AF.Exp, scale=ISQHD)
            s1 = sig("scalar", qA.drain(), "sc")
            bank_mark(bs, "sc")
            wait("vector", "sc", s1)
            qV.tensor_tensor(out=A_s, in0=E_s, in1=bdmask[:], op=OP.mult)
            qV.drain()
            qV.tensor_reduce(out=colA[:, 0:1], in_=A_s, axis=AX, op=OP.add)
            qV.reciprocal(out=colA[:, 1:2], in_=colA[:, 0:1])
            qV.tensor_scalar(out=A_s, in0=A_s, scalar1=colA[:, 1:2], scalar2=None,
                             op0=OP.mult)
            v1 = sig("vector", qV.drain(), "vec")
            wait("tensor", "vec", v1)
            bank_wait(6)
            qT.transpose(out=psT[:, 0:128], in_=A_s, identity=eyeb[:])
            qT.transpose(out=psT[:, 128:256], in_=VGT_s, identity=eyeb[:])
            p2 = sig("tensor", qT.drain(), "pe")
            wait("vector", "pe", p2)
            qV.tensor_copy(out=AT_s, in_=psT[:, 0:128])
            qV.tensor_copy(out=VG_s, in_=psT[:, 128:256])
            v2 = sig("vector", qV.drain(), "vec")
            bank_mark(6, "vec")
            wait("tensor", "vec", v2)
            bank_wait(bc)
            mm = qT.matmul(out=psum[bc][:, 0:128], lhsT=AT_s, rhs=VG_s, start=True, stop=True)
            p3 = sig("tensor", mm, "pe")
            wait("vector", "pe", p3)
            qV.tensor_copy(out=CG_s, in_=psum[bc][:, 0:128])
            v3 = sig("vector", qV.drain(), "vec")
            bank_mark(bc, "vec")
            wait("tensor", "vec", v3)
            bank_wait(6)
            qT.transpose(out=psT[:, 256:384], in_=CG_s, identity=eyeb[:])
            p4 = sig("tensor", qT.drain(), "pe")
            wait("vector", "pe", p4)
            qV.tensor_copy(out=yeR[:, t0:t0 + 8, :], in_=psT[:, 256:384])
            sig("vector", qV.drain(), "vec")
            bank_mark(6, "vec")
            if DEV_TAIL == "grp0" and e8 == 0 and g == 0:
                wait("gpsimd", "vec")
                dma("gpsimd", out_h[0:128, 0:128], QG_s)
                dma("gpsimd", out_h[0:128, 128:256], KG_s)
                dma("gpsimd", out_h[0:128, 256:384], E_s)
                dma("gpsimd", out_h[0:128, 384:512], tmpHH[:, sl + 128:sl + 256])
                dma("gpsimd", out_h[0:128, 512:640], tmpHH[:, sl + 384:sl + 512])
                dma("gpsimd", out_h[0:128, 640:768], tmpHH[:, sl + 512:sl + 640])
                dma("gpsimd", out_h[128:256, 0:2048], AR2[:, 12288:14336])
                dma("gpsimd", out_h[384:512, 0:2048], AR2[:, 0:2048])
                wait("gpsimd", "qG")
                for _e in ("sync", "vector", "tensor", "scalar"):
                    wait(_e, "qG")
        v_ye = cnt["vec"]
        for tt2 in range(2):
            rtt = e8 * 256 + tt2 * 128
            for foc in range(4):
                botok = dma("gpsimd", rbufB[:], bo_h[:, foc * 512:(foc + 1) * 512])
                xtok = dma("sync", wldA[:, 0:512], x_in[rtt:rtt + 128, foc * 512:(foc + 1) * 512])
                ftok = dma("sync", wldA[:, 512:1024], FFN0[rtt:rtt + 128, foc * 512:(foc + 1) * 512])
                for ff in range(16):
                    slot = WR[:, (ff % 2) * 512:(ff % 2) * 512 + 512]
                    wait("gpsimd", "pe")
                    wtok = dma("gpsimd", slot,
                               WoT[ff * 128:(ff + 1) * 128, foc * 512:(foc + 1) * 512])
                    wait("tensor", "qG", wtok)
                    if ff == 0:
                        wait("tensor", "vec", v_ye)
                        bank_wait(4 + tt2 % 2)
                    mmr = qT.matmul(out=psum[4 + tt2 % 2][:],
                                    lhsT=AR2[:, 12288 + ff * 256 + tt2 * 128: 12288 + ff * 256 + tt2 * 128 + 128],
                                    rhs=slot, start=(ff == 0), stop=False)
                    sig("tensor", mmr, "pe")
                wait("tensor", "qG", botok)
                mm = qT.matmul(out=psum[4 + tt2 % 2][:], lhsT=onesMb[:], rhs=rbufB[:],
                               start=False, stop=True)
                ptok = sig("tensor", mm, "pe")
                wait("vector", "pe", ptok)
                wait("vector", "qS", ftok)
                oslot = tmpB[:, (foc % 4) * 512:(foc % 4) * 512 + 512]
                qV.tensor_tensor(out=oslot, in0=psum[4 + tt2 % 2][:], in1=wldA[:, 0:512],
                                 op=OP.add)
                qV.tensor_tensor(out=oslot, in0=oslot, in1=wldA[:, 512:1024], op=OP.add)
                vtok = sig("vector", qV.drain(), "vec")
                bank_mark(4 + tt2 % 2, "vec")
                wait("gpsimd", "vec", vtok)
                dma("gpsimd", X1[rtt:rtt + 128, foc * 512:(foc + 1) * 512], oslot)
                wait("sync", "qG")

    barrier()

    def row_stats_bcast(src_r, n_rt, dst_ws, iws_region):
        for rt in range(n_rt):
            buf = (wldA, wldB)[rt % 2]
            wait("sync", "vec")
            tok = dma("sync", buf[:], src_r[rt * 128:(rt + 1) * 128, :])
            wait("vector", "qS", tok)
            qV.tensor_reduce(out=colA[:, 0:1], in_=buf[:], axis=AX, op=OP.max,
                             apply_absolute_value=True)
            qV.tensor_scalar(out=colA[:, 1:2], in0=colA[:, 0:1], scalar1=1.0 / 127.0,
                             scalar2=EPS_I8, op0=OP.mult, op1=OP.max)
            qV.reciprocal(out=colA[:, 2:3], in_=colA[:, 1:2])
            v1 = sig("vector", qV.drain(), "vec")
            wait("tensor", "vec", v1)
            bank_wait(7)
            qT.transpose(out=psM[0:1, 0:128], in_=colA[:, 1:2], identity=eyef[:])
            qT.transpose(out=psM[0:1, 128:256], in_=colA[:, 2:3], identity=eyef[:])
            p1 = sig("tensor", qT.drain(), "pe")
            wait("vector", "pe", p1)
            qV.tensor_copy(out=rbufR[0:1, 0:128], in_=psM[0:1, 0:128])
            qV.tensor_copy(out=rbufB[0:1, 0:128].bitcast(F32), in_=psM[0:1, 128:256]) \
                if False else qV.tensor_copy(out=colB[0:1, 0:64], in_=psM[0:1, 128:192])
            qV.tensor_copy(out=colC[0:1, 0:64], in_=psM[0:1, 192:256])
            v2 = sig("vector", qV.drain(), "vec")
            wait("tensor", "vec", v2)
            qT.matmul(out=psM[:, 256:384], lhsT=onesM[:], rhs=rbufR[0:1, 0:128],
                      start=True, stop=True)
            qT.matmul(out=psM[:, 384:448], lhsT=onesM[:], rhs=colB[0:1, 0:64],
                      start=True, stop=True)
            qT.matmul(out=psM[:, 448:512], lhsT=onesM[:], rhs=colC[0:1, 0:64],
                      start=True, stop=True)
            p2 = sig("tensor", qT.drain(), "pe")
            wait("vector", "pe", p2)
            qV.tensor_copy(out=dst_ws[:, rt * 128:(rt + 1) * 128], in_=psM[:, 256:384])
            qV.tensor_copy(out=iws_region[:, rt * 128:rt * 128 + 64], in_=psM[:, 384:448])
            qV.tensor_copy(out=iws_region[:, rt * 128 + 64:(rt + 1) * 128], in_=psM[:, 448:512])
            sig("vector", qV.drain(), "vec")
            bank_mark(7, "vec")

    iwsF = AR2[:, 0:FQ]
    iwsO = AR2[:, FQ:FQ + D]
    row_stats_bcast(Wf_r, FQ // 128, wsA, iwsF)
    row_stats_bcast(Wo1_r, ND, wsB, iwsO)

    def quant_int8_T(srcT, n_dd, width, iws_region, dstQ):
        for dd in range(n_dd):
            for c in range((width + 2047) // 2048):
                w = min(2048, width - c * 2048)
                buf = (wldA, wldB)[(dd + c) % 2]
                wait("sync", "vec")
                wait("sync", "qG")
                tok = dma("sync", buf[:, 0:w],
                          srcT[dd * 128:(dd + 1) * 128, c * 2048:c * 2048 + w])
                wait("vector", "qS", tok)
                qV.tensor_tensor(out=tmpA[:, 0:w], in0=buf[:, 0:w],
                                 in1=iws_region[:, c * 2048:c * 2048 + w], op=OP.mult)
                qV.tensor_copy(out=IH[:, 0:w], in_=tmpA[:, 0:w])
                qV.tensor_scalar(out=WSTG[:, 0:w], in0=IH[:, 0:w], scalar1=127.0,
                                 scalar2=-127.0, op0=OP.min, op1=OP.max)
                vtok = sig("vector", qV.drain(), "vec")
                wait("gpsimd", "vec", vtok)
                dma("gpsimd", dstQ[dd * 128:(dd + 1) * 128, c * 2048:c * 2048 + w],
                    WSTG[:, 0:w])
                wait("vector", "qG")

    quant_int8_T(WfT, ND, FQ, iwsF, WFQ)
    quant_int8_T(Wo1T, ND, D, iwsO, WOQ)

    barrier()

    TBW = WR[:, 0:2048]

    def quant_mx_w(src_r, n_rt, width, dstQT):
        nch = width // 2048
        for rti in range(n_rt * nch):
            rt, ch = divmod(rti, nch)
            buf = (wldA, wldB)[rti % 2]
            wait("sync", "vec")
            tok = dma("sync", buf[:],
                      src_r[rt * 128:(rt + 1) * 128, ch * 2048:(ch + 1) * 2048])
            wait("vector", "qS", tok)
            qV.tensor_reduce(out=colA[:, 0:64],
                             in_=buf[:].rearrange("p (b k) -> p b k", k=32),
                             axis=AX, op=OP.max, apply_absolute_value=True)
            v1 = sig("vector", qV.drain(), "vec")
            wait("scalar", "vec", v1)
            qA.activation(out=colB[:, 0:64], in_=colA[:, 0:64], func=AF.Ln)
            s1 = sig("scalar", qA.drain(), "sc")
            wait("vector", "sc", s1)
            qV.tensor_scalar(out=colB[:, 0:64], in0=colB[:, 0:64], scalar1=RLN2,
                             scalar2=None, op0=OP.mult)
            qV.tensor_copy(out=colI[:, 0:64], in_=colB[:, 0:64])
            qV.tensor_tensor(out=colC[:, 0:64], in0=colI[:, 0:64], in1=colB[:, 0:64],
                             op=OP.is_gt)
            qV.tensor_tensor(out=colD[:, 0:64], in0=colI[:, 0:64], in1=colC[:, 0:64],
                             op=OP.subtract)
            qV.tensor_scalar(out=colD[:, 0:64], in0=colD[:, 0:64], scalar1=-2.0,
                             scalar2=127.0, op0=OP.add, op1=OP.min)
            qV.tensor_scalar(out=colD[:, 0:64], in0=colD[:, 0:64], scalar1=-127.0,
                             scalar2=None, op0=OP.max)
            qV.tensor_tensor(out=tmpA[:], in0=buf[:], in1=buf[:], op=OP.mult)
            v2 = sig("vector", qV.drain(), "vec")
            wait("scalar", "vec", v2)
            qA.activation(out=tmpB[:], in_=tmpA[:], func=AF.Ln)
            s2 = sig("scalar", qA.drain(), "sc")
            wait("vector", "sc", s2)
            qV.tensor_scalar(out=tmpB[:], in0=tmpB[:], scalar1=RLN2 * 0.5,
                             scalar2=-0.5, op0=OP.mult, op1=OP.add)
            qV.tensor_copy(out=IH[:], in_=tmpB[:])
            qV.tensor_tensor(out=tmpB[:].rearrange("p (b k) -> p b k", k=32),
                             in0=IH[:].rearrange("p (b k) -> p b k", k=32),
                             in1=colD[:, 0:64].unsqueeze(2).broadcast_to((128, 64, 32)),
                             op=OP.max)
            v3 = sig("vector", qV.drain(), "vec")
            wait("scalar", "vec", v3)
            qA.activation(out=tmpA[:], in_=tmpB[:], func=AF.Exp, scale=-LN2C, bias=scal[:, 10:11])
            qA.activation(out=colC[:, 0:64], in_=colD[:, 0:64], func=AF.Exp,
                          scale=LN2C, bias=scal[:, 12:13])
            s3 = sig("scalar", qA.drain(), "sc")
            wait("vector", "sc", s3)
            qV.tensor_tensor(out=tmpA[:], in0=buf[:], in1=tmpA[:], op=OP.mult)
            qV.tensor_copy(out=IH[:], in_=tmpA[:])
            v4 = sig("vector", qV.drain(), "vec")
            wait("scalar", "vec", v4)
            qA.activation(out=tmpB[:], in_=tmpB[:], func=AF.Exp, scale=LN2C, bias=scal[:, 11:12])
            s4 = sig("scalar", qA.drain(), "sc")
            wait("vector", "sc", s4)
            qV.tensor_scalar(out=colB[:, 0:64], in0=colC[:, 0:64], scalar1=-1.0,
                             scalar2=None, op0=OP.mult)
            qV.tensor_tensor(out=tmpA[:], in0=IH[:], in1=tmpB[:], op=OP.mult)
            qV.tensor_tensor(out=tmpA[:].rearrange("p (b k) -> p b k", k=32),
                             in0=tmpA[:].rearrange("p (b k) -> p b k", k=32),
                             in1=colC[:, 0:64].unsqueeze(2).broadcast_to((128, 64, 32)),
                             op=OP.min)
            qV.tensor_tensor(out=WSTG[:].rearrange("p (b k) -> p b k", k=32),
                             in0=tmpA[:].rearrange("p (b k) -> p b k", k=32),
                             in1=colB[:, 0:64].unsqueeze(2).broadcast_to((128, 64, 32)),
                             op=OP.max)
            vtok = sig("vector", qV.drain(), "vec")
            wait("tensor", "vec", vtok)
            for cg in range(4):
                wait("tensor", "gp")
                for c in range(cg * 4, cg * 4 + 4):
                    tr = qT.transpose(out=psT[:, (c % 4) * 128:((c % 4) + 1) * 128],
                                      in_=WSTG[:, c * 128:(c + 1) * 128],
                                      identity=eyeb[:])
                ptk = sig("tensor", tr, "pe")
                wait("vector", "pe", ptk)
                if cg == 0:
                    wait("vector", "qG")
                for c in range(cg * 4, cg * 4 + 4):
                    qV.tensor_copy(out=TBW[:, c * 128:(c + 1) * 128],
                                   in_=psT[:, (c % 4) * 128:((c % 4) + 1) * 128])
                sig("vector", qV.drain(), "gp")
            wait("gpsimd", "gp")
            dma("gpsimd", dstQT[ch * 2048:(ch + 1) * 2048, rt * 128:(rt + 1) * 128]
                .rearrange("(c p) j -> p c j", p=128),
                TBW[:].rearrange("p (c j) -> p c j", j=128))
            wait("vector", "pe")

    quant_mx_w(Wg_r, NF, D, WGQT)
    quant_mx_w(Wu_r, NF, D, WUQT)
    quant_mx_w(Wd_r, ND, DFF, WDQT)

    barrier()

    def mx_act_quant(src_f32, w, dst_bf16, nblk, dbg=None):
        kb = 32
        qV.tensor_reduce(out=colA[:, 0:nblk],
                         in_=src_f32.rearrange("p (b k) -> p b k", k=kb),
                         axis=AX, op=OP.max, apply_absolute_value=True)
        v1 = sig("vector", qV.drain(), "vec")
        wait("scalar", "vec", v1)
        qA.activation(out=colB[:, 0:nblk], in_=colA[:, 0:nblk], func=AF.Ln)
        s1 = sig("scalar", qA.drain(), "sc")
        wait("vector", "sc", s1)
        if dbg is not None:
            qV.tensor_copy(out=wsB[:, dbg:dbg + 8], in_=colA[:, 0:nblk])
            qV.tensor_copy(out=wsB[:, dbg + 8:dbg + 16], in_=colB[:, 0:nblk])
        qV.tensor_scalar(out=colB[:, 0:nblk], in0=colB[:, 0:nblk], scalar1=RLN2,
                         scalar2=None, op0=OP.mult)
        qV.tensor_copy(out=colI[:, 0:nblk], in_=colB[:, 0:nblk])
        qV.tensor_tensor(out=colC[:, 0:nblk], in0=colI[:, 0:nblk], in1=colB[:, 0:nblk],
                         op=OP.is_gt)
        qV.tensor_tensor(out=colD[:, 0:nblk], in0=colI[:, 0:nblk], in1=colC[:, 0:nblk],
                         op=OP.subtract)
        qV.tensor_scalar(out=colD[:, 0:nblk], in0=colD[:, 0:nblk], scalar1=-2.0,
                         scalar2=127.0, op0=OP.add, op1=OP.min)
        qV.tensor_scalar(out=colD[:, 0:nblk], in0=colD[:, 0:nblk], scalar1=-127.0,
                         scalar2=None, op0=OP.max)
        if dbg is not None:
            qV.drain()
            qV.tensor_copy(out=wsB[:, dbg + 16:dbg + 24], in_=colB[:, 0:nblk])
            qV.tensor_copy(out=wsB[:, dbg + 24:dbg + 32], in_=colI[:, 0:nblk])
            qV.tensor_copy(out=wsB[:, dbg + 32:dbg + 40], in_=colD[:, 0:nblk])
        v2 = sig("vector", qV.drain(), "vec")
        wait("scalar", "vec", v2)
        qA.activation(out=colB[:, 0:nblk], in_=colD[:, 0:nblk], func=AF.Exp,
                      scale=-LN2C, bias=scal[:, 14:15])
        qA.activation(out=colC[:, 0:nblk], in_=colD[:, 0:nblk], func=AF.Exp,
                      scale=LN2C, bias=scal[:, 13:14])
        s2 = sig("scalar", qA.drain(), "sc")
        wait("vector", "sc", s2)
        if dbg is not None:
            qV.tensor_copy(out=wsB[:, dbg + 40:dbg + 48], in_=colB[:, 0:nblk])
            qV.tensor_copy(out=wsB[:, dbg + 48:dbg + 56], in_=colC[:, 0:nblk])
        qV.tensor_tensor(out=tmpA[:, 0:w].rearrange("p (b k) -> p b k", k=kb),
                         in0=src_f32.rearrange("p (b k) -> p b k", k=kb),
                         in1=colB[:, 0:nblk].unsqueeze(2).broadcast_to((128, nblk, kb)),
                         op=OP.mult)
        qV.tensor_scalar(out=tmpA[:, 0:w], in0=tmpA[:, 0:w], scalar1=7.5 / 64.0,
                         scalar2=-7.5 / 64.0, op0=OP.min, op1=OP.max)
        qV.tensor_copy(out=tmpF8[:, 0:w], in_=tmpA[:, 0:w])
        qV.drain()
        qV.tensor_tensor(out=dst_bf16.rearrange("p (b k) -> p b k", k=kb),
                         in0=tmpF8[:, 0:w].rearrange("p (b k) -> p b k", k=kb),
                         in1=colC[:, 0:nblk].unsqueeze(2).broadcast_to((128, nblk, kb)),
                         op=OP.mult)
        return sig("vector", qV.drain(), "vec")

    q_read = [cnt["vec"]]
    for half in range(2):
        pe_g = cnt["pe"]
        for th in range(8):
            tt = half * 8 + th
            wait("sync", "vec")
            tok = dma("sync", xtile[:], X1[tt * 128:(tt + 1) * 128, :])
            wait("vector", "qS", tok)
            if th == 0:
                wait("vector", "pe", pe_g)
            for c in range(4):
                qV.bn_stats(out=tmpB[:, c * 6:c * 6 + 6], in_=xtile[:, c * 512:(c + 1) * 512])
            qV.drain()
            qV.bn_aggr(out=colE[:, 0:2], in_=tmpB[:, 0:24])
            qV.tensor_scalar(out=colE[:, 2:3], in0=colE[:, 1:2], scalar1=EPS_LN,
                             scalar2=None, op0=OP.add)
            qV.reciprocal(out=colE[:, 3:4], in_=colE[:, 2:3])
            v1 = sig("vector", qV.drain(), "vec")
            wait("scalar", "vec", v1)
            qA.activation(out=colE[:, 4:5], in_=colE[:, 3:4], func=AF.Sqrt)
            s1 = sig("scalar", qA.drain(), "sc")
            wait("vector", "sc", s1)
            qV.tensor_scalar(out=tmpB[:], in0=xtile[:], scalar1=colE[:, 0:1],
                             scalar2=colE[:, 4:5], op0=OP.subtract, op1=OP.mult)
            qV.drain()
            vtok = mx_act_quant(tmpB[:], 2048, WSTG[:], 64)
            wait("sync", "vec", vtok)
            for c in range(ND):
                dma("sync", AR2[:, c * 1024 + th * 128: c * 1024 + (th + 1) * 128],
                    WSTG[:, c * 128:(c + 1) * 128], transpose=True)
        t_half = cnt["qS"]
        wait("tensor", "qS", t_half)
        for fc in range(32):
            pe_now = cnt["pe"]
            wait("sync", "pe", pe_now)
            wait("scalar", "pe", pe_now)
            for dd in range(ND):
                qe = ("sync", "scalar")[dd % 2]
                dma(qe, WR[:, dd * 256:(dd + 1) * 256],
                    WGQT[dd * 128:(dd + 1) * 128, fc * 256:(fc + 1) * 256])
                dma(qe, WR[:, 4096 + dd * 256: 4096 + (dd + 1) * 256],
                    WUQT[dd * 128:(dd + 1) * 128, fc * 256:(fc + 1) * 256])
            wait("tensor", "qS", cnt["qS"])
            wait("tensor", "qA", cnt["qA"])
            for th in range(8):
                tt = half * 8 + th
                bg, bu = th % 2, 2 + th % 2
                bank_wait(bg)
                bank_wait(bu)
                for dd in range(ND):
                    qT.matmul(out=psum[bg][:, 0:256],
                              lhsT=AR2[:, dd * 1024 + th * 128: dd * 1024 + (th + 1) * 128],
                              rhs=WR[:, dd * 256:(dd + 1) * 256],
                              start=(dd == 0), stop=(dd == ND - 1))
                    mmr = qT.matmul(out=psum[bu][:, 0:256],
                                    lhsT=AR2[:, dd * 1024 + th * 128: dd * 1024 + (th + 1) * 128],
                                    rhs=WR[:, 4096 + dd * 256: 4096 + (dd + 1) * 256],
                                    start=(dd == 0), stop=(dd == ND - 1))
                ptok = sig("tensor", mmr, "pe")
                wait("scalar", "pe", ptok)
                if th == 0:
                    wait("scalar", "vec", q_read[0])
                sgs = tmpB[:, th * 256:(th + 1) * 256]
                qA.activation(out=sgs, in_=psum[bg][:, 0:256], func=AF.Silu)
                stok = sig("scalar", qA.drain(), "sc")
                bank_mark(bg, "sc")
                wait("vector", "sc", stok)
                qV.tensor_tensor(out=sgs, in0=sgs, in1=psum[bu][:, 0:256], op=OP.mult)
                sig("vector", qV.drain(), "vec")
                bank_mark(bu, "vec")
            wait("vector", "qG")
            q_read[0] = mx_act_quant(tmpB[:], 2048, WSTG[:], 64)
            wait("gpsimd", "vec", q_read[0])
            dma("gpsimd",
                QINT[half * 1024:(half + 1) * 1024, fc * 256:(fc + 1) * 256]
                .rearrange("(t p) c -> p t c", p=128),
                WSTG[:].rearrange("p (t c) -> p t c", c=256))

    if DEV_TAIL == "qx":
        wait("gpsimd", "qS")
        for c in range(ND):
            dma("gpsimd", GSP[c * 128:(c + 1) * 128, 0:1024], AR2[:, c * 1024:(c + 1) * 1024])
        wait("gpsimd", "qG")

    barrier()

    for tt in range(NT):
        wait("sync", "vec")
        tok = dma("sync", xtile[:], X1[tt * 128:(tt + 1) * 128, :])
        wait("vector", "qS", tok)
        for c in range(4):
            qV.bn_stats(out=tmpB[:, c * 6:c * 6 + 6], in_=xtile[:, c * 512:(c + 1) * 512])
        qV.drain()
        qV.bn_aggr(out=colE[:, 0:2], in_=tmpB[:, 0:24])
        qV.tensor_scalar(out=colE[:, 2:3], in0=colE[:, 1:2], scalar1=EPS_LN,
                         scalar2=None, op0=OP.add)
        qV.reciprocal(out=colE[:, 3:4], in_=colE[:, 2:3])
        v1 = sig("vector", qV.drain(), "vec")
        wait("scalar", "vec", v1)
        qA.activation(out=colE[:, 4:5], in_=colE[:, 3:4], func=AF.Sqrt)
        s1 = sig("scalar", qA.drain(), "sc")
        wait("vector", "sc", s1)
        qV.tensor_reduce(out=colD[:, 0:1], in_=xtile[:], axis=AX, op=OP.max)
        qV.tensor_reduce(out=colD[:, 1:2], in_=xtile[:], axis=AX, op=OP.min)
        qV.drain()
        qV.tensor_tensor(out=colD[:, 2:3], in0=colD[:, 0:1], in1=colE[:, 0:1], op=OP.subtract)
        qV.tensor_tensor(out=colD[:, 3:4], in0=colE[:, 0:1], in1=colD[:, 1:2], op=OP.subtract)
        qV.tensor_tensor(out=colD[:, 4:5], in0=colD[:, 2:3], in1=colD[:, 3:4], op=OP.max)
        qV.tensor_tensor(out=colD[:, 5:6], in0=colD[:, 4:5], in1=colE[:, 4:5], op=OP.mult)
        qV.tensor_scalar(out=stat[:, tt * 8 + 4:tt * 8 + 5], in0=colD[:, 5:6],
                         scalar1=1.0 / 127.0, scalar2=EPS_I8, op0=OP.mult, op1=OP.max)
        qV.drain()
        qV.reciprocal(out=colD[:, 6:7], in_=stat[:, tt * 8 + 4:tt * 8 + 5])
        qV.tensor_tensor(out=colD[:, 7:8], in0=colE[:, 4:5], in1=colD[:, 6:7], op=OP.mult)
        qV.drain()
        qV.tensor_scalar(out=tmpB[:], in0=xtile[:], scalar1=colE[:, 0:1],
                         scalar2=colD[:, 7:8], op0=OP.subtract, op1=OP.mult)
        qV.tensor_copy(out=IH[:], in_=tmpB[:])
        qV.tensor_scalar(out=WSTG[:], in0=IH[:], scalar1=127.0, scalar2=-127.0,
                         op0=OP.min, op1=OP.max)
        vtok = sig("vector", qV.drain(), "vec")
        wait("sync", "vec", vtok)
        for c in range(ND):
            dma("sync", AR1[:, c * S + tt * 128: c * S + (tt + 1) * 128],
                WSTG[:, c * 128:(c + 1) * 128], transpose=True)

    barrier()

    ebuf = WR[:, 0:8192]
    kT_g = AR2[:, 0:2048]
    qT_h = AR2[:, 2048:4096]
    WFS = AR2[:, 4096:4096 + 16 * 128]

    def l1_proj(fslice, dst_kind, rms_rope, gain_col, out_cb):
        wait("sync", "pe")
        for dd in range(ND):
            dma("sync", WFS[:, dd * 128:(dd + 1) * 128],
                WFQ[dd * 128:(dd + 1) * 128, fslice:fslice + 128])
        wtok = cnt["qS"]
        wait("tensor", "qS", wtok)
        for ttg in range(4):
            for ti in range(4):
                tt = ttg * 4 + ti
                if ti == 0:
                    bank_wait(4)
                for dd in range(ND):
                    mmr = qT.matmul(out=psum[4][:, ti * 128:(ti + 1) * 128],
                                    lhsT=AR1[:, dd * S + tt * 128: dd * S + (tt + 1) * 128],
                                    rhs=WFS[:, dd * 128:(dd + 1) * 128],
                                    start=(dd == 0), stop=(dd == ND - 1))
            ptok = sig("tensor", mmr, "pe")
            wait("vector", "pe", ptok)
            for ti in range(4):
                tt = ttg * 4 + ti
                pslot = psum[4][:, ti * 128:(ti + 1) * 128]
                fslot = tmpA[:, (ti % 2) * 1024:(ti % 2) * 1024 + 128]
                qV.scalar_tensor_tensor(out=fslot, in0=pslot,
                                        scalar=stat[:, tt * 8 + 4:tt * 8 + 5],
                                        in1=wsA[:, fslice:fslice + 128],
                                        op0=OP.mult, op1=OP.mult)
                if rms_rope:
                    qV.drain()
                    qV.scalar_tensor_tensor(out=tmpB[:, 0:128], in0=fslot, scalar=1.0,
                                            in1=fslot, op0=OP.mult, op1=OP.mult,
                                            accum_out=colA[:, 0:1])
                    qV.tensor_scalar(out=colA[:, 1:2], in0=colA[:, 0:1],
                                     scalar1=1.0 / 128.0, scalar2=F32EPS,
                                     op0=OP.mult, op1=OP.add)
                    qV.reciprocal(out=colA[:, 2:3], in_=colA[:, 1:2])
                    vv = sig("vector", qV.drain(), "vec")
                    wait("scalar", "vec", vv)
                    qA.activation(out=colA[:, 3:4], in_=colA[:, 2:3], func=AF.Sqrt)
                    ss = sig("scalar", qA.drain(), "sc")
                    wait("vector", "sc", ss)
                    if gain_col is not None:
                        qV.tensor_tensor(out=colA[:, 4:5], in0=colA[:, 3:4],
                                         in1=gain_col, op=OP.mult)
                    else:
                        qV.tensor_copy(out=colA[:, 4:5], in_=colA[:, 3:4])
                    qV.drain()
                    qV.tensor_scalar(out=fslot, in0=fslot, scalar1=colA[:, 4:5],
                                     scalar2=None, op0=OP.mult)
                    cosb = ropeC[:, tt * 64:(tt + 1) * 64]
                    sinb = ropeS[:, tt * 64:(tt + 1) * 64]
                    qV.tensor_tensor(out=tmpB[:, 0:64], in0=fslot[:, 0:64], in1=cosb, op=OP.mult)
                    qV.tensor_tensor(out=tmpB[:, 64:128], in0=fslot[:, 64:128], in1=sinb, op=OP.mult)
                    qV.tensor_tensor(out=tmpB[:, 128:192], in0=fslot[:, 0:64], in1=sinb, op=OP.mult)
                    qV.tensor_tensor(out=tmpB[:, 192:256], in0=fslot[:, 64:128], in1=cosb, op=OP.mult)
                    qV.tensor_tensor(out=WSTG[:, 0:64], in0=tmpB[:, 0:64],
                                     in1=tmpB[:, 64:128], op=OP.add)
                    qV.tensor_tensor(out=WSTG[:, 64:128], in0=tmpB[:, 192:256],
                                     in1=tmpB[:, 128:192], op=OP.subtract)
                else:
                    qV.tensor_copy(out=WSTG[:, 0:128], in_=fslot)
                vtok = sig("vector", qV.drain(), "vec")
                out_cb(tt, WSTG[:, 0:128], vtok)
            bank_mark(4, "vec")

    kv_gain = None
    for kv in range(NKV):
        def k_cb(tt, slot, vtok):
            wait("sync", "vec", vtok)
            dma("sync", kT_g[:, tt * 128:(tt + 1) * 128], slot, transpose=True)
        l1_proj(D + kv * 128, "k", True, None, k_cb)
        def v_cb(tt, slot, vtok):
            wait("gpsimd", "vec", vtok)
            g1 = dma("gpsimd", YSP[tt * 128:(tt + 1) * 128, 0:128], slot)
            wait("vector", "qG", g1)
        def v_cb2(tt, slot, vtok):
            qV.tensor_copy(out=vext[:, tt * 132:tt * 132 + 128], in_=slot)
            sig("vector", qV.drain(), "vec")
        l1_proj(D + KVD + kv * 128, "v", False, None, v_cb2)
        for qh in range(kv * 4, kv * 4 + 4):
            def q_cb(tt, slot, vtok):
                wait("sync", "vec", vtok)
                dma("sync", qT_h[:, tt * 128:(tt + 1) * 128], slot, transpose=True)
            l1_proj(qh * 128, "q", True, gainb[:, qh:qh + 1], q_cb)
            wait("tensor", "qS")
            wait("tensor", "vec")
            for tqc in range(4):
                ntk = 4 * tqc + 4
                sc_base = cnt["sc"]
                for tkt in range(ntk):
                    b = tkt % 2
                    bank_wait(b)
                    mm = qT.matmul(out=psum[b][:], lhsT=kT_g[:, tkt * 128:(tkt + 1) * 128],
                                   rhs=qT_h[:, tqc * 512:(tqc + 1) * 512],
                                   start=True, stop=True)
                    ptok = sig("tensor", mm, "pe")
                    wait("scalar", "pe", ptok)
                    qA.activation(out=ebuf[:, tkt * 512:(tkt + 1) * 512], in_=psum[b][:],
                                  func=AF.Exp, scale=ISQHD)
                    stok = sig("scalar", qA.drain(), "sc")
                    bank_mark(b, "sc")
                    if tkt >= 4 * tqc:
                        a = tkt - 4 * tqc
                        wait("vector", "sc", stok)
                        qV.tensor_tensor(out=ebuf[:, tkt * 512:(tkt + 1) * 512],
                                         in0=ebuf[:, tkt * 512:(tkt + 1) * 512],
                                         in1=dgmask[:, a * 512:(a + 1) * 512], op=OP.mult)
                        sig("vector", qV.drain(), "vec")
                wait("tensor", "sc")
                wait("tensor", "vec")
                for ti in range(4):
                    Tg = tqc * 4 + ti
                    b2 = 2 + ti % 2
                    bank_wait(b2)
                    for tkt in range(Tg + 1):
                        mm = qT.matmul(out=psum[b2][:, 0:129],
                                       lhsT=ebuf[:, tkt * 512 + ti * 128: tkt * 512 + (ti + 1) * 128],
                                       rhs=vext[:, tkt * 132:tkt * 132 + 129],
                                       start=(tkt == 0), stop=(tkt == Tg))
                    ptok = sig("tensor", mm, "pe")
                    wait("vector", "pe", ptok)
                    qV.reciprocal(out=colA[:, 8:9], in_=psum[b2][:, 128:129])
                    qV.drain()
                    yslot = tmpA[:, 1536 + (ti % 2) * 128: 1536 + (ti % 2) * 128 + 128]
                    qV.tensor_scalar(out=yslot, in0=psum[b2][:, 0:128],
                                     scalar1=colA[:, 8:9], scalar2=None, op0=OP.mult)
                    vtok = sig("vector", qV.drain(), "vec")
                    bank_mark(b2, "vec")
                    wait("gpsimd", "vec", vtok)
                    g1 = dma("gpsimd", YSP[Tg * 128:(Tg + 1) * 128, qh * 128:(qh + 1) * 128],
                             yslot.bitcast(F32) if False else yslot)
                    wait("vector", "qG", g1)

    barrier()

    YB = WR[:, 0:4096]
    for tt in range(NT):
        wait("sync", "vec")
        tok = dma("sync", YB[:, 0:2048], YSP[tt * 128:(tt + 1) * 128, :])
        wait("vector", "qS", tok)
        qV.tensor_reduce(out=colA[:, 0:1], in_=YB[:, 0:2048], axis=AX, op=OP.max,
                         apply_absolute_value=True)
        qV.tensor_scalar(out=stat[:, tt * 8 + 5:tt * 8 + 6], in0=colA[:, 0:1],
                         scalar1=1.0 / 127.0, scalar2=EPS_I8, op0=OP.mult, op1=OP.max)
        qV.drain()
        qV.reciprocal(out=colA[:, 1:2], in_=stat[:, tt * 8 + 5:tt * 8 + 6])
        qV.drain()
        qV.tensor_scalar(out=tmpB[:], in0=YB[:, 0:2048], scalar1=colA[:, 1:2],
                         scalar2=None, op0=OP.mult)
        qV.tensor_copy(out=IH[:], in_=tmpB[:])
        qV.tensor_scalar(out=WSTG[:], in0=IH[:], scalar1=127.0, scalar2=-127.0,
                         op0=OP.min, op1=OP.max)
        vtok = sig("vector", qV.drain(), "vec")
        wait("sync", "vec", vtok)
        for c in range(ND):
            dma("sync", AR1[:, c * S + tt * 128: c * S + (tt + 1) * 128],
                WSTG[:, c * 128:(c + 1) * 128], transpose=True)

    for foc in range(4):
        for ttg in range(4):
            wait("sync", "pe")
            rtoks = []
            for dd in range(ND):
                slot = WR[:, 4096 + (dd % 2) * 512: 4096 + (dd % 2) * 512 + 512]
                wait("sync", "pe")
                rtoks.append(dma("sync", slot, WOQ[dd * 128:(dd + 1) * 128,
                                                   foc * 512:(foc + 1) * 512]))
                wait("tensor", "qS", rtoks[-1])
                for ti in range(4):
                    tt = ttg * 4 + ti
                    if dd == 0:
                        bank_wait(ti)
                    mmr = qT.matmul(out=psum[ti][:],
                                    lhsT=AR1[:, dd * S + tt * 128: dd * S + (tt + 1) * 128],
                                    rhs=slot, start=(dd == 0), stop=(dd == ND - 1))
                    if ti == 3:
                        sig("tensor", mmr, "pe")
            for ti in range(4):
                tt = ttg * 4 + ti
                wait("vector", "pe")
                oslot = tmpB[:, (ti % 4) * 512:(ti % 4) * 512 + 512]
                qV.scalar_tensor_tensor(out=oslot, in0=psum[ti][:],
                                        scalar=stat[:, tt * 8 + 5:tt * 8 + 6],
                                        in1=wsB[:, foc * 512:(foc + 1) * 512],
                                        op0=OP.mult, op1=OP.mult)
                vtok = sig("vector", qV.drain(), "vec")
                bank_mark(ti, "vec")
                wait("gpsimd", "vec", vtok)
                dma("gpsimd", ATT1[tt * 128:(tt + 1) * 128, foc * 512:(foc + 1) * 512], oslot)
            wait("vector", "qG")

    barrier()

    for grp in range(4):
        pe_g = cnt["pe"]
        for ti in range(4):
            tt = grp * 4 + ti
            qe = ("sync", "scalar")[ti % 2]
            wait(qe, "pe", pe_g)
            for ddg in range(NF):
                dma(qe, AR1[:, ddg * 512 + ti * 128: ddg * 512 + (ti + 1) * 128],
                    QINT[tt * 128:(tt + 1) * 128, ddg * 128:(ddg + 1) * 128],
                    transpose=True)
        t_grpS, t_grpA = cnt["qS"], cnt["qA"]
        slot_tok = [0, 0]
        for foc in range(4):
            wait("tensor", "qS", t_grpS)
            wait("tensor", "qA", t_grpA)
            for ddg in range(NF):
                slot = WR[:, 8192 + (ddg % 2) * 512: 8192 + (ddg % 2) * 512 + 512]
                qe = ("sync", "scalar")[ddg % 2]
                wait(qe, "pe", slot_tok[ddg % 2])
                dtok = dma(qe, slot, WDQT[ddg * 128:(ddg + 1) * 128,
                                          foc * 512:(foc + 1) * 512])
                wait("tensor", "qS" if qe == "sync" else "qA", dtok)
                for ti in range(4):
                    if ddg == 0:
                        bank_wait(ti)
                    mmr = qT.matmul(out=psum[ti][:],
                                    lhsT=AR1[:, ddg * 512 + ti * 128: ddg * 512 + (ti + 1) * 128],
                                    rhs=slot, start=(ddg == 0), stop=(ddg == NF - 1))
                    if ti == 3:
                        slot_tok[ddg % 2] = sig("tensor", mmr, "pe")
            for ti in range(4):
                tt = grp * 4 + ti
                xtok = dma("sync", wldA[:, 0:512],
                           X1[tt * 128:(tt + 1) * 128, foc * 512:(foc + 1) * 512])
                atok = dma("sync", wldA[:, 512:1024],
                           ATT1[tt * 128:(tt + 1) * 128, foc * 512:(foc + 1) * 512])
                wait("vector", "pe")
                wait("vector", "qS", atok)
                oslot = tmpB[:, (ti % 4) * 512:(ti % 4) * 512 + 512]
                qV.tensor_tensor(out=oslot, in0=psum[ti][:], in1=wldA[:, 0:512], op=OP.add)
                qV.tensor_tensor(out=oslot, in0=oslot, in1=wldA[:, 512:1024], op=OP.add)
                vtok = sig("vector", qV.drain(), "vec")
                bank_mark(ti, "vec")
                wait("gpsimd", "vec", vtok)
                dma("gpsimd", out_h[tt * 128:(tt + 1) * 128, foc * 512:(foc + 1) * 512],
                    oslot)
                wait("sync", "qG")
            wait("vector", "qG")

    wait("gpsimd", "qG")


    src_map = {"x": x_in, "gsp": GSP, "x1": X1, "ffn0": FFN0, "att1": ATT1,
               "qts": None}
    if DEV_TAIL == "gsp":
        for tt in range(NT):
            dma("sync", out_h[tt * 128:(tt + 1) * 128, :],
                GSP[tt * 128:(tt + 1) * 128, 0:2048])
    elif DEV_TAIL in ("x1", "ffn0", "att1"):
        t = src_map[DEV_TAIL]
        for tt in range(NT):
            dma("sync", out_h[tt * 128:(tt + 1) * 128, :], t[tt * 128:(tt + 1) * 128, :])
    elif DEV_TAIL in ("qts", "kts", "vts"):
        _src = {"qts": QTs, "kts": KTs, "vts": VTs}[DEV_TAIL]
        for tt in range(NT):
            dma("gpsimd", out_h[tt * 128:(tt + 1) * 128, 0:2048],
                _src[tt * 128:(tt + 1) * 128, 0:2048])
        wait("gpsimd", "qG")
    elif DEV_TAIL in ("wgq", "wuq", "wdq"):
        _srcq = {"wgq": WGQT, "wuq": WUQT, "wdq": WDQT}[DEV_TAIL]
        for tt in range(NT):
            dma("gpsimd", out_h[tt * 128:(tt + 1) * 128, :], _srcq[tt * 128:(tt + 1) * 128, 0:2048])
        wait("gpsimd", "qG")
    elif DEV_TAIL == "qx":
        for tt in range(NT):
            dma("gpsimd", out_h[tt * 128:(tt + 1) * 128, 0:1024],
                GSP[tt * 128:(tt + 1) * 128, 0:1024])
        wait("gpsimd", "qG")
    elif DEV_TAIL == "gup":
        dma("gpsimd", out_h[0:128, 0:2048], GSP[0:128, 0:2048])
        dma("gpsimd", out_h[128:256, 0:1024], GSP[128:256, 0:1024])
        dma("gpsimd", out_h[128:256, 1024:1152], GSP[128:256, 1024:1152])
        wait("gpsimd", "qG")
    elif DEV_TAIL == "qint":
        for tt in range(NT):
            dma("gpsimd", out_h[tt * 128:(tt + 1) * 128, :], QINT[tt * 128:(tt + 1) * 128, 0:2048])
        wait("gpsimd", "qG")
    elif DEV_TAIL == "w2q":
        for tt in range(NT):
            dma("gpsimd", out_h[tt * 128:(tt + 1) * 128, :], W2Q[tt * 128:(tt + 1) * 128, :])
        wait("gpsimd", "qG")
    elif DEV_TAIL == "xq2b":
        for tt in range(8):
            wait("sync", "vec")
            wait("sync", "qG")
            tok = dma("sync", wldA[:], GSP[tt * 128:(tt + 1) * 128, 0:2048])
            wait("vector", "qS", tok)
            qV.tensor_scalar(out=tmpA[:], in0=wldA[:], scalar1=stat[:, tt * 8 + 2:tt * 8 + 3],
                             scalar2=stat[:, tt * 8 + 1:tt * 8 + 2], op0=OP.subtract, op1=OP.mult)
            qV.tensor_copy(out=IH[:], in_=tmpA[:])
            qV.tensor_scalar(out=tmpB[:], in0=IH[:], scalar1=127.0, scalar2=-128.0,
                             op0=OP.min, op1=OP.max)
            vtok = sig("vector", qV.drain(), "vec")
            wait("gpsimd", "vec", vtok)
            g1 = dma("gpsimd", out_h[tt * 128:(tt + 1) * 128, :], tmpB[:])
            g2 = dma("gpsimd", out_h[1024 + tt * 128:1024 + (tt + 1) * 128, :], wldA[:])
            wait("vector", "qG", g2)
    elif DEV_TAIL == "xq2":
        for tt in range(NT):
            wait("sync", "vec")
            wait("sync", "qG")
            tok = dma("sync", wldA[:], GSP[tt * 128:(tt + 1) * 128, 0:2048])
            wait("vector", "qS", tok)
            qV.tensor_scalar(out=tmpA[:], in0=wldA[:], scalar1=stat[:, tt * 8 + 2:tt * 8 + 3],
                             scalar2=stat[:, tt * 8 + 1:tt * 8 + 2], op0=OP.subtract, op1=OP.mult)
            qV.tensor_copy(out=IH[:], in_=tmpA[:])
            qV.tensor_scalar(out=tmpB[:], in0=IH[:], scalar1=127.0, scalar2=-128.0,
                             op0=OP.min, op1=OP.max)
            vtok = sig("vector", qV.drain(), "vec")
            wait("gpsimd", "vec", vtok)
            g2 = dma("gpsimd", out_h[tt * 128:(tt + 1) * 128, :], tmpB[:])
            wait("vector", "qG", g2)
    elif DEV_TAIL == "xq1t":
        for ddx in range(ND):
            dma("gpsimd", out_h[ddx * 128:(ddx + 1) * 128, :], AR1[:, ddx * S:(ddx + 1) * S])
        wait("gpsimd", "qG")
    elif DEV_TAIL == "srow":
        dma("gpsimd", out_h[0:1, 0:S], srow[:])
        dma("gpsimd", out_h[2:3, 0:S], srow2[:])
        dma("gpsimd", out_h[4:132, 0:NT * 8], stat[:])
        dma("gpsimd", out_h[140:268, 0:NT], gmax[:])
        dma("gpsimd", out_h[270:398, 0:NT], gmin[:])
        wait("gpsimd", "qG")
    elif DEV_TAIL == "ye":
        dma("gpsimd", out_h[0:128, 0:2048], AR2[:, 12288:14336])
        dma("gpsimd", out_h[128:256, 0:2048], AR2[:, 14336:16384])
        wait("gpsimd", "qG")
    elif DEV_TAIL in ("none", "ar2g", "grp0"):
        pass
    else:
        for tt in range(NT):
            dma("sync", out_h[tt * 128:(tt + 1) * 128, :], x_in[tt * 128:(tt + 1) * 128, :])
    wait("sync", "qS")

    blk = nc.Block().__enter__()
    for ename in ENGS:
        getattr(blk, ename)(lambda e, _q=Q[ename]: _q.replay(e))
    blk.__exit__(None, None, None)
    for cm in reversed(ctxs):
        cm.__exit__(None, None, None)
    return nc


def _stage(inputs):
    A = {k: np.ascontiguousarray(v) for k, v in inputs.items()}
    wf = np.concatenate([A["L1_attn_w_q"], A["L1_attn_w_k"], A["L1_attn_w_v"]], axis=0)
    common = dict(
        WqT=np.ascontiguousarray(A["L0_attn_Wq"].T), bq=A["L0_attn_bq"].reshape(1, D),
        WkT=np.ascontiguousarray(A["L0_attn_Wk"].T), bk=A["L0_attn_bk"].reshape(1, D),
        WvT=np.ascontiguousarray(A["L0_attn_Wv"].T), bv=A["L0_attn_bv"].reshape(1, D),
        WoT=np.ascontiguousarray(A["L0_attn_Wo"].T), bo=A["L0_attn_bo"].reshape(1, D),
        W1T=np.ascontiguousarray(A["L0_ffn_w1"].T), b1=A["L0_ffn_b1"].reshape(1, DFF),
        W2T=np.ascontiguousarray(A["L0_ffn_w2"].T), b2=A["L0_ffn_b2"].reshape(1, D),
        WfT=np.ascontiguousarray(wf.T), Wf_r=wf,
        Wo1T=np.ascontiguousarray(A["L1_attn_w_o"].T), Wo1_r=A["L1_attn_w_o"],
        qgain=A["L1_attn_q_gain"].reshape(1, NH),
        Wg_r=A["L1_ffn_w_gate"], Wu_r=A["L1_ffn_w_up"], Wd_r=A["L1_ffn_w_down"],
    )
    xs = A["L0_attn_x"]
    return [dict(common, x=np.ascontiguousarray(xs[i])) for i in range(8)]


def kernel(**inputs):
    nc = build()
    in_maps = _stage(inputs)
    res = run_bass_kernel_spmd(nc, in_maps, core_ids=list(range(8)))
    out = np.stack([np.asarray(r["out"]) for r in res.results], axis=0)
    return out.astype(np.float32)


# version 04
# speedup vs baseline: 1.3078x; 1.0320x over previous
import math
import numpy as np

import concourse.bass as bass
import concourse.mybir as mybir
from concourse.bass_utils import run_bass_kernel_spmd
from concourse.alu_op_type import AluOpType as OP

F32 = mybir.dt.float32
BF16 = mybir.dt.bfloat16
I32 = mybir.dt.int32
FP8E4 = mybir.dt.float8e4
AF = mybir.ActivationFunctionType
AX = mybir.AxisListType.X

S = 2048
D = 2048
DFF = 8192
NH = 16
HD = 128
NKV = 4
KVD = NKV * HD
FQ = D + 2 * KVD
NT = S // 128
ND = D // 128
NF = DFF // 128
LN2C = float(np.float32(math.log(2.0)))
RLN2 = float(np.float32(1.0 / math.log(2.0)))
LN6 = float(np.float32(math.log(6.0)))
LN64 = float(np.float32(math.log(64.0)))
L10K64 = float(-math.log(10000.0) / 64.0)
EPS_LN = 1e-5
EPS_ACT = 1e-5
EPS_W = 1e-8
EPS_I8 = 1e-5
F32EPS = float(np.finfo(np.float32).eps)
ISQHD = float(np.float32(1.0 / math.sqrt(HD)))
PI2_HI = float(np.float32(6.2831854820251465))
PI2_LO = float(np.float32(-1.7484556000744487e-07))
HALF_PI = float(np.float32(math.pi / 2))
I2PI = float(np.float32(1.0 / (2 * math.pi)))

ENGS = ("sync", "scalar", "vector", "tensor", "gpsimd")
DEV_TAIL = "none"
DEV_DBG_SERIAL = False


class Thunk:
    __slots__ = ("op", "a", "k", "inc")

    def __init__(self, op, a, k):
        self.op, self.a, self.k, self.inc = op, a, k, None

    def then_inc(self, sem, n):
        self.inc = (sem, n)
        return self


class EngQ:
    def __init__(self):
        self.L = []

    def __getattr__(self, op):
        if op.startswith("_"):
            raise AttributeError(op)

        def call(*a, **k):
            t = Thunk(op, a, k)
            self.L.append(t)
            return t
        return call

    def replay(self, e):
        drainable = {"tensor_scalar", "tensor_tensor", "scalar_tensor_tensor",
                     "tensor_copy", "tensor_reduce", "activation", "reciprocal",
                     "bn_aggr", "select", "tensor_tensor_reduce"}

        def is_col_ap(v):
            try:
                return hasattr(v, "free_size") and v.free_size() <= 8
            except Exception:
                return False

        for t in self.L:
            if t.op in drainable:
                ins = list(t.a) + [v for kk, v in t.k.items() if kk != "out"]
                if any(is_col_ap(v) for v in ins):
                    e.drain()
            r = getattr(e, t.op)(*t.a, **t.k)
            if t.inc is not None:
                r.then_inc(*t.inc)


def build():
    nc = bass.Bass("TRN2", target_bir_lowering=False)
    dt = nc.dram_tensor

    x_in = dt("x", [S, D], F32, kind="ExternalInput").ap()
    out_h = dt("out", [S, D], F32, kind="ExternalOutput").ap()

    WqT = dt("WqT", [D, D], F32, kind="ExternalInput").ap()
    WkT = dt("WkT", [D, D], F32, kind="ExternalInput").ap()
    WvT = dt("WvT", [D, D], F32, kind="ExternalInput").ap()
    WoT = dt("WoT", [D, D], F32, kind="ExternalInput").ap()
    bq_h = dt("bq", [1, D], F32, kind="ExternalInput").ap()
    bk_h = dt("bk", [1, D], F32, kind="ExternalInput").ap()
    bv_h = dt("bv", [1, D], F32, kind="ExternalInput").ap()
    bo_h = dt("bo", [1, D], F32, kind="ExternalInput").ap()
    W1T = dt("W1T", [D, DFF], F32, kind="ExternalInput").ap()
    W2T = dt("W2T", [DFF, D], F32, kind="ExternalInput").ap()
    b1_h = dt("b1", [1, DFF], F32, kind="ExternalInput").ap()
    b2_h = dt("b2", [1, D], F32, kind="ExternalInput").ap()
    WfT = dt("WfT", [D, FQ], F32, kind="ExternalInput").ap()
    Wf_r = dt("Wf_r", [FQ, D], F32, kind="ExternalInput").ap()
    Wo1T = dt("Wo1T", [D, D], F32, kind="ExternalInput").ap()
    Wo1_r = dt("Wo1_r", [D, D], F32, kind="ExternalInput").ap()
    qgain_h = dt("qgain", [1, NH], F32, kind="ExternalInput").ap()
    Wg_r = dt("Wg_r", [DFF, D], F32, kind="ExternalInput").ap()
    Wu_r = dt("Wu_r", [DFF, D], F32, kind="ExternalInput").ap()
    Wd_r = dt("Wd_r", [D, DFF], F32, kind="ExternalInput").ap()

    X1 = dt("X1", [S, D], F32, kind="Internal").ap()
    GSP = dt("GSP", [S, DFF], F32, kind="Internal").ap()
    FFN0 = dt("FFN0", [S, D], F32, kind="Internal").ap()
    ATT1 = dt("ATT1", [S, D], F32, kind="Internal").ap()
    YSP = dt("YSP", [S, D], BF16, kind="Internal").ap()
    QINT = dt("QINT", [S, DFF], BF16, kind="Internal").ap()
    QTs = dt("QTs", [D, S], BF16, kind="Internal").ap()
    KTs = dt("KTs", [D, S], BF16, kind="Internal").ap()
    VTs = dt("VTs", [D, S], BF16, kind="Internal").ap()
    W2Q = dt("W2Q", [DFF, D], BF16, kind="Internal").ap()
    WFQ = dt("WFQ", [D, FQ], BF16, kind="Internal").ap()
    WOQ = dt("WOQ", [D, D], BF16, kind="Internal").ap()
    WGQT = dt("WGQT", [D, DFF], BF16, kind="Internal").ap()
    WUQT = dt("WUQT", [D, DFF], BF16, kind="Internal").ap()
    WDQT = dt("WDQT", [DFF, D], BF16, kind="Internal").ap()

    ctxs = []

    def sbt(name, shape, dtype):
        cm = nc.sbuf_tensor(name, list(shape), dtype)
        t = cm.__enter__()
        ctxs.append(cm)
        return t

    def pstt(name, shape, dtype=F32):
        cm = nc.psum_tensor(name, list(shape), dtype)
        t = cm.__enter__()
        ctxs.append(cm)
        return t

    AR1 = sbt("AR1", [128, 32768], BF16)
    AR2 = sbt("AR2", [128, 16384], BF16)
    WR = sbt("WR", [128, 10240], BF16)
    wldA = sbt("wldA", [128, 2048], F32)
    wldB = sbt("wldB", [128, 2048], F32)
    tmpA = sbt("tmpA", [128, 2048], F32)
    tmpB = sbt("tmpB", [128, 2048], F32)
    tmpHH = sbt("tmpHH", [128, 4096], BF16)
    tmpF8 = sbt("tmpF8", [128, 2048], FP8E4)
    vext = sbt("vext", [128, NT * 132], BF16)
    colA = sbt("colA", [128, 64], F32)
    colB = sbt("colB", [128, 64], F32)
    colC = sbt("colC", [128, 64], F32)
    colD = sbt("colD", [128, 64], F32)
    colE = sbt("colE", [128, 64], F32)
    colI = sbt("colI", [128, 64], I32)
    pidx = sbt("pidx", [128, 1], F32)
    stat = sbt("stat", [128, NT * 8], F32)
    gsum = sbt("gsum", [128, NT], F32)
    gsq = sbt("gsq", [128, NT], F32)
    sqscr = sbt("sqscr", [128, 512], F32)
    gmax = sbt("gmax", [128, NT], F32)
    gmin = sbt("gmin", [128, NT], F32)
    invf = sbt("invf", [128, 64], F32)
    ropeC = sbt("ropeC", [128, NT * 64], BF16)
    ropeS = sbt("ropeS", [128, NT * 64], BF16)
    srow = sbt("srow", [1, S], BF16)
    srow2 = sbt("srow2", [1, S], BF16)
    rbufA = sbt("rbufA", [1, 512], BF16)
    rbufB = sbt("rbufB", [1, 512], BF16)
    rbufR = sbt("rbufR", [1, 128], F32)
    onesM = sbt("onesM", [1, 128], F32)
    onesMb = sbt("onesMb", [1, 128], BF16)
    ones_r = sbt("ones_r", [1, 512], BF16)
    onesc_f = sbt("onesc_f", [128, 1], F32)
    eyeb = sbt("eyeb", [128, 128], BF16)
    eyef = sbt("eyef", [128, 128], F32)
    bdmask = sbt("bdmask", [128, 128], BF16)
    bdmask4 = sbt("bdmask4", [128, 512], BF16)
    dgmask = sbt("dgmask", [128, 4 * 512], BF16)
    gainb = sbt("gainb", [128, NH], F32)
    scal = sbt("scal", [128, 16], F32)
    xtile = sbt("xtile", [128, 2048], F32)
    wsA = sbt("wsA", [128, FQ], BF16)
    wsB = sbt("wsB", [128, D], BF16)

    psum = [pstt(f"ps{i}", [128, 512], F32) for i in range(6)]
    psT = pstt("psT", [128, 640], BF16)
    psM = pstt("psM", [128, 512], F32)

    sems = {}
    for name in ["qS", "qA", "qG", "pe", "vec", "sc", "gp", "ph", "evd"]:
        sems[name] = nc.semaphore(name).__enter__()

    cnt = {n: 0 for n in sems}
    Q = {e: EngQ() for e in ENGS}
    qS, qA, qV, qT, qG = (Q[e] for e in ENGS)

    def inc(name, v=1):
        cnt[name] += v
        return cnt[name]

    def dma(qeng, out, in_, transpose=False):
        sem_name = {"sync": "qS", "scalar": "qA", "gpsimd": "qG"}[qeng]
        Q[qeng].dma_start(out=out, in_=in_, transpose=transpose).then_inc(
            sems[sem_name], 16)
        return inc(sem_name, 16)

    def wait(eng, sem_name, val=None):
        Q[eng].wait_ge(sems[sem_name], cnt[sem_name] if val is None else val)

    def sig(eng, th, sem_name="vec" if True else None, n=1):
        th.then_inc(sems[sem_name], n)
        return inc(sem_name, n)

    bar_n = [0]

    def barrier():
        bar_n[0] += 1
        tgt = 5 * bar_n[0]
        for e in ENGS:
            if e == "sync":
                wait(e, "qS")
            elif e == "scalar":
                wait(e, "qA")
                Q[e].drain()
            elif e == "gpsimd":
                wait(e, "qG")
                Q[e].drain()
            else:
                Q[e].drain()
            Q[e].sem_inc(sems["ph"], 1)
            Q[e].wait_ge(sems["ph"], tgt)

    bankfree = [("ph", 0)] * 8

    def bank_wait(i):
        sname, v = bankfree[i]
        Q["tensor"].wait_ge(sems[sname], v)

    def bank_mark(i, sem_name):
        bankfree[i] = (sem_name, cnt[sem_name])

    def ar1(dd, t0, w):
        return AR1[:, dd * S + t0: dd * S + t0 + w]

    IH = tmpHH[:].bitcast(I32)

    qV.memset(onesM[:], 1.0)
    qV.memset(onesMb[:], 1.0)
    qV.memset(ones_r[:], 1.0)
    qV.memset(onesc_f[:], 1.0)
    qV.memset(gmax[:], -3.0e38)
    qV.memset(gmin[:], 3.0e38)
    qV.memset(gsum[:], 0.0)
    qV.memset(gsq[:], 0.0)
    qV.memset(scal[:, 10:11], LN2C)
    qV.memset(scal[:, 11:12], -LN2C)
    qV.memset(scal[:, 12:13], LN6)
    qV.memset(scal[:, 13:14], LN64)
    qV.memset(scal[:, 14:15], -LN64)
    for _vc in range(NT):
        qV.memset(vext[:, _vc * 132 + 128:_vc * 132 + 129], 1.0)
    sig("vector", qV.drain(), "vec")

    qG.iota(out=pidx[:], pattern=[[0, 1]], base=0, channel_multiplier=1,
            allow_small_or_imprecise_dtypes=True)
    qG.iota(out=tmpB[:, 0:128], pattern=[[1, 128]], base=0, channel_multiplier=0,
            allow_small_or_imprecise_dtypes=True)
    qG.iota(out=tmpB[:, 512:1024], pattern=[[1, 512]], base=0, channel_multiplier=-1,
            allow_small_or_imprecise_dtypes=True)
    qG.iota(out=colC[0:1, 0:64], pattern=[[1, 64]], base=0, channel_multiplier=0,
            allow_small_or_imprecise_dtypes=True)
    qG.memset(eyeb[:], 1.0)
    qG.memset(eyef[:], 1.0)
    qG.drain()
    qG.affine_select(out=eyeb[:], in_=eyeb[:], pattern=[[1, 128]],
                     compare_op=OP.is_equal, fill=0.0, base=0, channel_multiplier=-1)
    qG.affine_select(out=eyef[:], in_=eyef[:], pattern=[[1, 128]],
                     compare_op=OP.is_equal, fill=0.0, base=0, channel_multiplier=-1)
    sig("gpsimd", qG.drain(), "gp")

    wait("vector", "gp")
    qV.tensor_scalar(out=colB[:, 0:1], in0=pidx[:], scalar1=1.0 / 16.0,
                     scalar2=-0.499, op0=OP.mult, op1=OP.add)
    qV.tensor_copy(out=colI[:, 0:1], in_=colB[:, 0:1])
    qV.tensor_copy(out=colB[:, 1:2], in_=colI[:, 0:1])
    qV.tensor_scalar(out=tmpA[:, 0:128], in0=tmpB[:, 0:128], scalar1=1.0 / 16.0,
                     scalar2=-0.499, op0=OP.mult, op1=OP.add)
    qV.tensor_copy(out=IH[:, 0:128], in_=tmpA[:, 0:128])
    qV.tensor_copy(out=tmpA[:, 128:256], in_=IH[:, 0:128])
    qV.tensor_tensor(out=bdmask[:], in0=tmpA[:, 128:256],
                     in1=colB[:, 1:2].broadcast_to((128, 128)), op=OP.is_equal)
    for a in range(4):
        qV.tensor_scalar(out=dgmask[:, a * 512:(a + 1) * 512], in0=tmpB[:, 512:1024],
                         scalar1=float(128 * a), scalar2=None, op0=OP.is_ge)
    sig("vector", qV.drain(), "vec")

    wait("tensor", "vec")
    bank_wait(7)
    qT.matmul(out=psM[:, 0:64], lhsT=onesM[:], rhs=colC[0:1, 0:64], start=True, stop=True)
    sig("tensor", qT.drain(), "pe")
    wait("scalar", "pe")
    qA.activation(out=invf[:], in_=psM[:, 0:64], func=AF.Exp, scale=L10K64)
    sc_invf = sig("scalar", qA.drain(), "sc")
    bank_mark(7, "sc")
    dma("sync", rbufR[0:1, 0:16], qgain_h[:, 0:NH])
    wait("tensor", "qS")
    bank_wait(7)
    qT.matmul(out=psM[:, 64:64 + NH], lhsT=onesM[:], rhs=rbufR[0:1, 0:16], start=True, stop=True)
    sig("tensor", qT.drain(), "pe")
    wait("vector", "pe")
    qV.tensor_copy(out=gainb[:], in_=psM[:, 64:64 + NH])
    v_g = sig("vector", qV.drain(), "vec")
    bank_mark(7, "vec")

    wait("vector", "sc", sc_invf)
    for tt in range(NT):
        qV.tensor_scalar(out=colD[:, 0:1], in0=pidx[:], scalar1=float(tt * 128),
                         scalar2=None, op0=OP.add)
        qV.drain()
        qV.tensor_scalar(out=colA[:, 0:64], in0=invf[:], scalar1=colD[:, 0:1],
                         scalar2=None, op0=OP.mult)
        qV.tensor_scalar(out=colE[:, 0:64], in0=colA[:, 0:64], scalar1=I2PI,
                         scalar2=None, op0=OP.mult)
        qV.tensor_copy(out=colI[:, 0:64], in_=colE[:, 0:64])
        qV.scalar_tensor_tensor(out=colE[:, 0:64], in0=colI[:, 0:64], scalar=-PI2_HI,
                                in1=colA[:, 0:64], op0=OP.mult, op1=OP.add)
        qV.scalar_tensor_tensor(out=colD[:, 0:64], in0=colI[:, 0:64], scalar=-PI2_LO,
                                in1=colE[:, 0:64], op0=OP.mult, op1=OP.add)
        qV.tensor_scalar(out=colB[:, 0:64], in0=colA[:, 0:64], scalar1=HALF_PI,
                         scalar2=None, op0=OP.add)
        qV.tensor_scalar(out=colE[:, 0:64], in0=colB[:, 0:64], scalar1=I2PI,
                         scalar2=None, op0=OP.mult)
        qV.tensor_copy(out=colI[:, 0:64], in_=colE[:, 0:64])
        qV.scalar_tensor_tensor(out=colE[:, 0:64], in0=colI[:, 0:64], scalar=-PI2_HI,
                                in1=colB[:, 0:64], op0=OP.mult, op1=OP.add)
        qV.scalar_tensor_tensor(out=colB[:, 0:64], in0=colI[:, 0:64], scalar=-PI2_LO,
                                in1=colE[:, 0:64], op0=OP.mult, op1=OP.add)
        v_r = sig("vector", qV.drain(), "vec")
        wait("scalar", "vec", v_r)
        qA.activation(out=ropeS[:, tt * 64:(tt + 1) * 64], in_=colD[:, 0:64], func=AF.Sin)
        qA.activation(out=ropeC[:, tt * 64:(tt + 1) * 64], in_=colB[:, 0:64], func=AF.Sin)
        s_r = sig("scalar", qA.drain(), "sc")
        wait("vector", "sc", s_r)

    qV.memset(colA[:, 2:4], 0.0)
    qV.drain()

    def stream_stat(src, n_rt, n_ch, acc_col):
        toks = []
        wait("sync", "vec")
        for i in range(n_rt * n_ch):
            rr, cc = divmod(i, n_ch)
            buf = (wldA, wldB)[i % 2]
            if len(toks) >= 2:
                wait("sync", "vec", toks[-2])
            tok = dma("sync", buf[:], src[rr * 128:(rr + 1) * 128, cc * 2048:(cc + 1) * 2048])
            wait("vector", "qS", tok)
            qV.tensor_reduce(out=colB[:, 8:9], in_=buf[:], axis=AX, op=OP.add,
                             apply_absolute_value=True)
            qV.tensor_tensor(out=acc_col, in0=acc_col, in1=colB[:, 8:9], op=OP.add)
            toks.append(sig("vector", qV.drain(), "vec"))

    stream_stat(W1T, ND, 4, colA[:, 2:3])
    stream_stat(W2T, NF, 1, colA[:, 3:4])
    v_now = cnt["vec"]
    wait("tensor", "vec", v_now)
    bank_wait(7)
    qT.matmul(out=psM[0:1, 128:129], lhsT=colA[:, 2:3], rhs=onesc_f[:], start=True, stop=True)
    qT.matmul(out=psM[0:1, 129:130], lhsT=colA[:, 3:4], rhs=onesc_f[:], start=True, stop=True)
    sig("tensor", qT.drain(), "pe")
    wait("vector", "pe")
    qV.tensor_scalar(out=scal[0:1, 8:9], in0=psM[0:1, 128:129],
                     scalar1=1.0 / (D * DFF), scalar2=EPS_W, op0=OP.mult, op1=OP.max)
    qV.tensor_scalar(out=scal[0:1, 9:10], in0=psM[0:1, 129:130],
                     scalar1=1.0 / (D * DFF), scalar2=EPS_W, op0=OP.mult, op1=OP.max)
    v_now = sig("vector", qV.drain(), "vec")
    wait("tensor", "vec", v_now)
    qT.matmul(out=psM[:, 130:131], lhsT=onesM[:], rhs=scal[0:1, 8:9], start=True, stop=True)
    qT.matmul(out=psM[:, 131:132], lhsT=onesM[:], rhs=scal[0:1, 9:10], start=True, stop=True)
    sig("tensor", qT.drain(), "pe")
    wait("vector", "pe")
    qV.tensor_copy(out=scal[:, 0:1], in_=psM[:, 130:131])
    qV.tensor_copy(out=scal[:, 2:3], in_=psM[:, 131:132])
    qV.drain()
    qV.reciprocal(out=scal[:, 1:2], in_=scal[:, 0:1])
    qV.reciprocal(out=scal[:, 3:4], in_=scal[:, 2:3])
    sig("vector", qV.drain(), "vec")
    bank_mark(7, "vec")

    g_toks = []
    for rt in range(NF):
        buf = (wldA, wldB)[rt % 2]
        if rt >= 2:
            wait("sync", "vec", v_w2q[rt - 2])
        tok = dma("sync", buf[:], W2T[rt * 128:(rt + 1) * 128, :])
        wait("vector", "qS", tok)
        if rt >= 2:
            wait("vector", "qG", g_toks[rt - 2])
        ob = WR[:, (rt % 2) * 2048:(rt % 2) * 2048 + 2048]
        qV.tensor_scalar(out=tmpA[:], in0=buf[:], scalar1=scal[:, 3:4], scalar2=None,
                         op0=OP.mult)
        qV.tensor_copy(out=IH[:], in_=tmpA[:])
        qV.tensor_scalar(out=ob, in0=IH[:], scalar1=1.0, scalar2=-1.0,
                         op0=OP.min, op1=OP.max)
        vtok = sig("vector", qV.drain(), "vec")
        if rt == 0:
            v_w2q = []
        v_w2q.append(vtok)
        wait("gpsimd", "vec", vtok)
        g_toks.append(dma("gpsimd", W2Q[rt * 128:(rt + 1) * 128, :], ob))

    barrier()

    for tt in range(NT):
        tok = dma("sync", xtile[:], x_in[tt * 128:(tt + 1) * 128, :])
        wait("vector", "qS", tok)
        for c in range(4):
            qV.bn_stats(out=tmpB[:, c * 6:c * 6 + 6], in_=xtile[:, c * 512:(c + 1) * 512])
        qV.drain()
        qV.bn_aggr(out=colA[:, 0:2], in_=tmpB[:, 0:24])
        qV.drain()
        qV.tensor_scalar(out=colB[:, 0:1], in0=colA[:, 1:2], scalar1=EPS_LN,
                         scalar2=None, op0=OP.add)
        qV.drain()
        qV.reciprocal(out=colB[:, 1:2], in_=colB[:, 0:1])
        v1 = sig("vector", qV.drain(), "vec")
        wait("scalar", "vec", v1)
        qA.activation(out=colB[:, 2:3], in_=colB[:, 1:2], func=AF.Sqrt)
        s1 = sig("scalar", qA.drain(), "sc")
        wait("vector", "sc", s1)
        qV.tensor_scalar(out=tmpA[:], in0=xtile[:], scalar1=colA[:, 0:1],
                         scalar2=colB[:, 2:3], op0=OP.subtract, op1=OP.mult)
        for c in range(4):
            qV.bn_stats(out=tmpB[:, c * 6:c * 6 + 6], in_=tmpA[:, c * 512:(c + 1) * 512])
        qV.tensor_reduce(out=colD[:, 0:1], in_=tmpA[:], axis=AX, op=OP.max)
        qV.tensor_reduce(out=colD[:, 1:2], in_=tmpA[:], axis=AX, op=OP.min)
        qV.drain()
        qV.bn_aggr(out=colC[:, 0:2], in_=tmpB[:, 0:24])
        qV.drain()
        qV.tensor_scalar(out=colB[:, 0:1], in0=colC[:, 1:2], scalar1=EPS_LN,
                         scalar2=None, op0=OP.add)
        qV.drain()
        qV.reciprocal(out=colB[:, 1:2], in_=colB[:, 0:1])
        v2 = sig("vector", qV.drain(), "vec")
        wait("scalar", "vec", v2)
        qA.activation(out=colC[:, 2:3], in_=colB[:, 1:2], func=AF.Sqrt)
        s2 = sig("scalar", qA.drain(), "sc")
        wait("vector", "sc", s2)
        qV.tensor_tensor(out=colE[:, 0:1], in0=colD[:, 0:1], in1=colC[:, 0:1], op=OP.subtract)
        qV.tensor_tensor(out=colE[:, 1:2], in0=colC[:, 0:1], in1=colD[:, 1:2], op=OP.subtract)
        qV.tensor_tensor(out=colE[:, 2:3], in0=colE[:, 0:1], in1=colE[:, 1:2], op=OP.max)
        qV.tensor_tensor(out=colE[:, 3:4], in0=colE[:, 2:3], in1=colC[:, 2:3], op=OP.mult)
        qV.tensor_scalar(out=colE[:, 4:5], in0=colE[:, 3:4], scalar1=EPS_ACT,
                         scalar2=None, op0=OP.max)
        qV.drain()
        qV.reciprocal(out=colE[:, 5:6], in_=colE[:, 4:5])
        qV.tensor_tensor(out=colD[:, 2:3], in0=colE[:, 4:5], in1=scal[:, 0:1], op=OP.mult)
        qV.tensor_scalar(out=stat[:, tt * 8:tt * 8 + 1], in0=colD[:, 2:3],
                         scalar1=1.0 / 127.0, scalar2=None, op0=OP.mult)
        qV.tensor_tensor(out=colD[:, 3:4], in0=colE[:, 5:6], in1=scal[:, 1:2], op=OP.mult)
        qV.tensor_scalar(out=colD[:, 4:5], in0=colD[:, 3:4], scalar1=127.0,
                         scalar2=None, op0=OP.mult)
        qV.tensor_tensor(out=colE[:, 6:7], in0=colE[:, 5:6], in1=colC[:, 2:3], op=OP.mult)
        qV.tensor_scalar(out=colE[:, 7:8], in0=colE[:, 6:7], scalar1=127.0,
                         scalar2=None, op0=OP.mult)
        qV.drain()
        qV.tensor_scalar(out=tmpB[:], in0=tmpA[:], scalar1=colC[:, 0:1],
                         scalar2=colE[:, 7:8], op0=OP.subtract, op1=OP.mult)
        qV.tensor_copy(out=IH[:], in_=tmpB[:])
        qV.tensor_scalar(out=WR[:, 0:2048], in0=IH[:], scalar1=127.0, scalar2=-128.0,
                         op0=OP.min, op1=OP.max)
        v3 = sig("vector", qV.drain(), "vec")
        wait("tensor", "vec", v3)
        bank_wait(7)
        qT.transpose(out=psM[0:1, 0:128], in_=colD[:, 4:5], identity=eyef[:])
        p1 = sig("tensor", qT.drain(), "pe")
        wait("vector", "pe", p1)
        qV.tensor_copy(out=srow[0:1, tt * 128:(tt + 1) * 128], in_=psM[0:1, 0:128])
        sig("vector", qV.drain(), "vec")
        bank_mark(7, "vec")
        wait("sync", "vec")
        for c in range(ND):
            dma("sync", AR1[:, c * S + tt * 128: c * S + (tt + 1) * 128],
                WR[:, c * 128:(c + 1) * 128], transpose=True)

    barrier()

    NFC = DFF // 512
    a2rhs = [WR[:, 0:8192], AR2[:, 0:8192]]
    a2bias = [rbufA, rbufB]
    a2slot = [tmpB[:, i * 512:(i + 1) * 512] for i in range(4)] + \
             [xtile[:, i * 512:(i + 1) * 512] for i in range(4)]
    wld_tok = [0, 0]
    a2_pe_end = {}
    a2_btok = {}
    a2_rhs_done = {}
    a2_store_tok = [0] * 8

    def a2_emit_quant(fc, ddc):
        rb = a2rhs[fc % 2]
        buf = (wldA, wldB)[ddc % 2]
        if ddc == 0:
            if fc >= 2:
                wait("gpsimd", "pe", a2_pe_end[fc - 2])
            a2_btok[fc] = dma("gpsimd", a2bias[fc % 2][:],
                              b1_h[:, fc * 512:(fc + 1) * 512])
        wait("sync", "vec", wld_tok[ddc % 2])
        tok = dma("sync", buf[:, 0:512],
                  W1T[ddc * 128:(ddc + 1) * 128, fc * 512:(fc + 1) * 512])
        wait("vector", "qS", tok)
        if ddc == 0 and fc >= 2:
            wait("vector", "pe", a2_pe_end[fc - 2])
        qV.tensor_scalar(out=tmpA[:, 0:512], in0=buf[:, 0:512], scalar1=scal[:, 1:2],
                         scalar2=None, op0=OP.mult)
        qV.tensor_copy(out=IH[:, 0:512], in_=tmpA[:, 0:512])
        qV.tensor_scalar(out=rb[:, ddc * 512:(ddc + 1) * 512], in0=IH[:, 0:512],
                         scalar1=1.0, scalar2=-1.0, op0=OP.min, op1=OP.max)
        wld_tok[ddc % 2] = sig("vector", qV.drain(), "vec")
        if ddc == ND - 1:
            a2_rhs_done[fc] = wld_tok[ddc % 2]

    for ddc in range(ND):
        a2_emit_quant(0, ddc)
    for fc in range(NFC):
        rb = a2rhs[fc % 2]
        wait("tensor", "vec", a2_rhs_done[fc])
        wait("tensor", "qG", a2_btok[fc])
        for tt in range(NT):
            if fc + 1 < NFC:
                a2_emit_quant(fc + 1, tt)
            b = tt % 6
            bank_wait(b)
            for ddc in range(ND):
                qT.matmul(out=psum[b][:], lhsT=AR1[:, ddc * S + tt * 128: ddc * S + (tt + 1) * 128],
                          rhs=rb[:, ddc * 512:(ddc + 1) * 512],
                          start=(ddc == 0), stop=False)
            mm = qT.matmul(out=psum[b][:], lhsT=srow[0:1, tt * 128:(tt + 1) * 128],
                           rhs=a2bias[fc % 2][:], start=False, stop=True)
            ptok = sig("tensor", mm, "pe")
            if tt == NT - 1:
                a2_pe_end[fc] = ptok
            wait("scalar", "pe", ptok)
            wait("scalar", "qG", a2_store_tok[tt % 8])
            gslot = a2slot[tt % 8]
            qA.activation(out=gslot, in_=psum[b][:], func=AF.Gelu,
                          scale=stat[:, tt * 8:tt * 8 + 1])
            stok = sig("scalar", qA.drain(), "sc")
            bank_mark(b, "sc")
            wait("vector", "sc", stok)
            qV.tensor_reduce(out=colA[:, 8:9], in_=gslot, axis=AX, op=OP.max)
            qV.tensor_reduce(out=colA[:, 9:10], in_=gslot, axis=AX, op=OP.min)
            qV.tensor_scalar(out=sqscr[:], in0=gslot, scalar1=1.0, scalar2=0.0,
                             op0=OP.mult, op1=OP.add, accum_out=colA[:, 10:11])
            qV.scalar_tensor_tensor(out=sqscr[:], in0=gslot, scalar=1.0, in1=gslot,
                                    op0=OP.mult, op1=OP.mult, accum_out=colA[:, 11:12])
            qV.drain()
            qV.tensor_tensor(out=gmax[:, tt:tt + 1], in0=gmax[:, tt:tt + 1],
                             in1=colA[:, 8:9], op=OP.max)
            qV.tensor_tensor(out=gmin[:, tt:tt + 1], in0=gmin[:, tt:tt + 1],
                             in1=colA[:, 9:10], op=OP.min)
            qV.tensor_tensor(out=gsum[:, tt:tt + 1], in0=gsum[:, tt:tt + 1],
                             in1=colA[:, 10:11], op=OP.add)
            qV.tensor_tensor(out=gsq[:, tt:tt + 1], in0=gsq[:, tt:tt + 1],
                             in1=colA[:, 11:12], op=OP.add)
            vtok = sig("vector", qV.drain(), "vec")
            wait("gpsimd", "vec", vtok)
            a2_store_tok[tt % 8] = dma(
                "gpsimd", GSP[tt * 128:(tt + 1) * 128, fc * 512:(fc + 1) * 512], gslot)

    barrier()

    for tt in range(NT):
        qV.tensor_scalar(out=colA[:, 0:1], in0=gsum[:, tt:tt + 1], scalar1=1.0 / DFF,
                         scalar2=None, op0=OP.mult)
        qV.tensor_scalar(out=colA[:, 2:3], in0=gsq[:, tt:tt + 1], scalar1=1.0 / DFF,
                         scalar2=None, op0=OP.mult)
        qV.drain()
        qV.tensor_tensor(out=colA[:, 3:4], in0=colA[:, 0:1], in1=colA[:, 0:1], op=OP.mult)
        qV.tensor_tensor(out=colA[:, 1:2], in0=colA[:, 2:3], in1=colA[:, 3:4],
                         op=OP.subtract)
        qV.tensor_scalar(out=colB[:, 0:1], in0=colA[:, 1:2], scalar1=EPS_LN,
                         scalar2=None, op0=OP.add)
        qV.drain()
        qV.reciprocal(out=colB[:, 1:2], in_=colB[:, 0:1])
        v1 = sig("vector", qV.drain(), "vec")
        wait("scalar", "vec", v1)
        qA.activation(out=colB[:, 2:3], in_=colB[:, 1:2], func=AF.Sqrt)
        s1 = sig("scalar", qA.drain(), "sc")
        wait("vector", "sc", s1)
        qV.tensor_tensor(out=colE[:, 0:1], in0=gmax[:, tt:tt + 1], in1=colA[:, 0:1],
                         op=OP.subtract)
        qV.tensor_tensor(out=colE[:, 1:2], in0=colA[:, 0:1], in1=gmin[:, tt:tt + 1],
                         op=OP.subtract)
        qV.tensor_tensor(out=colE[:, 2:3], in0=colE[:, 0:1], in1=colE[:, 1:2], op=OP.max)
        qV.tensor_tensor(out=colE[:, 3:4], in0=colE[:, 2:3], in1=colB[:, 2:3], op=OP.mult)
        qV.tensor_scalar(out=colE[:, 4:5], in0=colE[:, 3:4], scalar1=EPS_ACT,
                         scalar2=None, op0=OP.max)
        qV.drain()
        qV.reciprocal(out=colE[:, 5:6], in_=colE[:, 4:5])
        qV.drain()
        qV.tensor_scalar(out=colE[:, 6:7], in0=colE[:, 5:6], scalar1=127.0,
                         scalar2=None, op0=OP.mult)
        qV.tensor_tensor(out=stat[:, tt * 8 + 1:tt * 8 + 2], in0=colE[:, 6:7],
                         in1=colB[:, 2:3], op=OP.mult)
        qV.tensor_copy(out=stat[:, tt * 8 + 2:tt * 8 + 3], in_=colA[:, 0:1])
        qV.tensor_tensor(out=colD[:, 0:1], in0=colE[:, 4:5], in1=scal[:, 2:3], op=OP.mult)
        qV.tensor_scalar(out=stat[:, tt * 8 + 3:tt * 8 + 4], in0=colD[:, 0:1],
                         scalar1=1.0 / 127.0, scalar2=None, op0=OP.mult)
        qV.tensor_tensor(out=colD[:, 1:2], in0=colE[:, 6:7], in1=scal[:, 3:4], op=OP.mult)
        v2 = sig("vector", qV.drain(), "vec")
        wait("tensor", "vec", v2)
        bank_wait(7)
        qT.transpose(out=psM[0:1, 0:128], in_=colD[:, 1:2], identity=eyef[:])
        p1 = sig("tensor", qT.drain(), "pe")
        wait("vector", "pe", p1)
        qV.tensor_copy(out=srow2[0:1, tt * 128:(tt + 1) * 128], in_=psM[0:1, 0:128])
        sig("vector", qV.drain(), "vec")
        bank_mark(7, "vec")

    barrier()

    WSTG = WR[:, 8192:10240]
    prev_vq = [0, 0]
    for grp in range(8):
        pe_guard = cnt["pe"]
        vq_toks = list(prev_vq)
        for ti in range(2):
            tt = grp * 2 + ti
            for c in range(4):
                i_lin = ti * 4 + c
                buf = (wldA, wldB)[i_lin % 2]
                wait("sync", "vec", vq_toks[i_lin])
                tok = dma("sync", buf[:], GSP[tt * 128:(tt + 1) * 128, c * 2048:(c + 1) * 2048])
                wait("vector", "qS", tok)
                if ti == 0 and c == 0:
                    wait("vector", "pe", pe_guard)
                wait("vector", "qS")
                qV.tensor_scalar(out=tmpA[:], in0=buf[:], scalar1=stat[:, tt * 8 + 2:tt * 8 + 3],
                                 scalar2=stat[:, tt * 8 + 1:tt * 8 + 2], op0=OP.subtract, op1=OP.mult)
                qV.tensor_copy(out=IH[:], in_=tmpA[:])
                qV.tensor_scalar(out=WSTG[:], in0=IH[:], scalar1=127.0, scalar2=-128.0,
                                 op0=OP.min, op1=OP.max)
                vtok = sig("vector", qV.drain(), "vec")
                vq_toks.append(vtok)
                if i_lin == 6:
                    prev_vq[0] = vtok
                if i_lin == 7:
                    prev_vq[1] = vtok
                wait("sync", "vec", vtok)
                for j in range(16):
                    ddg = c * 16 + j
                    dma("sync", AR2[:, ddg * 256 + ti * 128: ddg * 256 + (ti + 1) * 128],
                        WSTG[:, j * 128:(j + 1) * 128], transpose=True)
        t_group_ready = cnt["qS"]
        if DEV_TAIL == "ar2g" and grp == 5:
            wait("gpsimd", "qS", t_group_ready)
            for rr in range(16):
                dma("gpsimd", out_h[rr * 128:(rr + 1) * 128, 0:1024],
                    AR2[:, rr * 1024:(rr + 1) * 1024])
            wait("gpsimd", "qG")
            wait("sync", "qG")
            wait("vector", "qG")
            wait("tensor", "qG")
        for foc in range(4):
            wait("gpsimd", "pe")
            b2tok = dma("gpsimd", rbufA[:], b2_h[:, foc * 512:(foc + 1) * 512])
            wait("tensor", "qS", t_group_ready)
            wait("tensor", "qG", b2tok)
            rtoks = []
            for ddg in range(NF):
                slot = WR[:, (ddg % 2) * 512:(ddg % 2) * 512 + 512]
                wait("sync", "pe")
                rtoks.append(dma("sync", slot, W2Q[ddg * 128:(ddg + 1) * 128,
                                                   foc * 512:(foc + 1) * 512]))
                wait("tensor", "qS", rtoks[-1])
                for ti in range(2):
                    if ddg == 0:
                        bank_wait(ti)
                    mmr = qT.matmul(out=psum[ti][:], lhsT=AR2[:, ddg * 256 + ti * 128: ddg * 256 + (ti + 1) * 128],
                                    rhs=slot, start=(ddg == 0), stop=False)
                    if ti == 1:
                        sig("tensor", mmr, "pe")
            for ti in range(2):
                tt = grp * 2 + ti
                mm = qT.matmul(out=psum[ti][:], lhsT=srow2[0:1, tt * 128:(tt + 1) * 128],
                               rhs=rbufA[:], start=False, stop=True)
                ptok = sig("tensor", mm, "pe")
                wait("vector", "pe", ptok)
                oslot = tmpB[:, (ti % 4) * 512:(ti % 4) * 512 + 512]
                qV.tensor_scalar(out=oslot, in0=psum[ti][:], scalar1=stat[:, tt * 8 + 3:tt * 8 + 4],
                                 scalar2=None, op0=OP.mult)
                vtok = sig("vector", qV.drain(), "vec")
                bank_mark(ti, "vec")
                wait("gpsimd", "vec", vtok)
                dma("gpsimd", FFN0[tt * 128:(tt + 1) * 128, foc * 512:(foc + 1) * 512], oslot)
            wait("vector", "qG")

    barrier()

    for tt in range(NT):
        tok = dma("sync", xtile[:], x_in[tt * 128:(tt + 1) * 128, :])
        wait("vector", "qS", tok)
        if tt > 0:
            wait("vector", "qS")
        for c in range(4):
            qV.bn_stats(out=tmpB[:, c * 6:c * 6 + 6], in_=xtile[:, c * 512:(c + 1) * 512])
        qV.drain()
        qV.bn_aggr(out=colA[:, 0:2], in_=tmpB[:, 0:24])
        qV.tensor_scalar(out=colB[:, 0:1], in0=colA[:, 1:2], scalar1=EPS_LN,
                         scalar2=None, op0=OP.add)
        qV.reciprocal(out=colB[:, 1:2], in_=colB[:, 0:1])
        v1 = sig("vector", qV.drain(), "vec")
        wait("scalar", "vec", v1)
        qA.activation(out=colB[:, 2:3], in_=colB[:, 1:2], func=AF.Sqrt)
        s1 = sig("scalar", qA.drain(), "sc")
        wait("vector", "sc", s1)
        qV.tensor_scalar(out=WSTG[:], in0=xtile[:], scalar1=colA[:, 0:1],
                         scalar2=colB[:, 2:3], op0=OP.subtract, op1=OP.mult)
        vtok = sig("vector", qV.drain(), "vec")
        wait("sync", "vec", vtok)
        for c in range(ND):
            dma("sync", AR1[:, c * S + tt * 128: c * S + (tt + 1) * 128],
                WSTG[:, c * 128:(c + 1) * 128], transpose=True)

    barrier()

    for (WT_h, b_h, dst) in ((WqT, bq_h, QTs), (WkT, bk_h, KTs), (WvT, bv_h, VTs)):
        for fc in range(4):
            btok = dma("gpsimd", rbufA[:], b_h[:, fc * 512:(fc + 1) * 512])
            for tc in range(4):
                for dd in range(ND):
                    slot = WR[:, (dd % 2) * 512:(dd % 2) * 512 + 512]
                    wait("gpsimd", "pe")
                    wtok = dma("gpsimd", slot,
                               WT_h[dd * 128:(dd + 1) * 128, fc * 512:(fc + 1) * 512])
                    wait("tensor", "qG", wtok)
                    for fi in range(4):
                        if dd == 0:
                            bank_wait(fi)
                        mmr = qT.matmul(out=psum[fi][:],
                                        lhsT=slot[:, fi * 128:(fi + 1) * 128],
                                        rhs=AR1[:, dd * S + tc * 512: dd * S + (tc + 1) * 512],
                                        start=(dd == 0), stop=False)
                        if fi == 3:
                            sig("tensor", mmr, "pe")
                wait("tensor", "qG", btok)
                for fi in range(4):
                    mm = qT.matmul(out=psum[fi][:], lhsT=rbufA[0:1, fi * 128:(fi + 1) * 128],
                                   rhs=ones_r[:], start=False, stop=True)
                    ptok = sig("tensor", mm, "pe")
                    wait("vector", "pe", ptok)
                    if not (fc == 0 and tc == 0 and fi == 0):
                        wait("vector", "qG")
                    oslot = tmpHH[:, (fi % 2) * 512:(fi % 2) * 512 + 512]
                    qV.tensor_copy(out=oslot, in_=psum[fi][:])
                    vtok = sig("vector", qV.drain(), "vec")
                    bank_mark(fi, "vec")
                    wait("gpsimd", "vec", vtok)
                    dma("gpsimd", dst[(fc * 4 + fi) * 128:(fc * 4 + fi + 1) * 128,
                                      tc * 512:(tc + 1) * 512], oslot)

    barrier()

    yeR = AR2[:, 12288:16384].rearrange("p (h t) -> p t h", h=16)
    qR = AR2[:, 0:4096].rearrange("p (h t) -> p t h", h=16)
    kR = AR2[:, 4096:8192].rearrange("p (h t) -> p t h", h=16)
    vR = AR2[:, 8192:12288].rearrange("p (h t) -> p t h", h=16)
    for e8 in range(8):
        pe_guard = cnt["pe"]
        v_guard = cnt["vec"]
        wait("sync", "pe", pe_guard)
        wait("sync", "vec", v_guard)
        for ft in range(16):
            dma("sync", AR2[:, 0 + ft * 256:0 + (ft + 1) * 256],
                QTs[ft * 128:(ft + 1) * 128, e8 * 256:(e8 + 1) * 256])
            dma("sync", AR2[:, 4096 + ft * 256:4096 + (ft + 1) * 256],
                KTs[ft * 128:(ft + 1) * 128, e8 * 256:(e8 + 1) * 256])
            dma("sync", AR2[:, 8192 + ft * 256:8192 + (ft + 1) * 256],
                VTs[ft * 128:(ft + 1) * 128, e8 * 256:(e8 + 1) * 256])
        t_loaded = cnt["qS"]
        wait("tensor", "qS", t_loaded)
        wait("vector", "qS", t_loaded)
        for g in range(32):
            t0 = g * 8
            bs = g % 2
            bc = 2 + (g % 2)
            qAP = qR[:, t0:t0 + 8, :]
            kAP = kR[:, t0:t0 + 8, :]
            vAP = vR[:, t0:t0 + 8, :]
            sl = (g % 2) * 1024
            E_s = tmpHH[:, sl + 0:sl + 128]
            A_s = tmpHH[:, sl + 128:sl + 256]
            AT_s = tmpHH[:, sl + 256:sl + 384]
            VG_s = tmpHH[:, sl + 384:sl + 512]
            CG_s = tmpHH[:, sl + 512:sl + 640]
            KG_s = tmpHH[:, sl + 640:sl + 768]
            VGT_s = tmpHH[:, sl + 768:sl + 896]
            QG_s = tmpHH[:, sl + 896:sl + 1024]
            qV.tensor_copy(out=KG_s, in_=kAP)
            qV.tensor_reduce(out=colB[:, 0:8], in_=vAP, axis=AX, op=OP.add)
            qV.drain()
            qV.tensor_copy(out=VGT_s.rearrange("p (t h) -> p t h", h=16),
                           in_=colB[:, 0:8].unsqueeze(2).broadcast_to((128, 8, 16)))
            qV.tensor_copy(out=QG_s, in_=qAP)
            vk = sig("vector", qV.drain(), "vec")
            wait("tensor", "vec", vk)
            bank_wait(bs)
            mm = qT.matmul(out=psum[bs][:, 0:128], lhsT=QG_s, rhs=KG_s, start=True, stop=True)
            p1 = sig("tensor", mm, "pe")
            wait("scalar", "pe", p1)
            qA.activation(out=E_s, in_=psum[bs][:, 0:128], func=AF.Exp, scale=ISQHD)
            s1 = sig("scalar", qA.drain(), "sc")
            bank_mark(bs, "sc")
            wait("vector", "sc", s1)
            qV.tensor_tensor(out=A_s, in0=E_s, in1=bdmask[:], op=OP.mult)
            qV.drain()
            qV.tensor_reduce(out=colA[:, 0:1], in_=A_s, axis=AX, op=OP.add)
            qV.reciprocal(out=colA[:, 1:2], in_=colA[:, 0:1])
            qV.tensor_scalar(out=A_s, in0=A_s, scalar1=colA[:, 1:2], scalar2=None,
                             op0=OP.mult)
            v1 = sig("vector", qV.drain(), "vec")
            wait("tensor", "vec", v1)
            bank_wait(6)
            qT.transpose(out=psT[:, 0:128], in_=A_s, identity=eyeb[:])
            qT.transpose(out=psT[:, 128:256], in_=VGT_s, identity=eyeb[:])
            p2 = sig("tensor", qT.drain(), "pe")
            wait("vector", "pe", p2)
            qV.tensor_copy(out=AT_s, in_=psT[:, 0:128])
            qV.tensor_copy(out=VG_s, in_=psT[:, 128:256])
            v2 = sig("vector", qV.drain(), "vec")
            bank_mark(6, "vec")
            wait("tensor", "vec", v2)
            bank_wait(bc)
            mm = qT.matmul(out=psum[bc][:, 0:128], lhsT=AT_s, rhs=VG_s, start=True, stop=True)
            p3 = sig("tensor", mm, "pe")
            wait("vector", "pe", p3)
            qV.tensor_copy(out=CG_s, in_=psum[bc][:, 0:128])
            v3 = sig("vector", qV.drain(), "vec")
            bank_mark(bc, "vec")
            wait("tensor", "vec", v3)
            bank_wait(6)
            qT.transpose(out=psT[:, 256:384], in_=CG_s, identity=eyeb[:])
            p4 = sig("tensor", qT.drain(), "pe")
            wait("vector", "pe", p4)
            qV.tensor_copy(out=yeR[:, t0:t0 + 8, :], in_=psT[:, 256:384])
            sig("vector", qV.drain(), "vec")
            bank_mark(6, "vec")
            if DEV_TAIL == "grp0" and e8 == 0 and g == 0:
                wait("gpsimd", "vec")
                dma("gpsimd", out_h[0:128, 0:128], QG_s)
                dma("gpsimd", out_h[0:128, 128:256], KG_s)
                dma("gpsimd", out_h[0:128, 256:384], E_s)
                dma("gpsimd", out_h[0:128, 384:512], tmpHH[:, sl + 128:sl + 256])
                dma("gpsimd", out_h[0:128, 512:640], tmpHH[:, sl + 384:sl + 512])
                dma("gpsimd", out_h[0:128, 640:768], tmpHH[:, sl + 512:sl + 640])
                dma("gpsimd", out_h[128:256, 0:2048], AR2[:, 12288:14336])
                dma("gpsimd", out_h[384:512, 0:2048], AR2[:, 0:2048])
                wait("gpsimd", "qG")
                for _e in ("sync", "vector", "tensor", "scalar"):
                    wait(_e, "qG")
        v_ye = cnt["vec"]
        for tt2 in range(2):
            rtt = e8 * 256 + tt2 * 128
            for foc in range(4):
                botok = dma("gpsimd", rbufB[:], bo_h[:, foc * 512:(foc + 1) * 512])
                xtok = dma("sync", wldA[:, 0:512], x_in[rtt:rtt + 128, foc * 512:(foc + 1) * 512])
                ftok = dma("sync", wldA[:, 512:1024], FFN0[rtt:rtt + 128, foc * 512:(foc + 1) * 512])
                for ff in range(16):
                    slot = WR[:, (ff % 2) * 512:(ff % 2) * 512 + 512]
                    wait("gpsimd", "pe")
                    wtok = dma("gpsimd", slot,
                               WoT[ff * 128:(ff + 1) * 128, foc * 512:(foc + 1) * 512])
                    wait("tensor", "qG", wtok)
                    if ff == 0:
                        wait("tensor", "vec", v_ye)
                        bank_wait(4 + tt2 % 2)
                    mmr = qT.matmul(out=psum[4 + tt2 % 2][:],
                                    lhsT=AR2[:, 12288 + ff * 256 + tt2 * 128: 12288 + ff * 256 + tt2 * 128 + 128],
                                    rhs=slot, start=(ff == 0), stop=False)
                    sig("tensor", mmr, "pe")
                wait("tensor", "qG", botok)
                mm = qT.matmul(out=psum[4 + tt2 % 2][:], lhsT=onesMb[:], rhs=rbufB[:],
                               start=False, stop=True)
                ptok = sig("tensor", mm, "pe")
                wait("vector", "pe", ptok)
                wait("vector", "qS", ftok)
                oslot = tmpB[:, (foc % 4) * 512:(foc % 4) * 512 + 512]
                qV.tensor_tensor(out=oslot, in0=psum[4 + tt2 % 2][:], in1=wldA[:, 0:512],
                                 op=OP.add)
                qV.tensor_tensor(out=oslot, in0=oslot, in1=wldA[:, 512:1024], op=OP.add)
                vtok = sig("vector", qV.drain(), "vec")
                bank_mark(4 + tt2 % 2, "vec")
                wait("gpsimd", "vec", vtok)
                dma("gpsimd", X1[rtt:rtt + 128, foc * 512:(foc + 1) * 512], oslot)
                wait("sync", "qG")

    barrier()

    def row_stats_bcast(src_r, n_rt, dst_ws, iws_region):
        for rt in range(n_rt):
            buf = (wldA, wldB)[rt % 2]
            wait("sync", "vec")
            tok = dma("sync", buf[:], src_r[rt * 128:(rt + 1) * 128, :])
            wait("vector", "qS", tok)
            qV.tensor_reduce(out=colA[:, 0:1], in_=buf[:], axis=AX, op=OP.max,
                             apply_absolute_value=True)
            qV.tensor_scalar(out=colA[:, 1:2], in0=colA[:, 0:1], scalar1=1.0 / 127.0,
                             scalar2=EPS_I8, op0=OP.mult, op1=OP.max)
            qV.reciprocal(out=colA[:, 2:3], in_=colA[:, 1:2])
            v1 = sig("vector", qV.drain(), "vec")
            wait("tensor", "vec", v1)
            bank_wait(7)
            qT.transpose(out=psM[0:1, 0:128], in_=colA[:, 1:2], identity=eyef[:])
            qT.transpose(out=psM[0:1, 128:256], in_=colA[:, 2:3], identity=eyef[:])
            p1 = sig("tensor", qT.drain(), "pe")
            wait("vector", "pe", p1)
            qV.tensor_copy(out=rbufR[0:1, 0:128], in_=psM[0:1, 0:128])
            qV.tensor_copy(out=rbufB[0:1, 0:128].bitcast(F32), in_=psM[0:1, 128:256]) \
                if False else qV.tensor_copy(out=colB[0:1, 0:64], in_=psM[0:1, 128:192])
            qV.tensor_copy(out=colC[0:1, 0:64], in_=psM[0:1, 192:256])
            v2 = sig("vector", qV.drain(), "vec")
            wait("tensor", "vec", v2)
            qT.matmul(out=psM[:, 256:384], lhsT=onesM[:], rhs=rbufR[0:1, 0:128],
                      start=True, stop=True)
            qT.matmul(out=psM[:, 384:448], lhsT=onesM[:], rhs=colB[0:1, 0:64],
                      start=True, stop=True)
            qT.matmul(out=psM[:, 448:512], lhsT=onesM[:], rhs=colC[0:1, 0:64],
                      start=True, stop=True)
            p2 = sig("tensor", qT.drain(), "pe")
            wait("vector", "pe", p2)
            qV.tensor_copy(out=dst_ws[:, rt * 128:(rt + 1) * 128], in_=psM[:, 256:384])
            qV.tensor_copy(out=iws_region[:, rt * 128:rt * 128 + 64], in_=psM[:, 384:448])
            qV.tensor_copy(out=iws_region[:, rt * 128 + 64:(rt + 1) * 128], in_=psM[:, 448:512])
            sig("vector", qV.drain(), "vec")
            bank_mark(7, "vec")

    iwsF = AR2[:, 0:FQ]
    iwsO = AR2[:, FQ:FQ + D]
    row_stats_bcast(Wf_r, FQ // 128, wsA, iwsF)
    row_stats_bcast(Wo1_r, ND, wsB, iwsO)

    def quant_int8_T(srcT, n_dd, width, iws_region, dstQ):
        for dd in range(n_dd):
            for c in range((width + 2047) // 2048):
                w = min(2048, width - c * 2048)
                buf = (wldA, wldB)[(dd + c) % 2]
                wait("sync", "vec")
                wait("sync", "qG")
                tok = dma("sync", buf[:, 0:w],
                          srcT[dd * 128:(dd + 1) * 128, c * 2048:c * 2048 + w])
                wait("vector", "qS", tok)
                qV.tensor_tensor(out=tmpA[:, 0:w], in0=buf[:, 0:w],
                                 in1=iws_region[:, c * 2048:c * 2048 + w], op=OP.mult)
                qV.tensor_copy(out=IH[:, 0:w], in_=tmpA[:, 0:w])
                qV.tensor_scalar(out=WSTG[:, 0:w], in0=IH[:, 0:w], scalar1=127.0,
                                 scalar2=-127.0, op0=OP.min, op1=OP.max)
                vtok = sig("vector", qV.drain(), "vec")
                wait("gpsimd", "vec", vtok)
                dma("gpsimd", dstQ[dd * 128:(dd + 1) * 128, c * 2048:c * 2048 + w],
                    WSTG[:, 0:w])
                wait("vector", "qG")

    quant_int8_T(WfT, ND, FQ, iwsF, WFQ)
    quant_int8_T(Wo1T, ND, D, iwsO, WOQ)

    barrier()

    TBW = WR[:, 0:2048]

    def quant_mx_w(src_r, n_rt, width, dstQT):
        nch = width // 2048
        for rti in range(n_rt * nch):
            rt, ch = divmod(rti, nch)
            buf = (wldA, wldB)[rti % 2]
            wait("sync", "vec")
            tok = dma("sync", buf[:],
                      src_r[rt * 128:(rt + 1) * 128, ch * 2048:(ch + 1) * 2048])
            wait("vector", "qS", tok)
            qV.tensor_reduce(out=colA[:, 0:64],
                             in_=buf[:].rearrange("p (b k) -> p b k", k=32),
                             axis=AX, op=OP.max, apply_absolute_value=True)
            v1 = sig("vector", qV.drain(), "vec")
            wait("scalar", "vec", v1)
            qA.activation(out=colB[:, 0:64], in_=colA[:, 0:64], func=AF.Ln)
            s1 = sig("scalar", qA.drain(), "sc")
            wait("vector", "sc", s1)
            qV.tensor_scalar(out=colB[:, 0:64], in0=colB[:, 0:64], scalar1=RLN2,
                             scalar2=None, op0=OP.mult)
            qV.tensor_copy(out=colI[:, 0:64], in_=colB[:, 0:64])
            qV.tensor_tensor(out=colC[:, 0:64], in0=colI[:, 0:64], in1=colB[:, 0:64],
                             op=OP.is_gt)
            qV.tensor_tensor(out=colD[:, 0:64], in0=colI[:, 0:64], in1=colC[:, 0:64],
                             op=OP.subtract)
            qV.tensor_scalar(out=colD[:, 0:64], in0=colD[:, 0:64], scalar1=-2.0,
                             scalar2=127.0, op0=OP.add, op1=OP.min)
            qV.tensor_scalar(out=colD[:, 0:64], in0=colD[:, 0:64], scalar1=-127.0,
                             scalar2=None, op0=OP.max)
            qV.tensor_tensor(out=tmpA[:], in0=buf[:], in1=buf[:], op=OP.mult)
            v2 = sig("vector", qV.drain(), "vec")
            wait("scalar", "vec", v2)
            qA.activation(out=tmpB[:], in_=tmpA[:], func=AF.Ln)
            s2 = sig("scalar", qA.drain(), "sc")
            wait("vector", "sc", s2)
            qV.tensor_scalar(out=tmpB[:], in0=tmpB[:], scalar1=RLN2 * 0.5,
                             scalar2=-0.5, op0=OP.mult, op1=OP.add)
            qV.tensor_copy(out=IH[:], in_=tmpB[:])
            qV.tensor_tensor(out=tmpB[:].rearrange("p (b k) -> p b k", k=32),
                             in0=IH[:].rearrange("p (b k) -> p b k", k=32),
                             in1=colD[:, 0:64].unsqueeze(2).broadcast_to((128, 64, 32)),
                             op=OP.max)
            v3 = sig("vector", qV.drain(), "vec")
            wait("scalar", "vec", v3)
            qA.activation(out=tmpA[:], in_=tmpB[:], func=AF.Exp, scale=-LN2C, bias=scal[:, 10:11])
            qA.activation(out=colC[:, 0:64], in_=colD[:, 0:64], func=AF.Exp,
                          scale=LN2C, bias=scal[:, 12:13])
            s3 = sig("scalar", qA.drain(), "sc")
            wait("vector", "sc", s3)
            qV.tensor_tensor(out=tmpA[:], in0=buf[:], in1=tmpA[:], op=OP.mult)
            qV.tensor_copy(out=IH[:], in_=tmpA[:])
            v4 = sig("vector", qV.drain(), "vec")
            wait("scalar", "vec", v4)
            qA.activation(out=tmpB[:], in_=tmpB[:], func=AF.Exp, scale=LN2C, bias=scal[:, 11:12])
            s4 = sig("scalar", qA.drain(), "sc")
            wait("vector", "sc", s4)
            qV.tensor_scalar(out=colB[:, 0:64], in0=colC[:, 0:64], scalar1=-1.0,
                             scalar2=None, op0=OP.mult)
            qV.tensor_tensor(out=tmpA[:], in0=IH[:], in1=tmpB[:], op=OP.mult)
            qV.tensor_tensor(out=tmpA[:].rearrange("p (b k) -> p b k", k=32),
                             in0=tmpA[:].rearrange("p (b k) -> p b k", k=32),
                             in1=colC[:, 0:64].unsqueeze(2).broadcast_to((128, 64, 32)),
                             op=OP.min)
            qV.tensor_tensor(out=WSTG[:].rearrange("p (b k) -> p b k", k=32),
                             in0=tmpA[:].rearrange("p (b k) -> p b k", k=32),
                             in1=colB[:, 0:64].unsqueeze(2).broadcast_to((128, 64, 32)),
                             op=OP.max)
            vtok = sig("vector", qV.drain(), "vec")
            wait("tensor", "vec", vtok)
            for cg in range(4):
                wait("tensor", "gp")
                for c in range(cg * 4, cg * 4 + 4):
                    tr = qT.transpose(out=psT[:, (c % 4) * 128:((c % 4) + 1) * 128],
                                      in_=WSTG[:, c * 128:(c + 1) * 128],
                                      identity=eyeb[:])
                qT.transpose(out=psT[:, 512:640], in_=eyeb[:], identity=eyeb[:])
                ptk = sig("tensor", qT.drain(), "pe")
                wait("vector", "pe", ptk)
                if cg == 0:
                    wait("vector", "qG")
                for c in range(cg * 4, cg * 4 + 4):
                    qV.tensor_copy(out=TBW[:, c * 128:(c + 1) * 128],
                                   in_=psT[:, (c % 4) * 128:((c % 4) + 1) * 128])
                sig("vector", qV.drain(), "gp")
            wait("gpsimd", "gp")
            dma("gpsimd", dstQT[ch * 2048:(ch + 1) * 2048, rt * 128:(rt + 1) * 128]
                .rearrange("(c p) j -> p c j", p=128),
                TBW[:].rearrange("p (c j) -> p c j", j=128))
            wait("vector", "pe")

    quant_mx_w(Wg_r, NF, D, WGQT)
    quant_mx_w(Wu_r, NF, D, WUQT)
    quant_mx_w(Wd_r, ND, DFF, WDQT)

    barrier()

    def mx_act_quant(src_f32, w, dst_bf16, nblk, dbg=None):
        kb = 32
        qV.tensor_reduce(out=colA[:, 0:nblk],
                         in_=src_f32.rearrange("p (b k) -> p b k", k=kb),
                         axis=AX, op=OP.max, apply_absolute_value=True)
        v1 = sig("vector", qV.drain(), "vec")
        wait("scalar", "vec", v1)
        qA.activation(out=colB[:, 0:nblk], in_=colA[:, 0:nblk], func=AF.Ln)
        s1 = sig("scalar", qA.drain(), "sc")
        wait("vector", "sc", s1)
        if dbg is not None:
            qV.tensor_copy(out=wsB[:, dbg:dbg + 8], in_=colA[:, 0:nblk])
            qV.tensor_copy(out=wsB[:, dbg + 8:dbg + 16], in_=colB[:, 0:nblk])
        qV.tensor_scalar(out=colB[:, 0:nblk], in0=colB[:, 0:nblk], scalar1=RLN2,
                         scalar2=None, op0=OP.mult)
        qV.tensor_copy(out=colI[:, 0:nblk], in_=colB[:, 0:nblk])
        qV.tensor_tensor(out=colC[:, 0:nblk], in0=colI[:, 0:nblk], in1=colB[:, 0:nblk],
                         op=OP.is_gt)
        qV.tensor_tensor(out=colD[:, 0:nblk], in0=colI[:, 0:nblk], in1=colC[:, 0:nblk],
                         op=OP.subtract)
        qV.tensor_scalar(out=colD[:, 0:nblk], in0=colD[:, 0:nblk], scalar1=-2.0,
                         scalar2=127.0, op0=OP.add, op1=OP.min)
        qV.tensor_scalar(out=colD[:, 0:nblk], in0=colD[:, 0:nblk], scalar1=-127.0,
                         scalar2=None, op0=OP.max)
        if dbg is not None:
            qV.drain()
            qV.tensor_copy(out=wsB[:, dbg + 16:dbg + 24], in_=colB[:, 0:nblk])
            qV.tensor_copy(out=wsB[:, dbg + 24:dbg + 32], in_=colI[:, 0:nblk])
            qV.tensor_copy(out=wsB[:, dbg + 32:dbg + 40], in_=colD[:, 0:nblk])
        v2 = sig("vector", qV.drain(), "vec")
        wait("scalar", "vec", v2)
        qA.activation(out=colB[:, 0:nblk], in_=colD[:, 0:nblk], func=AF.Exp,
                      scale=-LN2C, bias=scal[:, 14:15])
        qA.activation(out=colC[:, 0:nblk], in_=colD[:, 0:nblk], func=AF.Exp,
                      scale=LN2C, bias=scal[:, 13:14])
        s2 = sig("scalar", qA.drain(), "sc")
        wait("vector", "sc", s2)
        if dbg is not None:
            qV.tensor_copy(out=wsB[:, dbg + 40:dbg + 48], in_=colB[:, 0:nblk])
            qV.tensor_copy(out=wsB[:, dbg + 48:dbg + 56], in_=colC[:, 0:nblk])
        qV.tensor_tensor(out=tmpA[:, 0:w].rearrange("p (b k) -> p b k", k=kb),
                         in0=src_f32.rearrange("p (b k) -> p b k", k=kb),
                         in1=colB[:, 0:nblk].unsqueeze(2).broadcast_to((128, nblk, kb)),
                         op=OP.mult)
        qV.tensor_scalar(out=tmpA[:, 0:w], in0=tmpA[:, 0:w], scalar1=7.5 / 64.0,
                         scalar2=-7.5 / 64.0, op0=OP.min, op1=OP.max)
        qV.tensor_copy(out=tmpF8[:, 0:w], in_=tmpA[:, 0:w])
        qV.drain()
        qV.tensor_tensor(out=dst_bf16.rearrange("p (b k) -> p b k", k=kb),
                         in0=tmpF8[:, 0:w].rearrange("p (b k) -> p b k", k=kb),
                         in1=colC[:, 0:nblk].unsqueeze(2).broadcast_to((128, nblk, kb)),
                         op=OP.mult)
        return sig("vector", qV.drain(), "vec")

    q_read = [cnt["vec"]]
    for half in range(2):
        pe_g = cnt["pe"]
        for th in range(8):
            tt = half * 8 + th
            wait("sync", "vec")
            tok = dma("sync", xtile[:], X1[tt * 128:(tt + 1) * 128, :])
            wait("vector", "qS", tok)
            if th == 0:
                wait("vector", "pe", pe_g)
            for c in range(4):
                qV.bn_stats(out=tmpB[:, c * 6:c * 6 + 6], in_=xtile[:, c * 512:(c + 1) * 512])
            qV.drain()
            qV.bn_aggr(out=colE[:, 0:2], in_=tmpB[:, 0:24])
            qV.tensor_scalar(out=colE[:, 2:3], in0=colE[:, 1:2], scalar1=EPS_LN,
                             scalar2=None, op0=OP.add)
            qV.reciprocal(out=colE[:, 3:4], in_=colE[:, 2:3])
            v1 = sig("vector", qV.drain(), "vec")
            wait("scalar", "vec", v1)
            qA.activation(out=colE[:, 4:5], in_=colE[:, 3:4], func=AF.Sqrt)
            s1 = sig("scalar", qA.drain(), "sc")
            wait("vector", "sc", s1)
            qV.tensor_scalar(out=tmpB[:], in0=xtile[:], scalar1=colE[:, 0:1],
                             scalar2=colE[:, 4:5], op0=OP.subtract, op1=OP.mult)
            qV.drain()
            vtok = mx_act_quant(tmpB[:], 2048, WSTG[:], 64)
            wait("sync", "vec", vtok)
            for c in range(ND):
                dma("sync", AR2[:, c * 1024 + th * 128: c * 1024 + (th + 1) * 128],
                    WSTG[:, c * 128:(c + 1) * 128], transpose=True)
        t_half = cnt["qS"]
        wait("tensor", "qS", t_half)
        for fc in range(32):
            pe_now = cnt["pe"]
            wait("sync", "pe", pe_now)
            wait("scalar", "pe", pe_now)
            for dd in range(ND):
                qe = ("sync", "scalar")[dd % 2]
                dma(qe, WR[:, dd * 256:(dd + 1) * 256],
                    WGQT[dd * 128:(dd + 1) * 128, fc * 256:(fc + 1) * 256])
                dma(qe, WR[:, 4096 + dd * 256: 4096 + (dd + 1) * 256],
                    WUQT[dd * 128:(dd + 1) * 128, fc * 256:(fc + 1) * 256])
            wait("tensor", "qS", cnt["qS"])
            wait("tensor", "qA", cnt["qA"])
            for th in range(8):
                tt = half * 8 + th
                bg, bu = th % 2, 2 + th % 2
                bank_wait(bg)
                bank_wait(bu)
                for dd in range(ND):
                    qT.matmul(out=psum[bg][:, 0:256],
                              lhsT=AR2[:, dd * 1024 + th * 128: dd * 1024 + (th + 1) * 128],
                              rhs=WR[:, dd * 256:(dd + 1) * 256],
                              start=(dd == 0), stop=(dd == ND - 1))
                    mmr = qT.matmul(out=psum[bu][:, 0:256],
                                    lhsT=AR2[:, dd * 1024 + th * 128: dd * 1024 + (th + 1) * 128],
                                    rhs=WR[:, 4096 + dd * 256: 4096 + (dd + 1) * 256],
                                    start=(dd == 0), stop=(dd == ND - 1))
                ptok = sig("tensor", mmr, "pe")
                wait("scalar", "pe", ptok)
                if th == 0:
                    wait("scalar", "vec", q_read[0])
                sgs = tmpB[:, th * 256:(th + 1) * 256]
                qA.activation(out=sgs, in_=psum[bg][:, 0:256], func=AF.Silu)
                stok = sig("scalar", qA.drain(), "sc")
                bank_mark(bg, "sc")
                wait("vector", "sc", stok)
                qV.tensor_tensor(out=sgs, in0=sgs, in1=psum[bu][:, 0:256], op=OP.mult)
                sig("vector", qV.drain(), "vec")
                bank_mark(bu, "vec")
            wait("vector", "qG")
            q_read[0] = mx_act_quant(tmpB[:], 2048, WSTG[:], 64)
            wait("gpsimd", "vec", q_read[0])
            dma("gpsimd",
                QINT[half * 1024:(half + 1) * 1024, fc * 256:(fc + 1) * 256]
                .rearrange("(t p) c -> p t c", p=128),
                WSTG[:].rearrange("p (t c) -> p t c", c=256))

    if DEV_TAIL == "qx":
        wait("gpsimd", "qS")
        for c in range(ND):
            dma("gpsimd", GSP[c * 128:(c + 1) * 128, 0:1024], AR2[:, c * 1024:(c + 1) * 1024])
        wait("gpsimd", "qG")

    barrier()

    for tt in range(NT):
        wait("sync", "vec")
        tok = dma("sync", xtile[:], X1[tt * 128:(tt + 1) * 128, :])
        wait("vector", "qS", tok)
        for c in range(4):
            qV.bn_stats(out=tmpB[:, c * 6:c * 6 + 6], in_=xtile[:, c * 512:(c + 1) * 512])
        qV.drain()
        qV.bn_aggr(out=colE[:, 0:2], in_=tmpB[:, 0:24])
        qV.tensor_scalar(out=colE[:, 2:3], in0=colE[:, 1:2], scalar1=EPS_LN,
                         scalar2=None, op0=OP.add)
        qV.reciprocal(out=colE[:, 3:4], in_=colE[:, 2:3])
        v1 = sig("vector", qV.drain(), "vec")
        wait("scalar", "vec", v1)
        qA.activation(out=colE[:, 4:5], in_=colE[:, 3:4], func=AF.Sqrt)
        s1 = sig("scalar", qA.drain(), "sc")
        wait("vector", "sc", s1)
        qV.tensor_reduce(out=colD[:, 0:1], in_=xtile[:], axis=AX, op=OP.max)
        qV.tensor_reduce(out=colD[:, 1:2], in_=xtile[:], axis=AX, op=OP.min)
        qV.drain()
        qV.tensor_tensor(out=colD[:, 2:3], in0=colD[:, 0:1], in1=colE[:, 0:1], op=OP.subtract)
        qV.tensor_tensor(out=colD[:, 3:4], in0=colE[:, 0:1], in1=colD[:, 1:2], op=OP.subtract)
        qV.tensor_tensor(out=colD[:, 4:5], in0=colD[:, 2:3], in1=colD[:, 3:4], op=OP.max)
        qV.tensor_tensor(out=colD[:, 5:6], in0=colD[:, 4:5], in1=colE[:, 4:5], op=OP.mult)
        qV.tensor_scalar(out=stat[:, tt * 8 + 4:tt * 8 + 5], in0=colD[:, 5:6],
                         scalar1=1.0 / 127.0, scalar2=EPS_I8, op0=OP.mult, op1=OP.max)
        qV.drain()
        qV.reciprocal(out=colD[:, 6:7], in_=stat[:, tt * 8 + 4:tt * 8 + 5])
        qV.tensor_tensor(out=colD[:, 7:8], in0=colE[:, 4:5], in1=colD[:, 6:7], op=OP.mult)
        qV.drain()
        qV.tensor_scalar(out=tmpB[:], in0=xtile[:], scalar1=colE[:, 0:1],
                         scalar2=colD[:, 7:8], op0=OP.subtract, op1=OP.mult)
        qV.tensor_copy(out=IH[:], in_=tmpB[:])
        qV.tensor_scalar(out=WSTG[:], in0=IH[:], scalar1=127.0, scalar2=-127.0,
                         op0=OP.min, op1=OP.max)
        vtok = sig("vector", qV.drain(), "vec")
        wait("sync", "vec", vtok)
        for c in range(ND):
            dma("sync", AR1[:, c * S + tt * 128: c * S + (tt + 1) * 128],
                WSTG[:, c * 128:(c + 1) * 128], transpose=True)

    barrier()

    ebuf = WR[:, 0:8192]
    kT_g = AR2[:, 0:2048]
    qT_h = AR2[:, 2048:4096]
    WFS = AR2[:, 4096:4096 + 16 * 128]

    def l1_proj(fslice, dst_kind, rms_rope, gain_col, out_cb):
        wait("sync", "pe")
        for dd in range(ND):
            dma("sync", WFS[:, dd * 128:(dd + 1) * 128],
                WFQ[dd * 128:(dd + 1) * 128, fslice:fslice + 128])
        wtok = cnt["qS"]
        wait("tensor", "qS", wtok)
        for ttg in range(4):
            for ti in range(4):
                tt = ttg * 4 + ti
                if ti == 0:
                    bank_wait(4)
                for dd in range(ND):
                    mmr = qT.matmul(out=psum[4][:, ti * 128:(ti + 1) * 128],
                                    lhsT=AR1[:, dd * S + tt * 128: dd * S + (tt + 1) * 128],
                                    rhs=WFS[:, dd * 128:(dd + 1) * 128],
                                    start=(dd == 0), stop=(dd == ND - 1))
            ptok = sig("tensor", mmr, "pe")
            wait("vector", "pe", ptok)
            for ti in range(4):
                tt = ttg * 4 + ti
                pslot = psum[4][:, ti * 128:(ti + 1) * 128]
                fslot = tmpA[:, (ti % 2) * 1024:(ti % 2) * 1024 + 128]
                qV.scalar_tensor_tensor(out=fslot, in0=pslot,
                                        scalar=stat[:, tt * 8 + 4:tt * 8 + 5],
                                        in1=wsA[:, fslice:fslice + 128],
                                        op0=OP.mult, op1=OP.mult)
                if rms_rope:
                    qV.drain()
                    qV.scalar_tensor_tensor(out=tmpB[:, 0:128], in0=fslot, scalar=1.0,
                                            in1=fslot, op0=OP.mult, op1=OP.mult,
                                            accum_out=colA[:, 0:1])
                    qV.tensor_scalar(out=colA[:, 1:2], in0=colA[:, 0:1],
                                     scalar1=1.0 / 128.0, scalar2=F32EPS,
                                     op0=OP.mult, op1=OP.add)
                    qV.reciprocal(out=colA[:, 2:3], in_=colA[:, 1:2])
                    vv = sig("vector", qV.drain(), "vec")
                    wait("scalar", "vec", vv)
                    qA.activation(out=colA[:, 3:4], in_=colA[:, 2:3], func=AF.Sqrt)
                    ss = sig("scalar", qA.drain(), "sc")
                    wait("vector", "sc", ss)
                    if gain_col is not None:
                        qV.tensor_tensor(out=colA[:, 4:5], in0=colA[:, 3:4],
                                         in1=gain_col, op=OP.mult)
                    else:
                        qV.tensor_copy(out=colA[:, 4:5], in_=colA[:, 3:4])
                    qV.drain()
                    qV.tensor_scalar(out=fslot, in0=fslot, scalar1=colA[:, 4:5],
                                     scalar2=None, op0=OP.mult)
                    cosb = ropeC[:, tt * 64:(tt + 1) * 64]
                    sinb = ropeS[:, tt * 64:(tt + 1) * 64]
                    qV.tensor_tensor(out=tmpB[:, 0:64], in0=fslot[:, 0:64], in1=cosb, op=OP.mult)
                    qV.tensor_tensor(out=tmpB[:, 64:128], in0=fslot[:, 64:128], in1=sinb, op=OP.mult)
                    qV.tensor_tensor(out=tmpB[:, 128:192], in0=fslot[:, 0:64], in1=sinb, op=OP.mult)
                    qV.tensor_tensor(out=tmpB[:, 192:256], in0=fslot[:, 64:128], in1=cosb, op=OP.mult)
                    qV.tensor_tensor(out=WSTG[:, 0:64], in0=tmpB[:, 0:64],
                                     in1=tmpB[:, 64:128], op=OP.add)
                    qV.tensor_tensor(out=WSTG[:, 64:128], in0=tmpB[:, 192:256],
                                     in1=tmpB[:, 128:192], op=OP.subtract)
                else:
                    qV.tensor_copy(out=WSTG[:, 0:128], in_=fslot)
                vtok = sig("vector", qV.drain(), "vec")
                out_cb(tt, WSTG[:, 0:128], vtok)
            bank_mark(4, "vec")

    kv_gain = None
    for kv in range(NKV):
        def k_cb(tt, slot, vtok):
            wait("sync", "vec", vtok)
            dma("sync", kT_g[:, tt * 128:(tt + 1) * 128], slot, transpose=True)
        l1_proj(D + kv * 128, "k", True, None, k_cb)
        def v_cb(tt, slot, vtok):
            wait("gpsimd", "vec", vtok)
            g1 = dma("gpsimd", YSP[tt * 128:(tt + 1) * 128, 0:128], slot)
            wait("vector", "qG", g1)
        def v_cb2(tt, slot, vtok):
            qV.tensor_copy(out=vext[:, tt * 132:tt * 132 + 128], in_=slot)
            sig("vector", qV.drain(), "vec")
        l1_proj(D + KVD + kv * 128, "v", False, None, v_cb2)
        for qh in range(kv * 4, kv * 4 + 4):
            def q_cb(tt, slot, vtok):
                wait("sync", "vec", vtok)
                dma("sync", qT_h[:, tt * 128:(tt + 1) * 128], slot, transpose=True)
            l1_proj(qh * 128, "q", True, gainb[:, qh:qh + 1], q_cb)
            wait("tensor", "qS")
            wait("tensor", "vec")
            for tqc in range(4):
                ntk = 4 * tqc + 4
                sc_base = cnt["sc"]
                for tkt in range(ntk):
                    b = tkt % 2
                    bank_wait(b)
                    mm = qT.matmul(out=psum[b][:], lhsT=kT_g[:, tkt * 128:(tkt + 1) * 128],
                                   rhs=qT_h[:, tqc * 512:(tqc + 1) * 512],
                                   start=True, stop=True)
                    ptok = sig("tensor", mm, "pe")
                    wait("scalar", "pe", ptok)
                    qA.activation(out=ebuf[:, tkt * 512:(tkt + 1) * 512], in_=psum[b][:],
                                  func=AF.Exp, scale=ISQHD)
                    stok = sig("scalar", qA.drain(), "sc")
                    bank_mark(b, "sc")
                    if tkt >= 4 * tqc:
                        a = tkt - 4 * tqc
                        wait("vector", "sc", stok)
                        qV.tensor_tensor(out=ebuf[:, tkt * 512:(tkt + 1) * 512],
                                         in0=ebuf[:, tkt * 512:(tkt + 1) * 512],
                                         in1=dgmask[:, a * 512:(a + 1) * 512], op=OP.mult)
                        sig("vector", qV.drain(), "vec")
                wait("tensor", "sc")
                wait("tensor", "vec")
                for ti in range(4):
                    Tg = tqc * 4 + ti
                    b2 = 2 + ti % 2
                    bank_wait(b2)
                    for tkt in range(Tg + 1):
                        mm = qT.matmul(out=psum[b2][:, 0:129],
                                       lhsT=ebuf[:, tkt * 512 + ti * 128: tkt * 512 + (ti + 1) * 128],
                                       rhs=vext[:, tkt * 132:tkt * 132 + 129],
                                       start=(tkt == 0), stop=(tkt == Tg))
                    ptok = sig("tensor", mm, "pe")
                    wait("vector", "pe", ptok)
                    qV.reciprocal(out=colA[:, 8:9], in_=psum[b2][:, 128:129])
                    qV.drain()
                    yslot = tmpA[:, 1536 + (ti % 2) * 128: 1536 + (ti % 2) * 128 + 128]
                    qV.tensor_scalar(out=yslot, in0=psum[b2][:, 0:128],
                                     scalar1=colA[:, 8:9], scalar2=None, op0=OP.mult)
                    vtok = sig("vector", qV.drain(), "vec")
                    bank_mark(b2, "vec")
                    wait("gpsimd", "vec", vtok)
                    g1 = dma("gpsimd", YSP[Tg * 128:(Tg + 1) * 128, qh * 128:(qh + 1) * 128],
                             yslot.bitcast(F32) if False else yslot)
                    wait("vector", "qG", g1)

    barrier()

    YB = WR[:, 0:4096]
    for tt in range(NT):
        wait("sync", "vec")
        tok = dma("sync", YB[:, 0:2048], YSP[tt * 128:(tt + 1) * 128, :])
        wait("vector", "qS", tok)
        qV.tensor_reduce(out=colA[:, 0:1], in_=YB[:, 0:2048], axis=AX, op=OP.max,
                         apply_absolute_value=True)
        qV.tensor_scalar(out=stat[:, tt * 8 + 5:tt * 8 + 6], in0=colA[:, 0:1],
                         scalar1=1.0 / 127.0, scalar2=EPS_I8, op0=OP.mult, op1=OP.max)
        qV.drain()
        qV.reciprocal(out=colA[:, 1:2], in_=stat[:, tt * 8 + 5:tt * 8 + 6])
        qV.drain()
        qV.tensor_scalar(out=tmpB[:], in0=YB[:, 0:2048], scalar1=colA[:, 1:2],
                         scalar2=None, op0=OP.mult)
        qV.tensor_copy(out=IH[:], in_=tmpB[:])
        qV.tensor_scalar(out=WSTG[:], in0=IH[:], scalar1=127.0, scalar2=-127.0,
                         op0=OP.min, op1=OP.max)
        vtok = sig("vector", qV.drain(), "vec")
        wait("sync", "vec", vtok)
        for c in range(ND):
            dma("sync", AR1[:, c * S + tt * 128: c * S + (tt + 1) * 128],
                WSTG[:, c * 128:(c + 1) * 128], transpose=True)

    for foc in range(4):
        for ttg in range(4):
            wait("sync", "pe")
            rtoks = []
            for dd in range(ND):
                slot = WR[:, 4096 + (dd % 2) * 512: 4096 + (dd % 2) * 512 + 512]
                wait("sync", "pe")
                rtoks.append(dma("sync", slot, WOQ[dd * 128:(dd + 1) * 128,
                                                   foc * 512:(foc + 1) * 512]))
                wait("tensor", "qS", rtoks[-1])
                for ti in range(4):
                    tt = ttg * 4 + ti
                    if dd == 0:
                        bank_wait(ti)
                    mmr = qT.matmul(out=psum[ti][:],
                                    lhsT=AR1[:, dd * S + tt * 128: dd * S + (tt + 1) * 128],
                                    rhs=slot, start=(dd == 0), stop=(dd == ND - 1))
                    if ti == 3:
                        sig("tensor", mmr, "pe")
            for ti in range(4):
                tt = ttg * 4 + ti
                wait("vector", "pe")
                oslot = tmpB[:, (ti % 4) * 512:(ti % 4) * 512 + 512]
                qV.scalar_tensor_tensor(out=oslot, in0=psum[ti][:],
                                        scalar=stat[:, tt * 8 + 5:tt * 8 + 6],
                                        in1=wsB[:, foc * 512:(foc + 1) * 512],
                                        op0=OP.mult, op1=OP.mult)
                vtok = sig("vector", qV.drain(), "vec")
                bank_mark(ti, "vec")
                wait("gpsimd", "vec", vtok)
                dma("gpsimd", ATT1[tt * 128:(tt + 1) * 128, foc * 512:(foc + 1) * 512], oslot)
            wait("vector", "qG")

    barrier()

    for grp in range(4):
        pe_g = cnt["pe"]
        for ti in range(4):
            tt = grp * 4 + ti
            qe = ("sync", "scalar")[ti % 2]
            wait(qe, "pe", pe_g)
            for ddg in range(NF):
                dma(qe, AR1[:, ddg * 512 + ti * 128: ddg * 512 + (ti + 1) * 128],
                    QINT[tt * 128:(tt + 1) * 128, ddg * 128:(ddg + 1) * 128],
                    transpose=True)
        t_grpS, t_grpA = cnt["qS"], cnt["qA"]
        slot_tok = [0, 0]
        for foc in range(4):
            wait("tensor", "qS", t_grpS)
            wait("tensor", "qA", t_grpA)
            for ddg in range(NF):
                slot = WR[:, 8192 + (ddg % 2) * 512: 8192 + (ddg % 2) * 512 + 512]
                qe = ("sync", "scalar")[ddg % 2]
                wait(qe, "pe", slot_tok[ddg % 2])
                dtok = dma(qe, slot, WDQT[ddg * 128:(ddg + 1) * 128,
                                          foc * 512:(foc + 1) * 512])
                wait("tensor", "qS" if qe == "sync" else "qA", dtok)
                for ti in range(4):
                    if ddg == 0:
                        bank_wait(ti)
                    mmr = qT.matmul(out=psum[ti][:],
                                    lhsT=AR1[:, ddg * 512 + ti * 128: ddg * 512 + (ti + 1) * 128],
                                    rhs=slot, start=(ddg == 0), stop=(ddg == NF - 1))
                    if ti == 3:
                        slot_tok[ddg % 2] = sig("tensor", mmr, "pe")
            for ti in range(4):
                tt = grp * 4 + ti
                xtok = dma("sync", wldA[:, 0:512],
                           X1[tt * 128:(tt + 1) * 128, foc * 512:(foc + 1) * 512])
                atok = dma("sync", wldA[:, 512:1024],
                           ATT1[tt * 128:(tt + 1) * 128, foc * 512:(foc + 1) * 512])
                wait("vector", "pe")
                wait("vector", "qS", atok)
                oslot = tmpB[:, (ti % 4) * 512:(ti % 4) * 512 + 512]
                qV.tensor_tensor(out=oslot, in0=psum[ti][:], in1=wldA[:, 0:512], op=OP.add)
                qV.tensor_tensor(out=oslot, in0=oslot, in1=wldA[:, 512:1024], op=OP.add)
                vtok = sig("vector", qV.drain(), "vec")
                bank_mark(ti, "vec")
                wait("gpsimd", "vec", vtok)
                dma("gpsimd", out_h[tt * 128:(tt + 1) * 128, foc * 512:(foc + 1) * 512],
                    oslot)
                wait("sync", "qG")
            wait("vector", "qG")

    wait("gpsimd", "qG")


    src_map = {"x": x_in, "gsp": GSP, "x1": X1, "ffn0": FFN0, "att1": ATT1,
               "qts": None}
    if DEV_TAIL == "gsp":
        for tt in range(NT):
            dma("sync", out_h[tt * 128:(tt + 1) * 128, :],
                GSP[tt * 128:(tt + 1) * 128, 0:2048])
    elif DEV_TAIL in ("x1", "ffn0", "att1"):
        t = src_map[DEV_TAIL]
        for tt in range(NT):
            dma("sync", out_h[tt * 128:(tt + 1) * 128, :], t[tt * 128:(tt + 1) * 128, :])
    elif DEV_TAIL in ("qts", "kts", "vts"):
        _src = {"qts": QTs, "kts": KTs, "vts": VTs}[DEV_TAIL]
        for tt in range(NT):
            dma("gpsimd", out_h[tt * 128:(tt + 1) * 128, 0:2048],
                _src[tt * 128:(tt + 1) * 128, 0:2048])
        wait("gpsimd", "qG")
    elif DEV_TAIL in ("wgq", "wuq", "wdq"):
        _srcq = {"wgq": WGQT, "wuq": WUQT, "wdq": WDQT}[DEV_TAIL]
        for tt in range(NT):
            dma("gpsimd", out_h[tt * 128:(tt + 1) * 128, :], _srcq[tt * 128:(tt + 1) * 128, 0:2048])
        wait("gpsimd", "qG")
    elif DEV_TAIL == "qx":
        for tt in range(NT):
            dma("gpsimd", out_h[tt * 128:(tt + 1) * 128, 0:1024],
                GSP[tt * 128:(tt + 1) * 128, 0:1024])
        wait("gpsimd", "qG")
    elif DEV_TAIL == "gup":
        dma("gpsimd", out_h[0:128, 0:2048], GSP[0:128, 0:2048])
        dma("gpsimd", out_h[128:256, 0:1024], GSP[128:256, 0:1024])
        dma("gpsimd", out_h[128:256, 1024:1152], GSP[128:256, 1024:1152])
        wait("gpsimd", "qG")
    elif DEV_TAIL == "qint":
        for tt in range(NT):
            dma("gpsimd", out_h[tt * 128:(tt + 1) * 128, :], QINT[tt * 128:(tt + 1) * 128, 0:2048])
        wait("gpsimd", "qG")
    elif DEV_TAIL == "w2q":
        for tt in range(NT):
            dma("gpsimd", out_h[tt * 128:(tt + 1) * 128, :], W2Q[tt * 128:(tt + 1) * 128, :])
        wait("gpsimd", "qG")
    elif DEV_TAIL == "xq2b":
        for tt in range(8):
            wait("sync", "vec")
            wait("sync", "qG")
            tok = dma("sync", wldA[:], GSP[tt * 128:(tt + 1) * 128, 0:2048])
            wait("vector", "qS", tok)
            qV.tensor_scalar(out=tmpA[:], in0=wldA[:], scalar1=stat[:, tt * 8 + 2:tt * 8 + 3],
                             scalar2=stat[:, tt * 8 + 1:tt * 8 + 2], op0=OP.subtract, op1=OP.mult)
            qV.tensor_copy(out=IH[:], in_=tmpA[:])
            qV.tensor_scalar(out=tmpB[:], in0=IH[:], scalar1=127.0, scalar2=-128.0,
                             op0=OP.min, op1=OP.max)
            vtok = sig("vector", qV.drain(), "vec")
            wait("gpsimd", "vec", vtok)
            g1 = dma("gpsimd", out_h[tt * 128:(tt + 1) * 128, :], tmpB[:])
            g2 = dma("gpsimd", out_h[1024 + tt * 128:1024 + (tt + 1) * 128, :], wldA[:])
            wait("vector", "qG", g2)
    elif DEV_TAIL == "xq2":
        for tt in range(NT):
            wait("sync", "vec")
            wait("sync", "qG")
            tok = dma("sync", wldA[:], GSP[tt * 128:(tt + 1) * 128, 0:2048])
            wait("vector", "qS", tok)
            qV.tensor_scalar(out=tmpA[:], in0=wldA[:], scalar1=stat[:, tt * 8 + 2:tt * 8 + 3],
                             scalar2=stat[:, tt * 8 + 1:tt * 8 + 2], op0=OP.subtract, op1=OP.mult)
            qV.tensor_copy(out=IH[:], in_=tmpA[:])
            qV.tensor_scalar(out=tmpB[:], in0=IH[:], scalar1=127.0, scalar2=-128.0,
                             op0=OP.min, op1=OP.max)
            vtok = sig("vector", qV.drain(), "vec")
            wait("gpsimd", "vec", vtok)
            g2 = dma("gpsimd", out_h[tt * 128:(tt + 1) * 128, :], tmpB[:])
            wait("vector", "qG", g2)
    elif DEV_TAIL == "xq1t":
        for ddx in range(ND):
            dma("gpsimd", out_h[ddx * 128:(ddx + 1) * 128, :], AR1[:, ddx * S:(ddx + 1) * S])
        wait("gpsimd", "qG")
    elif DEV_TAIL == "srow":
        dma("gpsimd", out_h[0:1, 0:S], srow[:])
        dma("gpsimd", out_h[2:3, 0:S], srow2[:])
        dma("gpsimd", out_h[4:132, 0:NT * 8], stat[:])
        dma("gpsimd", out_h[140:268, 0:NT], gmax[:])
        dma("gpsimd", out_h[270:398, 0:NT], gmin[:])
        wait("gpsimd", "qG")
    elif DEV_TAIL == "ye":
        dma("gpsimd", out_h[0:128, 0:2048], AR2[:, 12288:14336])
        dma("gpsimd", out_h[128:256, 0:2048], AR2[:, 14336:16384])
        wait("gpsimd", "qG")
    elif DEV_TAIL in ("none", "ar2g", "grp0"):
        pass
    else:
        for tt in range(NT):
            dma("sync", out_h[tt * 128:(tt + 1) * 128, :], x_in[tt * 128:(tt + 1) * 128, :])
    wait("sync", "qS")

    blk = nc.Block().__enter__()
    for ename in ENGS:
        getattr(blk, ename)(lambda e, _q=Q[ename]: _q.replay(e))
    blk.__exit__(None, None, None)
    for cm in reversed(ctxs):
        cm.__exit__(None, None, None)
    return nc


def _stage(inputs):
    A = {k: np.ascontiguousarray(v) for k, v in inputs.items()}
    wf = np.concatenate([A["L1_attn_w_q"], A["L1_attn_w_k"], A["L1_attn_w_v"]], axis=0)
    common = dict(
        WqT=np.ascontiguousarray(A["L0_attn_Wq"].T), bq=A["L0_attn_bq"].reshape(1, D),
        WkT=np.ascontiguousarray(A["L0_attn_Wk"].T), bk=A["L0_attn_bk"].reshape(1, D),
        WvT=np.ascontiguousarray(A["L0_attn_Wv"].T), bv=A["L0_attn_bv"].reshape(1, D),
        WoT=np.ascontiguousarray(A["L0_attn_Wo"].T), bo=A["L0_attn_bo"].reshape(1, D),
        W1T=np.ascontiguousarray(A["L0_ffn_w1"].T), b1=A["L0_ffn_b1"].reshape(1, DFF),
        W2T=np.ascontiguousarray(A["L0_ffn_w2"].T), b2=A["L0_ffn_b2"].reshape(1, D),
        WfT=np.ascontiguousarray(wf.T), Wf_r=wf,
        Wo1T=np.ascontiguousarray(A["L1_attn_w_o"].T), Wo1_r=A["L1_attn_w_o"],
        qgain=A["L1_attn_q_gain"].reshape(1, NH),
        Wg_r=A["L1_ffn_w_gate"], Wu_r=A["L1_ffn_w_up"], Wd_r=A["L1_ffn_w_down"],
    )
    xs = A["L0_attn_x"]
    return [dict(common, x=np.ascontiguousarray(xs[i])) for i in range(8)]


def kernel(**inputs):
    nc = build()
    in_maps = _stage(inputs)
    res = run_bass_kernel_spmd(nc, in_maps, core_ids=list(range(8)))
    out = np.stack([np.asarray(r["out"]) for r in res.results], axis=0)
    return out.astype(np.float32)


# version 05 (final)
# speedup vs baseline: 1.3581x; 1.0385x over previous
import math
import numpy as np

import concourse.bass as bass
import concourse.mybir as mybir
from concourse.bass_utils import run_bass_kernel_spmd
from concourse.alu_op_type import AluOpType as OP

F32 = mybir.dt.float32
BF16 = mybir.dt.bfloat16
I32 = mybir.dt.int32
FP8E4 = mybir.dt.float8e4
AF = mybir.ActivationFunctionType
AX = mybir.AxisListType.X

S = 2048
D = 2048
DFF = 8192
NH = 16
HD = 128
NKV = 4
KVD = NKV * HD
FQ = D + 2 * KVD
NT = S // 128
ND = D // 128
NF = DFF // 128
LN2C = float(np.float32(math.log(2.0)))
RLN2 = float(np.float32(1.0 / math.log(2.0)))
LN6 = float(np.float32(math.log(6.0)))
LN64 = float(np.float32(math.log(64.0)))
L10K64 = float(-math.log(10000.0) / 64.0)
EPS_LN = 1e-5
EPS_ACT = 1e-5
EPS_W = 1e-8
EPS_I8 = 1e-5
F32EPS = float(np.finfo(np.float32).eps)
ISQHD = float(np.float32(1.0 / math.sqrt(HD)))
PI2_HI = float(np.float32(6.2831854820251465))
PI2_LO = float(np.float32(-1.7484556000744487e-07))
HALF_PI = float(np.float32(math.pi / 2))
I2PI = float(np.float32(1.0 / (2 * math.pi)))

ENGS = ("sync", "scalar", "vector", "tensor", "gpsimd")
DEV_TAIL = "none"
DEV_DBG_SERIAL = False


class Thunk:
    __slots__ = ("op", "a", "k", "inc")

    def __init__(self, op, a, k):
        self.op, self.a, self.k, self.inc = op, a, k, None

    def then_inc(self, sem, n):
        self.inc = (sem, n)
        return self


class EngQ:
    def __init__(self):
        self.L = []

    def __getattr__(self, op):
        if op.startswith("_"):
            raise AttributeError(op)

        def call(*a, **k):
            t = Thunk(op, a, k)
            self.L.append(t)
            return t
        return call

    def replay(self, e):
        drainable = {"tensor_scalar", "tensor_tensor", "scalar_tensor_tensor",
                     "tensor_copy", "tensor_reduce", "activation", "reciprocal",
                     "bn_aggr", "select", "tensor_tensor_reduce"}

        def is_col_ap(v):
            try:
                return hasattr(v, "free_size") and v.free_size() <= 8
            except Exception:
                return False

        for t in self.L:
            if t.op in drainable:
                ins = list(t.a) + [v for kk, v in t.k.items() if kk != "out"]
                if any(is_col_ap(v) for v in ins):
                    e.drain()
            r = getattr(e, t.op)(*t.a, **t.k)
            if t.inc is not None:
                r.then_inc(*t.inc)


def build():
    nc = bass.Bass("TRN2", target_bir_lowering=False)
    dt = nc.dram_tensor

    x_in = dt("x", [S, D], F32, kind="ExternalInput").ap()
    out_h = dt("out", [S, D], F32, kind="ExternalOutput").ap()

    WqT = dt("WqT", [D, D], F32, kind="ExternalInput").ap()
    WkT = dt("WkT", [D, D], F32, kind="ExternalInput").ap()
    WvT = dt("WvT", [D, D], F32, kind="ExternalInput").ap()
    WoT = dt("WoT", [D, D], F32, kind="ExternalInput").ap()
    bq_h = dt("bq", [1, D], F32, kind="ExternalInput").ap()
    bk_h = dt("bk", [1, D], F32, kind="ExternalInput").ap()
    bv_h = dt("bv", [1, D], F32, kind="ExternalInput").ap()
    bo_h = dt("bo", [1, D], F32, kind="ExternalInput").ap()
    W1T = dt("W1T", [D, DFF], F32, kind="ExternalInput").ap()
    W2T = dt("W2T", [DFF, D], F32, kind="ExternalInput").ap()
    b1_h = dt("b1", [1, DFF], F32, kind="ExternalInput").ap()
    b2_h = dt("b2", [1, D], F32, kind="ExternalInput").ap()
    WfT = dt("WfT", [D, FQ], F32, kind="ExternalInput").ap()
    Wf_r = dt("Wf_r", [FQ, D], F32, kind="ExternalInput").ap()
    Wo1T = dt("Wo1T", [D, D], F32, kind="ExternalInput").ap()
    Wo1_r = dt("Wo1_r", [D, D], F32, kind="ExternalInput").ap()
    qgain_h = dt("qgain", [1, NH], F32, kind="ExternalInput").ap()
    Wg_r = dt("Wg_r", [DFF, D], F32, kind="ExternalInput").ap()
    Wu_r = dt("Wu_r", [DFF, D], F32, kind="ExternalInput").ap()
    Wd_r = dt("Wd_r", [D, DFF], F32, kind="ExternalInput").ap()

    X1 = dt("X1", [S, D], F32, kind="Internal").ap()
    GSP = dt("GSP", [S, DFF], F32, kind="Internal").ap()
    FFN0 = dt("FFN0", [S, D], F32, kind="Internal").ap()
    ATT1 = dt("ATT1", [S, D], F32, kind="Internal").ap()
    YSP = dt("YSP", [S, D], BF16, kind="Internal").ap()
    QINT = dt("QINT", [S, DFF], BF16, kind="Internal").ap()
    QTs = dt("QTs", [D, S], BF16, kind="Internal").ap()
    KTs = dt("KTs", [D, S], BF16, kind="Internal").ap()
    VTs = dt("VTs", [D, S], BF16, kind="Internal").ap()
    W2Q = dt("W2Q", [DFF, D], BF16, kind="Internal").ap()
    WFQ = dt("WFQ", [D, FQ], BF16, kind="Internal").ap()
    WOQ = dt("WOQ", [D, D], BF16, kind="Internal").ap()
    WGQT = dt("WGQT", [D, DFF], BF16, kind="Internal").ap()
    WUQT = dt("WUQT", [D, DFF], BF16, kind="Internal").ap()
    WDQT = dt("WDQT", [DFF, D], BF16, kind="Internal").ap()

    ctxs = []

    def sbt(name, shape, dtype):
        cm = nc.sbuf_tensor(name, list(shape), dtype)
        t = cm.__enter__()
        ctxs.append(cm)
        return t

    def pstt(name, shape, dtype=F32):
        cm = nc.psum_tensor(name, list(shape), dtype)
        t = cm.__enter__()
        ctxs.append(cm)
        return t

    AR1 = sbt("AR1", [128, 32768], BF16)
    AR2 = sbt("AR2", [128, 16384], BF16)
    WR = sbt("WR", [128, 10240], BF16)
    wldA = sbt("wldA", [128, 2048], F32)
    wldB = sbt("wldB", [128, 2048], F32)
    tmpA = sbt("tmpA", [128, 2048], F32)
    tmpB = sbt("tmpB", [128, 2048], F32)
    tmpHH = sbt("tmpHH", [128, 4096], BF16)
    tmpF8 = sbt("tmpF8", [128, 2048], FP8E4)
    vext = sbt("vext", [128, NT * 132], BF16)
    colA = sbt("colA", [128, 64], F32)
    colB = sbt("colB", [128, 64], F32)
    colC = sbt("colC", [128, 64], F32)
    colD = sbt("colD", [128, 64], F32)
    colE = sbt("colE", [128, 64], F32)
    colI = sbt("colI", [128, 64], I32)
    pidx = sbt("pidx", [128, 1], F32)
    stat = sbt("stat", [128, NT * 8], F32)
    gsum = sbt("gsum", [128, NT], F32)
    gsq = sbt("gsq", [128, NT], F32)
    sqscr = sbt("sqscr", [128, 512], F32)
    gmax = sbt("gmax", [128, NT], F32)
    gmin = sbt("gmin", [128, NT], F32)
    invf = sbt("invf", [128, 64], F32)
    ropeC = sbt("ropeC", [128, NT * 64], BF16)
    ropeS = sbt("ropeS", [128, NT * 64], BF16)
    srow = sbt("srow", [1, S], BF16)
    srow2 = sbt("srow2", [1, S], BF16)
    rbufA = sbt("rbufA", [1, 512], BF16)
    rbufB = sbt("rbufB", [1, 512], BF16)
    rbufR = sbt("rbufR", [1, 128], F32)
    onesM = sbt("onesM", [1, 128], F32)
    onesMb = sbt("onesMb", [1, 128], BF16)
    ones_r = sbt("ones_r", [1, 512], BF16)
    onesc_f = sbt("onesc_f", [128, 1], F32)
    eyeb = sbt("eyeb", [128, 128], BF16)
    eyef = sbt("eyef", [128, 128], F32)
    bdmask = sbt("bdmask", [128, 128], BF16)
    bdmask4 = sbt("bdmask4", [128, 512], BF16)
    dgmask = sbt("dgmask", [128, 4 * 512], BF16)
    gainb = sbt("gainb", [128, NH], F32)
    scal = sbt("scal", [128, 16], F32)
    xtile = sbt("xtile", [128, 2048], F32)
    wsA = sbt("wsA", [128, FQ], BF16)
    wsB = sbt("wsB", [128, D], BF16)

    psum = [pstt(f"ps{i}", [128, 512], F32) for i in range(6)]
    psT = pstt("psT", [128, 640], BF16)
    psM = pstt("psM", [128, 512], F32)

    sems = {}
    for name in ["qS", "qA", "qG", "pe", "vec", "sc", "gp", "ph", "evd"]:
        sems[name] = nc.semaphore(name).__enter__()

    cnt = {n: 0 for n in sems}
    Q = {e: EngQ() for e in ENGS}
    qS, qA, qV, qT, qG = (Q[e] for e in ENGS)

    def inc(name, v=1):
        cnt[name] += v
        return cnt[name]

    def dma(qeng, out, in_, transpose=False):
        sem_name = {"sync": "qS", "scalar": "qA", "gpsimd": "qG"}[qeng]
        Q[qeng].dma_start(out=out, in_=in_, transpose=transpose).then_inc(
            sems[sem_name], 16)
        return inc(sem_name, 16)

    def wait(eng, sem_name, val=None):
        Q[eng].wait_ge(sems[sem_name], cnt[sem_name] if val is None else val)

    def sig(eng, th, sem_name="vec" if True else None, n=1):
        th.then_inc(sems[sem_name], n)
        return inc(sem_name, n)

    bar_n = [0]

    def barrier():
        bar_n[0] += 1
        tgt = 5 * bar_n[0]
        for e in ENGS:
            if e == "sync":
                wait(e, "qS")
            elif e == "scalar":
                wait(e, "qA")
                Q[e].drain()
            elif e == "gpsimd":
                wait(e, "qG")
                Q[e].drain()
            else:
                Q[e].drain()
            Q[e].sem_inc(sems["ph"], 1)
            Q[e].wait_ge(sems["ph"], tgt)

    bankfree = [("ph", 0)] * 8

    def bank_wait(i):
        sname, v = bankfree[i]
        Q["tensor"].wait_ge(sems[sname], v)

    def bank_mark(i, sem_name):
        bankfree[i] = (sem_name, cnt[sem_name])

    def ar1(dd, t0, w):
        return AR1[:, dd * S + t0: dd * S + t0 + w]

    IH = tmpHH[:].bitcast(I32)

    qV.memset(onesM[:], 1.0)
    qV.memset(onesMb[:], 1.0)
    qV.memset(ones_r[:], 1.0)
    qV.memset(onesc_f[:], 1.0)
    qV.memset(gmax[:], -3.0e38)
    qV.memset(gmin[:], 3.0e38)
    qV.memset(gsum[:], 0.0)
    qV.memset(gsq[:], 0.0)
    qV.memset(scal[:, 10:11], LN2C)
    qV.memset(scal[:, 11:12], -LN2C)
    qV.memset(scal[:, 12:13], LN6)
    qV.memset(scal[:, 13:14], LN64)
    qV.memset(scal[:, 14:15], -LN64)
    for _vc in range(NT):
        qV.memset(vext[:, _vc * 132 + 128:_vc * 132 + 129], 1.0)
    sig("vector", qV.drain(), "vec")

    qG.iota(out=pidx[:], pattern=[[0, 1]], base=0, channel_multiplier=1,
            allow_small_or_imprecise_dtypes=True)
    qG.iota(out=tmpB[:, 0:128], pattern=[[1, 128]], base=0, channel_multiplier=0,
            allow_small_or_imprecise_dtypes=True)
    qG.iota(out=tmpB[:, 512:1024], pattern=[[1, 512]], base=0, channel_multiplier=-1,
            allow_small_or_imprecise_dtypes=True)
    qG.iota(out=colC[0:1, 0:64], pattern=[[1, 64]], base=0, channel_multiplier=0,
            allow_small_or_imprecise_dtypes=True)
    qG.memset(eyeb[:], 1.0)
    qG.memset(eyef[:], 1.0)
    qG.drain()
    qG.affine_select(out=eyeb[:], in_=eyeb[:], pattern=[[1, 128]],
                     compare_op=OP.is_equal, fill=0.0, base=0, channel_multiplier=-1)
    qG.affine_select(out=eyef[:], in_=eyef[:], pattern=[[1, 128]],
                     compare_op=OP.is_equal, fill=0.0, base=0, channel_multiplier=-1)
    sig("gpsimd", qG.drain(), "gp")

    wait("vector", "gp")
    qV.tensor_scalar(out=colB[:, 0:1], in0=pidx[:], scalar1=1.0 / 16.0,
                     scalar2=-0.499, op0=OP.mult, op1=OP.add)
    qV.tensor_copy(out=colI[:, 0:1], in_=colB[:, 0:1])
    qV.tensor_copy(out=colB[:, 1:2], in_=colI[:, 0:1])
    qV.tensor_scalar(out=tmpA[:, 0:128], in0=tmpB[:, 0:128], scalar1=1.0 / 16.0,
                     scalar2=-0.499, op0=OP.mult, op1=OP.add)
    qV.tensor_copy(out=IH[:, 0:128], in_=tmpA[:, 0:128])
    qV.tensor_copy(out=tmpA[:, 128:256], in_=IH[:, 0:128])
    qV.tensor_tensor(out=bdmask[:], in0=tmpA[:, 128:256],
                     in1=colB[:, 1:2].broadcast_to((128, 128)), op=OP.is_equal)
    for a in range(4):
        qV.tensor_scalar(out=dgmask[:, a * 512:(a + 1) * 512], in0=tmpB[:, 512:1024],
                         scalar1=float(128 * a), scalar2=None, op0=OP.is_ge)
    sig("vector", qV.drain(), "vec")

    wait("tensor", "vec")
    bank_wait(7)
    qT.matmul(out=psM[:, 0:64], lhsT=onesM[:], rhs=colC[0:1, 0:64], start=True, stop=True)
    sig("tensor", qT.drain(), "pe")
    wait("scalar", "pe")
    qA.activation(out=invf[:], in_=psM[:, 0:64], func=AF.Exp, scale=L10K64)
    sc_invf = sig("scalar", qA.drain(), "sc")
    bank_mark(7, "sc")
    dma("sync", rbufR[0:1, 0:16], qgain_h[:, 0:NH])
    wait("tensor", "qS")
    bank_wait(7)
    qT.matmul(out=psM[:, 64:64 + NH], lhsT=onesM[:], rhs=rbufR[0:1, 0:16], start=True, stop=True)
    sig("tensor", qT.drain(), "pe")
    wait("vector", "pe")
    qV.tensor_copy(out=gainb[:], in_=psM[:, 64:64 + NH])
    v_g = sig("vector", qV.drain(), "vec")
    bank_mark(7, "vec")

    wait("vector", "sc", sc_invf)
    for tt in range(NT):
        qV.tensor_scalar(out=colD[:, 0:1], in0=pidx[:], scalar1=float(tt * 128),
                         scalar2=None, op0=OP.add)
        qV.drain()
        qV.tensor_scalar(out=colA[:, 0:64], in0=invf[:], scalar1=colD[:, 0:1],
                         scalar2=None, op0=OP.mult)
        qV.tensor_scalar(out=colE[:, 0:64], in0=colA[:, 0:64], scalar1=I2PI,
                         scalar2=None, op0=OP.mult)
        qV.tensor_copy(out=colI[:, 0:64], in_=colE[:, 0:64])
        qV.scalar_tensor_tensor(out=colE[:, 0:64], in0=colI[:, 0:64], scalar=-PI2_HI,
                                in1=colA[:, 0:64], op0=OP.mult, op1=OP.add)
        qV.scalar_tensor_tensor(out=colD[:, 0:64], in0=colI[:, 0:64], scalar=-PI2_LO,
                                in1=colE[:, 0:64], op0=OP.mult, op1=OP.add)
        qV.tensor_scalar(out=colB[:, 0:64], in0=colA[:, 0:64], scalar1=HALF_PI,
                         scalar2=None, op0=OP.add)
        qV.tensor_scalar(out=colE[:, 0:64], in0=colB[:, 0:64], scalar1=I2PI,
                         scalar2=None, op0=OP.mult)
        qV.tensor_copy(out=colI[:, 0:64], in_=colE[:, 0:64])
        qV.scalar_tensor_tensor(out=colE[:, 0:64], in0=colI[:, 0:64], scalar=-PI2_HI,
                                in1=colB[:, 0:64], op0=OP.mult, op1=OP.add)
        qV.scalar_tensor_tensor(out=colB[:, 0:64], in0=colI[:, 0:64], scalar=-PI2_LO,
                                in1=colE[:, 0:64], op0=OP.mult, op1=OP.add)
        v_r = sig("vector", qV.drain(), "vec")
        wait("scalar", "vec", v_r)
        qA.activation(out=ropeS[:, tt * 64:(tt + 1) * 64], in_=colD[:, 0:64], func=AF.Sin)
        qA.activation(out=ropeC[:, tt * 64:(tt + 1) * 64], in_=colB[:, 0:64], func=AF.Sin)
        s_r = sig("scalar", qA.drain(), "sc")
        wait("vector", "sc", s_r)

    qV.memset(colA[:, 2:4], 0.0)
    qV.drain()

    def stream_stat(src, n_rt, n_ch, acc_col):
        toks = []
        wait("sync", "vec")
        for i in range(n_rt * n_ch):
            rr, cc = divmod(i, n_ch)
            buf = (wldA, wldB)[i % 2]
            if len(toks) >= 2:
                wait("sync", "vec", toks[-2])
            tok = dma("sync", buf[:], src[rr * 128:(rr + 1) * 128, cc * 2048:(cc + 1) * 2048])
            wait("vector", "qS", tok)
            qV.tensor_reduce(out=colB[:, 8:9], in_=buf[:], axis=AX, op=OP.add,
                             apply_absolute_value=True)
            qV.tensor_tensor(out=acc_col, in0=acc_col, in1=colB[:, 8:9], op=OP.add)
            toks.append(sig("vector", qV.drain(), "vec"))

    stream_stat(W1T, ND, 4, colA[:, 2:3])
    stream_stat(W2T, NF, 1, colA[:, 3:4])
    v_now = cnt["vec"]
    wait("tensor", "vec", v_now)
    bank_wait(7)
    qT.matmul(out=psM[0:1, 128:129], lhsT=colA[:, 2:3], rhs=onesc_f[:], start=True, stop=True)
    qT.matmul(out=psM[0:1, 129:130], lhsT=colA[:, 3:4], rhs=onesc_f[:], start=True, stop=True)
    sig("tensor", qT.drain(), "pe")
    wait("vector", "pe")
    qV.tensor_scalar(out=scal[0:1, 8:9], in0=psM[0:1, 128:129],
                     scalar1=1.0 / (D * DFF), scalar2=EPS_W, op0=OP.mult, op1=OP.max)
    qV.tensor_scalar(out=scal[0:1, 9:10], in0=psM[0:1, 129:130],
                     scalar1=1.0 / (D * DFF), scalar2=EPS_W, op0=OP.mult, op1=OP.max)
    v_now = sig("vector", qV.drain(), "vec")
    wait("tensor", "vec", v_now)
    qT.matmul(out=psM[:, 130:131], lhsT=onesM[:], rhs=scal[0:1, 8:9], start=True, stop=True)
    qT.matmul(out=psM[:, 131:132], lhsT=onesM[:], rhs=scal[0:1, 9:10], start=True, stop=True)
    sig("tensor", qT.drain(), "pe")
    wait("vector", "pe")
    qV.tensor_copy(out=scal[:, 0:1], in_=psM[:, 130:131])
    qV.tensor_copy(out=scal[:, 2:3], in_=psM[:, 131:132])
    qV.drain()
    qV.reciprocal(out=scal[:, 1:2], in_=scal[:, 0:1])
    qV.reciprocal(out=scal[:, 3:4], in_=scal[:, 2:3])
    sig("vector", qV.drain(), "vec")
    bank_mark(7, "vec")

    g_toks = []
    for rt in range(NF):
        buf = (wldA, wldB)[rt % 2]
        if rt >= 2:
            wait("sync", "vec", v_w2q[rt - 2])
        tok = dma("sync", buf[:], W2T[rt * 128:(rt + 1) * 128, :])
        wait("vector", "qS", tok)
        if rt >= 2:
            wait("vector", "qG", g_toks[rt - 2])
        ob = WR[:, (rt % 2) * 2048:(rt % 2) * 2048 + 2048]
        qV.tensor_scalar(out=tmpA[:], in0=buf[:], scalar1=scal[:, 3:4], scalar2=None,
                         op0=OP.mult)
        qV.tensor_copy(out=IH[:], in_=tmpA[:])
        qV.tensor_scalar(out=ob, in0=IH[:], scalar1=1.0, scalar2=-1.0,
                         op0=OP.min, op1=OP.max)
        vtok = sig("vector", qV.drain(), "vec")
        if rt == 0:
            v_w2q = []
        v_w2q.append(vtok)
        wait("gpsimd", "vec", vtok)
        g_toks.append(dma("gpsimd", W2Q[rt * 128:(rt + 1) * 128, :], ob))

    barrier()

    for tt in range(NT):
        tok = dma("sync", xtile[:], x_in[tt * 128:(tt + 1) * 128, :])
        wait("vector", "qS", tok)
        for c in range(4):
            qV.bn_stats(out=tmpB[:, c * 6:c * 6 + 6], in_=xtile[:, c * 512:(c + 1) * 512])
        qV.drain()
        qV.bn_aggr(out=colA[:, 0:2], in_=tmpB[:, 0:24])
        qV.drain()
        qV.tensor_scalar(out=colB[:, 0:1], in0=colA[:, 1:2], scalar1=EPS_LN,
                         scalar2=None, op0=OP.add)
        qV.drain()
        qV.reciprocal(out=colB[:, 1:2], in_=colB[:, 0:1])
        v1 = sig("vector", qV.drain(), "vec")
        wait("scalar", "vec", v1)
        qA.activation(out=colB[:, 2:3], in_=colB[:, 1:2], func=AF.Sqrt)
        s1 = sig("scalar", qA.drain(), "sc")
        wait("vector", "sc", s1)
        qV.tensor_scalar(out=tmpA[:], in0=xtile[:], scalar1=colA[:, 0:1],
                         scalar2=colB[:, 2:3], op0=OP.subtract, op1=OP.mult)
        for c in range(4):
            qV.bn_stats(out=tmpB[:, c * 6:c * 6 + 6], in_=tmpA[:, c * 512:(c + 1) * 512])
        qV.tensor_reduce(out=colD[:, 0:1], in_=tmpA[:], axis=AX, op=OP.max)
        qV.tensor_reduce(out=colD[:, 1:2], in_=tmpA[:], axis=AX, op=OP.min)
        qV.drain()
        qV.bn_aggr(out=colC[:, 0:2], in_=tmpB[:, 0:24])
        qV.drain()
        qV.tensor_scalar(out=colB[:, 0:1], in0=colC[:, 1:2], scalar1=EPS_LN,
                         scalar2=None, op0=OP.add)
        qV.drain()
        qV.reciprocal(out=colB[:, 1:2], in_=colB[:, 0:1])
        v2 = sig("vector", qV.drain(), "vec")
        wait("scalar", "vec", v2)
        qA.activation(out=colC[:, 2:3], in_=colB[:, 1:2], func=AF.Sqrt)
        s2 = sig("scalar", qA.drain(), "sc")
        wait("vector", "sc", s2)
        qV.tensor_tensor(out=colE[:, 0:1], in0=colD[:, 0:1], in1=colC[:, 0:1], op=OP.subtract)
        qV.tensor_tensor(out=colE[:, 1:2], in0=colC[:, 0:1], in1=colD[:, 1:2], op=OP.subtract)
        qV.tensor_tensor(out=colE[:, 2:3], in0=colE[:, 0:1], in1=colE[:, 1:2], op=OP.max)
        qV.tensor_tensor(out=colE[:, 3:4], in0=colE[:, 2:3], in1=colC[:, 2:3], op=OP.mult)
        qV.tensor_scalar(out=colE[:, 4:5], in0=colE[:, 3:4], scalar1=EPS_ACT,
                         scalar2=None, op0=OP.max)
        qV.drain()
        qV.reciprocal(out=colE[:, 5:6], in_=colE[:, 4:5])
        qV.tensor_tensor(out=colD[:, 2:3], in0=colE[:, 4:5], in1=scal[:, 0:1], op=OP.mult)
        qV.tensor_scalar(out=stat[:, tt * 8:tt * 8 + 1], in0=colD[:, 2:3],
                         scalar1=1.0 / 127.0, scalar2=None, op0=OP.mult)
        qV.tensor_tensor(out=colD[:, 3:4], in0=colE[:, 5:6], in1=scal[:, 1:2], op=OP.mult)
        qV.tensor_scalar(out=colD[:, 4:5], in0=colD[:, 3:4], scalar1=127.0,
                         scalar2=None, op0=OP.mult)
        qV.tensor_tensor(out=colE[:, 6:7], in0=colE[:, 5:6], in1=colC[:, 2:3], op=OP.mult)
        qV.tensor_scalar(out=colE[:, 7:8], in0=colE[:, 6:7], scalar1=127.0,
                         scalar2=None, op0=OP.mult)
        qV.drain()
        qV.tensor_scalar(out=tmpB[:], in0=tmpA[:], scalar1=colC[:, 0:1],
                         scalar2=colE[:, 7:8], op0=OP.subtract, op1=OP.mult)
        qV.tensor_copy(out=IH[:], in_=tmpB[:])
        qV.tensor_scalar(out=WR[:, 0:2048], in0=IH[:], scalar1=127.0, scalar2=-128.0,
                         op0=OP.min, op1=OP.max)
        v3 = sig("vector", qV.drain(), "vec")
        wait("tensor", "vec", v3)
        bank_wait(7)
        qT.transpose(out=psM[0:1, 0:128], in_=colD[:, 4:5], identity=eyef[:])
        p1 = sig("tensor", qT.drain(), "pe")
        wait("vector", "pe", p1)
        qV.tensor_copy(out=srow[0:1, tt * 128:(tt + 1) * 128], in_=psM[0:1, 0:128])
        sig("vector", qV.drain(), "vec")
        bank_mark(7, "vec")
        wait("sync", "vec")
        for c in range(ND):
            dma("sync", AR1[:, c * S + tt * 128: c * S + (tt + 1) * 128],
                WR[:, c * 128:(c + 1) * 128], transpose=True)

    barrier()

    NFC = DFF // 512
    a2rhs = [WR[:, 0:8192], AR2[:, 0:8192]]
    a2bias = [rbufA, rbufB]
    a2slot = [tmpB[:, i * 512:(i + 1) * 512] for i in range(4)] + \
             [xtile[:, i * 512:(i + 1) * 512] for i in range(4)]
    wld_tok = [0, 0]
    a2_pe_end = {}
    a2_btok = {}
    a2_rhs_done = {}
    a2_store_tok = [0] * 8

    def a2_emit_quant(fc, ddc):
        rb = a2rhs[fc % 2]
        buf = (wldA, wldB)[ddc % 2]
        if ddc == 0:
            if fc >= 2:
                wait("gpsimd", "pe", a2_pe_end[fc - 2])
            a2_btok[fc] = dma("gpsimd", a2bias[fc % 2][:],
                              b1_h[:, fc * 512:(fc + 1) * 512])
        wait("sync", "vec", wld_tok[ddc % 2])
        tok = dma("sync", buf[:, 0:512],
                  W1T[ddc * 128:(ddc + 1) * 128, fc * 512:(fc + 1) * 512])
        wait("vector", "qS", tok)
        if ddc == 0 and fc >= 2:
            wait("vector", "pe", a2_pe_end[fc - 2])
        qV.tensor_scalar(out=tmpA[:, 0:512], in0=buf[:, 0:512], scalar1=scal[:, 1:2],
                         scalar2=None, op0=OP.mult)
        qV.tensor_copy(out=IH[:, 0:512], in_=tmpA[:, 0:512])
        qV.tensor_scalar(out=rb[:, ddc * 512:(ddc + 1) * 512], in0=IH[:, 0:512],
                         scalar1=1.0, scalar2=-1.0, op0=OP.min, op1=OP.max)
        wld_tok[ddc % 2] = sig("vector", qV.drain(), "vec")
        if ddc == ND - 1:
            a2_rhs_done[fc] = wld_tok[ddc % 2]

    for ddc in range(ND):
        a2_emit_quant(0, ddc)
    for fc in range(NFC):
        rb = a2rhs[fc % 2]
        wait("tensor", "vec", a2_rhs_done[fc])
        wait("tensor", "qG", a2_btok[fc])
        for tt in range(NT):
            if fc + 1 < NFC:
                a2_emit_quant(fc + 1, tt)
            b = tt % 6
            bank_wait(b)
            for ddc in range(ND):
                qT.matmul(out=psum[b][:], lhsT=AR1[:, ddc * S + tt * 128: ddc * S + (tt + 1) * 128],
                          rhs=rb[:, ddc * 512:(ddc + 1) * 512],
                          start=(ddc == 0), stop=False)
            mm = qT.matmul(out=psum[b][:], lhsT=srow[0:1, tt * 128:(tt + 1) * 128],
                           rhs=a2bias[fc % 2][:], start=False, stop=True)
            ptok = sig("tensor", mm, "pe")
            if tt == NT - 1:
                a2_pe_end[fc] = ptok
            wait("scalar", "pe", ptok)
            wait("scalar", "qG", a2_store_tok[tt % 8])
            gslot = a2slot[tt % 8]
            qA.activation(out=gslot, in_=psum[b][:], func=AF.Gelu,
                          scale=stat[:, tt * 8:tt * 8 + 1])
            stok = sig("scalar", qA.drain(), "sc")
            bank_mark(b, "sc")
            wait("vector", "sc", stok)
            qV.tensor_reduce(out=colA[:, 8:9], in_=gslot, axis=AX, op=OP.max)
            qV.tensor_reduce(out=colA[:, 9:10], in_=gslot, axis=AX, op=OP.min)
            qV.tensor_scalar(out=sqscr[:], in0=gslot, scalar1=1.0, scalar2=0.0,
                             op0=OP.mult, op1=OP.add, accum_out=colA[:, 10:11])
            qV.scalar_tensor_tensor(out=sqscr[:], in0=gslot, scalar=1.0, in1=gslot,
                                    op0=OP.mult, op1=OP.mult, accum_out=colA[:, 11:12])
            qV.drain()
            qV.tensor_tensor(out=gmax[:, tt:tt + 1], in0=gmax[:, tt:tt + 1],
                             in1=colA[:, 8:9], op=OP.max)
            qV.tensor_tensor(out=gmin[:, tt:tt + 1], in0=gmin[:, tt:tt + 1],
                             in1=colA[:, 9:10], op=OP.min)
            qV.tensor_tensor(out=gsum[:, tt:tt + 1], in0=gsum[:, tt:tt + 1],
                             in1=colA[:, 10:11], op=OP.add)
            qV.tensor_tensor(out=gsq[:, tt:tt + 1], in0=gsq[:, tt:tt + 1],
                             in1=colA[:, 11:12], op=OP.add)
            vtok = sig("vector", qV.drain(), "vec")
            wait("gpsimd", "vec", vtok)
            a2_store_tok[tt % 8] = dma(
                "gpsimd", GSP[tt * 128:(tt + 1) * 128, fc * 512:(fc + 1) * 512], gslot)

    barrier()

    for tt in range(NT):
        qV.tensor_scalar(out=colA[:, 0:1], in0=gsum[:, tt:tt + 1], scalar1=1.0 / DFF,
                         scalar2=None, op0=OP.mult)
        qV.tensor_scalar(out=colA[:, 2:3], in0=gsq[:, tt:tt + 1], scalar1=1.0 / DFF,
                         scalar2=None, op0=OP.mult)
        qV.drain()
        qV.tensor_tensor(out=colA[:, 3:4], in0=colA[:, 0:1], in1=colA[:, 0:1], op=OP.mult)
        qV.tensor_tensor(out=colA[:, 1:2], in0=colA[:, 2:3], in1=colA[:, 3:4],
                         op=OP.subtract)
        qV.tensor_scalar(out=colB[:, 0:1], in0=colA[:, 1:2], scalar1=EPS_LN,
                         scalar2=None, op0=OP.add)
        qV.drain()
        qV.reciprocal(out=colB[:, 1:2], in_=colB[:, 0:1])
        v1 = sig("vector", qV.drain(), "vec")
        wait("scalar", "vec", v1)
        qA.activation(out=colB[:, 2:3], in_=colB[:, 1:2], func=AF.Sqrt)
        s1 = sig("scalar", qA.drain(), "sc")
        wait("vector", "sc", s1)
        qV.tensor_tensor(out=colE[:, 0:1], in0=gmax[:, tt:tt + 1], in1=colA[:, 0:1],
                         op=OP.subtract)
        qV.tensor_tensor(out=colE[:, 1:2], in0=colA[:, 0:1], in1=gmin[:, tt:tt + 1],
                         op=OP.subtract)
        qV.tensor_tensor(out=colE[:, 2:3], in0=colE[:, 0:1], in1=colE[:, 1:2], op=OP.max)
        qV.tensor_tensor(out=colE[:, 3:4], in0=colE[:, 2:3], in1=colB[:, 2:3], op=OP.mult)
        qV.tensor_scalar(out=colE[:, 4:5], in0=colE[:, 3:4], scalar1=EPS_ACT,
                         scalar2=None, op0=OP.max)
        qV.drain()
        qV.reciprocal(out=colE[:, 5:6], in_=colE[:, 4:5])
        qV.drain()
        qV.tensor_scalar(out=colE[:, 6:7], in0=colE[:, 5:6], scalar1=127.0,
                         scalar2=None, op0=OP.mult)
        qV.tensor_tensor(out=stat[:, tt * 8 + 1:tt * 8 + 2], in0=colE[:, 6:7],
                         in1=colB[:, 2:3], op=OP.mult)
        qV.tensor_copy(out=stat[:, tt * 8 + 2:tt * 8 + 3], in_=colA[:, 0:1])
        qV.tensor_tensor(out=colD[:, 0:1], in0=colE[:, 4:5], in1=scal[:, 2:3], op=OP.mult)
        qV.tensor_scalar(out=stat[:, tt * 8 + 3:tt * 8 + 4], in0=colD[:, 0:1],
                         scalar1=1.0 / 127.0, scalar2=None, op0=OP.mult)
        qV.tensor_tensor(out=colD[:, 1:2], in0=colE[:, 6:7], in1=scal[:, 3:4], op=OP.mult)
        v2 = sig("vector", qV.drain(), "vec")
        wait("tensor", "vec", v2)
        bank_wait(7)
        qT.transpose(out=psM[0:1, 0:128], in_=colD[:, 1:2], identity=eyef[:])
        p1 = sig("tensor", qT.drain(), "pe")
        wait("vector", "pe", p1)
        qV.tensor_copy(out=srow2[0:1, tt * 128:(tt + 1) * 128], in_=psM[0:1, 0:128])
        sig("vector", qV.drain(), "vec")
        bank_mark(7, "vec")

    barrier()

    WSTG = WR[:, 8192:10240]
    prev_vq = [0, 0]
    for grp in range(8):
        pe_guard = cnt["pe"]
        vq_toks = list(prev_vq)
        for ti in range(2):
            tt = grp * 2 + ti
            for c in range(4):
                i_lin = ti * 4 + c
                buf = (wldA, wldB)[i_lin % 2]
                wait("sync", "vec", vq_toks[i_lin])
                tok = dma("sync", buf[:], GSP[tt * 128:(tt + 1) * 128, c * 2048:(c + 1) * 2048])
                wait("vector", "qS", tok)
                if ti == 0 and c == 0:
                    wait("vector", "pe", pe_guard)
                wait("vector", "qS")
                qV.tensor_scalar(out=tmpA[:], in0=buf[:], scalar1=stat[:, tt * 8 + 2:tt * 8 + 3],
                                 scalar2=stat[:, tt * 8 + 1:tt * 8 + 2], op0=OP.subtract, op1=OP.mult)
                qV.tensor_copy(out=IH[:], in_=tmpA[:])
                qV.tensor_scalar(out=WSTG[:], in0=IH[:], scalar1=127.0, scalar2=-128.0,
                                 op0=OP.min, op1=OP.max)
                vtok = sig("vector", qV.drain(), "vec")
                vq_toks.append(vtok)
                if i_lin == 6:
                    prev_vq[0] = vtok
                if i_lin == 7:
                    prev_vq[1] = vtok
                wait("sync", "vec", vtok)
                for j in range(16):
                    ddg = c * 16 + j
                    dma("sync", AR2[:, ddg * 256 + ti * 128: ddg * 256 + (ti + 1) * 128],
                        WSTG[:, j * 128:(j + 1) * 128], transpose=True)
        t_group_ready = cnt["qS"]
        if DEV_TAIL == "ar2g" and grp == 5:
            wait("gpsimd", "qS", t_group_ready)
            for rr in range(16):
                dma("gpsimd", out_h[rr * 128:(rr + 1) * 128, 0:1024],
                    AR2[:, rr * 1024:(rr + 1) * 1024])
            wait("gpsimd", "qG")
            wait("sync", "qG")
            wait("vector", "qG")
            wait("tensor", "qG")
        for foc in range(4):
            wait("gpsimd", "pe")
            b2tok = dma("gpsimd", rbufA[:], b2_h[:, foc * 512:(foc + 1) * 512])
            wait("tensor", "qS", t_group_ready)
            wait("tensor", "qG", b2tok)
            rtoks = []
            for ddg in range(NF):
                slot = WR[:, (ddg % 2) * 512:(ddg % 2) * 512 + 512]
                wait("sync", "pe")
                rtoks.append(dma("sync", slot, W2Q[ddg * 128:(ddg + 1) * 128,
                                                   foc * 512:(foc + 1) * 512]))
                wait("tensor", "qS", rtoks[-1])
                for ti in range(2):
                    if ddg == 0:
                        bank_wait(ti)
                    mmr = qT.matmul(out=psum[ti][:], lhsT=AR2[:, ddg * 256 + ti * 128: ddg * 256 + (ti + 1) * 128],
                                    rhs=slot, start=(ddg == 0), stop=False)
                    if ti == 1:
                        sig("tensor", mmr, "pe")
            for ti in range(2):
                tt = grp * 2 + ti
                mm = qT.matmul(out=psum[ti][:], lhsT=srow2[0:1, tt * 128:(tt + 1) * 128],
                               rhs=rbufA[:], start=False, stop=True)
                ptok = sig("tensor", mm, "pe")
                wait("vector", "pe", ptok)
                oslot = tmpB[:, (ti % 4) * 512:(ti % 4) * 512 + 512]
                qV.tensor_scalar(out=oslot, in0=psum[ti][:], scalar1=stat[:, tt * 8 + 3:tt * 8 + 4],
                                 scalar2=None, op0=OP.mult)
                vtok = sig("vector", qV.drain(), "vec")
                bank_mark(ti, "vec")
                wait("gpsimd", "vec", vtok)
                dma("gpsimd", FFN0[tt * 128:(tt + 1) * 128, foc * 512:(foc + 1) * 512], oslot)
            wait("vector", "qG")

    barrier()

    for tt in range(NT):
        tok = dma("sync", xtile[:], x_in[tt * 128:(tt + 1) * 128, :])
        wait("vector", "qS", tok)
        if tt > 0:
            wait("vector", "qS")
        for c in range(4):
            qV.bn_stats(out=tmpB[:, c * 6:c * 6 + 6], in_=xtile[:, c * 512:(c + 1) * 512])
        qV.drain()
        qV.bn_aggr(out=colA[:, 0:2], in_=tmpB[:, 0:24])
        qV.tensor_scalar(out=colB[:, 0:1], in0=colA[:, 1:2], scalar1=EPS_LN,
                         scalar2=None, op0=OP.add)
        qV.reciprocal(out=colB[:, 1:2], in_=colB[:, 0:1])
        v1 = sig("vector", qV.drain(), "vec")
        wait("scalar", "vec", v1)
        qA.activation(out=colB[:, 2:3], in_=colB[:, 1:2], func=AF.Sqrt)
        s1 = sig("scalar", qA.drain(), "sc")
        wait("vector", "sc", s1)
        qV.tensor_scalar(out=WSTG[:], in0=xtile[:], scalar1=colA[:, 0:1],
                         scalar2=colB[:, 2:3], op0=OP.subtract, op1=OP.mult)
        vtok = sig("vector", qV.drain(), "vec")
        wait("sync", "vec", vtok)
        for c in range(ND):
            dma("sync", AR1[:, c * S + tt * 128: c * S + (tt + 1) * 128],
                WSTG[:, c * 128:(c + 1) * 128], transpose=True)

    barrier()

    for (WT_h, b_h, dst) in ((WqT, bq_h, QTs), (WkT, bk_h, KTs), (WvT, bv_h, VTs)):
        for fc in range(4):
            btok = dma("gpsimd", rbufA[:], b_h[:, fc * 512:(fc + 1) * 512])
            for tc in range(4):
                for dd in range(ND):
                    slot = WR[:, (dd % 2) * 512:(dd % 2) * 512 + 512]
                    wait("gpsimd", "pe")
                    wtok = dma("gpsimd", slot,
                               WT_h[dd * 128:(dd + 1) * 128, fc * 512:(fc + 1) * 512])
                    wait("tensor", "qG", wtok)
                    for fi in range(4):
                        if dd == 0:
                            bank_wait(fi)
                        mmr = qT.matmul(out=psum[fi][:],
                                        lhsT=slot[:, fi * 128:(fi + 1) * 128],
                                        rhs=AR1[:, dd * S + tc * 512: dd * S + (tc + 1) * 512],
                                        start=(dd == 0), stop=False)
                        if fi == 3:
                            sig("tensor", mmr, "pe")
                wait("tensor", "qG", btok)
                for fi in range(4):
                    mm = qT.matmul(out=psum[fi][:], lhsT=rbufA[0:1, fi * 128:(fi + 1) * 128],
                                   rhs=ones_r[:], start=False, stop=True)
                    ptok = sig("tensor", mm, "pe")
                    wait("vector", "pe", ptok)
                    if not (fc == 0 and tc == 0 and fi == 0):
                        wait("vector", "qG")
                    oslot = tmpHH[:, (fi % 2) * 512:(fi % 2) * 512 + 512]
                    qV.tensor_copy(out=oslot, in_=psum[fi][:])
                    vtok = sig("vector", qV.drain(), "vec")
                    bank_mark(fi, "vec")
                    wait("gpsimd", "vec", vtok)
                    dma("gpsimd", dst[(fc * 4 + fi) * 128:(fc * 4 + fi + 1) * 128,
                                      tc * 512:(tc + 1) * 512], oslot)

    barrier()

    yeR = AR2[:, 12288:16384].rearrange("p (h t) -> p t h", h=16)
    qR = AR2[:, 0:4096].rearrange("p (h t) -> p t h", h=16)
    kR = AR2[:, 4096:8192].rearrange("p (h t) -> p t h", h=16)
    vR = AR2[:, 8192:12288].rearrange("p (h t) -> p t h", h=16)
    wo_tok = [0, 0]
    ost_tok = [0] * 4
    for e8 in range(8):
        pe_guard = cnt["pe"]
        v_guard = cnt["vec"]
        wait("sync", "pe", pe_guard)
        wait("sync", "vec", v_guard)
        for ft in range(16):
            dma("sync", AR2[:, 0 + ft * 256:0 + (ft + 1) * 256],
                QTs[ft * 128:(ft + 1) * 128, e8 * 256:(e8 + 1) * 256])
            dma("sync", AR2[:, 4096 + ft * 256:4096 + (ft + 1) * 256],
                KTs[ft * 128:(ft + 1) * 128, e8 * 256:(e8 + 1) * 256])
            dma("sync", AR2[:, 8192 + ft * 256:8192 + (ft + 1) * 256],
                VTs[ft * 128:(ft + 1) * 128, e8 * 256:(e8 + 1) * 256])
        t_loaded = cnt["qS"]
        wait("tensor", "qS", t_loaded)
        wait("vector", "qS", t_loaded)
        for g in range(32):
            t0 = g * 8
            bs = g % 2
            bc = 2 + (g % 2)
            qAP = qR[:, t0:t0 + 8, :]
            kAP = kR[:, t0:t0 + 8, :]
            vAP = vR[:, t0:t0 + 8, :]
            sl = (g % 2) * 1024
            E_s = tmpHH[:, sl + 0:sl + 128]
            A_s = tmpHH[:, sl + 128:sl + 256]
            AT_s = tmpHH[:, sl + 256:sl + 384]
            VG_s = tmpHH[:, sl + 384:sl + 512]
            CG_s = tmpHH[:, sl + 512:sl + 640]
            KG_s = tmpHH[:, sl + 640:sl + 768]
            VGT_s = tmpHH[:, sl + 768:sl + 896]
            QG_s = tmpHH[:, sl + 896:sl + 1024]
            qV.tensor_copy(out=KG_s, in_=kAP)
            qV.tensor_reduce(out=colB[:, 0:8], in_=vAP, axis=AX, op=OP.add)
            qV.drain()
            qV.tensor_copy(out=VGT_s.rearrange("p (t h) -> p t h", h=16),
                           in_=colB[:, 0:8].unsqueeze(2).broadcast_to((128, 8, 16)))
            qV.tensor_copy(out=QG_s, in_=qAP)
            vk = sig("vector", qV.drain(), "vec")
            wait("tensor", "vec", vk)
            bank_wait(bs)
            mm = qT.matmul(out=psum[bs][:, 0:128], lhsT=QG_s, rhs=KG_s, start=True, stop=True)
            p1 = sig("tensor", mm, "pe")
            wait("scalar", "pe", p1)
            qA.activation(out=E_s, in_=psum[bs][:, 0:128], func=AF.Exp, scale=ISQHD)
            s1 = sig("scalar", qA.drain(), "sc")
            bank_mark(bs, "sc")
            wait("vector", "sc", s1)
            qV.tensor_tensor(out=A_s, in0=E_s, in1=bdmask[:], op=OP.mult)
            qV.drain()
            qV.tensor_reduce(out=colA[:, 0:1], in_=A_s, axis=AX, op=OP.add)
            qV.reciprocal(out=colA[:, 1:2], in_=colA[:, 0:1])
            qV.tensor_scalar(out=A_s, in0=A_s, scalar1=colA[:, 1:2], scalar2=None,
                             op0=OP.mult)
            v1 = sig("vector", qV.drain(), "vec")
            wait("tensor", "vec", v1)
            bank_wait(6)
            qT.transpose(out=psT[:, 0:128], in_=A_s, identity=eyeb[:])
            qT.transpose(out=psT[:, 128:256], in_=VGT_s, identity=eyeb[:])
            p2 = sig("tensor", qT.drain(), "pe")
            wait("vector", "pe", p2)
            qV.tensor_copy(out=AT_s, in_=psT[:, 0:128])
            qV.tensor_copy(out=VG_s, in_=psT[:, 128:256])
            v2 = sig("vector", qV.drain(), "vec")
            bank_mark(6, "vec")
            wait("tensor", "vec", v2)
            bank_wait(bc)
            mm = qT.matmul(out=psum[bc][:, 0:128], lhsT=AT_s, rhs=VG_s, start=True, stop=True)
            p3 = sig("tensor", mm, "pe")
            wait("vector", "pe", p3)
            qV.tensor_copy(out=CG_s, in_=psum[bc][:, 0:128])
            v3 = sig("vector", qV.drain(), "vec")
            bank_mark(bc, "vec")
            wait("tensor", "vec", v3)
            bank_wait(6)
            qT.transpose(out=psT[:, 256:384], in_=CG_s, identity=eyeb[:])
            p4 = sig("tensor", qT.drain(), "pe")
            wait("vector", "pe", p4)
            qV.tensor_copy(out=yeR[:, t0:t0 + 8, :], in_=psT[:, 256:384])
            sig("vector", qV.drain(), "vec")
            bank_mark(6, "vec")
            if DEV_TAIL == "grp0" and e8 == 0 and g == 0:
                wait("gpsimd", "vec")
                dma("gpsimd", out_h[0:128, 0:128], QG_s)
                dma("gpsimd", out_h[0:128, 128:256], KG_s)
                dma("gpsimd", out_h[0:128, 256:384], E_s)
                dma("gpsimd", out_h[0:128, 384:512], tmpHH[:, sl + 128:sl + 256])
                dma("gpsimd", out_h[0:128, 512:640], tmpHH[:, sl + 384:sl + 512])
                dma("gpsimd", out_h[0:128, 640:768], tmpHH[:, sl + 512:sl + 640])
                dma("gpsimd", out_h[128:256, 0:2048], AR2[:, 12288:14336])
                dma("gpsimd", out_h[384:512, 0:2048], AR2[:, 0:2048])
                wait("gpsimd", "qG")
                for _e in ("sync", "vector", "tensor", "scalar"):
                    wait(_e, "qG")
        v_ye = cnt["vec"]
        for foc in range(4):
            wait("gpsimd", "pe")
            botok = dma("gpsimd", rbufB[:], bo_h[:, foc * 512:(foc + 1) * 512])
            xf_toks = []
            for tt2 in range(2):
                rtt = e8 * 256 + tt2 * 128
                wbuf = (wldA, wldB)[tt2]
                wait("sync", "vec")
                dma("sync", wbuf[:, 0:512], x_in[rtt:rtt + 128, foc * 512:(foc + 1) * 512])
                xf_toks.append(dma("sync", wbuf[:, 512:1024],
                                   FFN0[rtt:rtt + 128, foc * 512:(foc + 1) * 512]))
            for ff in range(16):
                slot = WR[:, (ff % 2) * 512:(ff % 2) * 512 + 512]
                wait("gpsimd", "pe")
                wtok = dma("gpsimd", slot,
                           WoT[ff * 128:(ff + 1) * 128, foc * 512:(foc + 1) * 512])
                wait("tensor", "qG", wtok)
                if ff == 0:
                    wait("tensor", "vec", v_ye)
                    bank_wait(4)
                    bank_wait(5)
                for tt2 in range(2):
                    mmr = qT.matmul(
                        out=psum[4 + tt2][:],
                        lhsT=AR2[:, 12288 + ff * 256 + tt2 * 128:
                                 12288 + ff * 256 + tt2 * 128 + 128],
                        rhs=slot, start=(ff == 0), stop=False)
                wo_tok[ff % 2] = sig("tensor", mmr, "pe")
            wait("tensor", "qG", botok)
            for tt2 in range(2):
                rtt = e8 * 256 + tt2 * 128
                wbuf = (wldA, wldB)[tt2]
                mm = qT.matmul(out=psum[4 + tt2][:], lhsT=onesMb[:], rhs=rbufB[:],
                               start=False, stop=True)
                ptok = sig("tensor", mm, "pe")
                wait("vector", "pe", ptok)
                wait("vector", "qS", xf_toks[tt2])
                wait("vector", "qG", ost_tok[(foc % 2) * 2 + tt2])
                oslot = tmpB[:, ((foc % 2) * 2 + tt2) * 512:((foc % 2) * 2 + tt2) * 512 + 512]
                qV.tensor_tensor(out=oslot, in0=psum[4 + tt2][:], in1=wbuf[:, 0:512],
                                 op=OP.add)
                qV.tensor_tensor(out=oslot, in0=oslot, in1=wbuf[:, 512:1024], op=OP.add)
                vtok = sig("vector", qV.drain(), "vec")
                bank_mark(4 + tt2, "vec")
                wait("gpsimd", "vec", vtok)
                ost_tok[(foc % 2) * 2 + tt2] = dma(
                    "gpsimd", X1[rtt:rtt + 128, foc * 512:(foc + 1) * 512], oslot)

    barrier()

    def row_stats_bcast(src_r, n_rt, dst_ws, iws_region):
        for rt in range(n_rt):
            buf = (wldA, wldB)[rt % 2]
            wait("sync", "vec")
            tok = dma("sync", buf[:], src_r[rt * 128:(rt + 1) * 128, :])
            wait("vector", "qS", tok)
            qV.tensor_reduce(out=colA[:, 0:1], in_=buf[:], axis=AX, op=OP.max,
                             apply_absolute_value=True)
            qV.tensor_scalar(out=colA[:, 1:2], in0=colA[:, 0:1], scalar1=1.0 / 127.0,
                             scalar2=EPS_I8, op0=OP.mult, op1=OP.max)
            qV.reciprocal(out=colA[:, 2:3], in_=colA[:, 1:2])
            v1 = sig("vector", qV.drain(), "vec")
            wait("tensor", "vec", v1)
            bank_wait(7)
            qT.transpose(out=psM[0:1, 0:128], in_=colA[:, 1:2], identity=eyef[:])
            qT.transpose(out=psM[0:1, 128:256], in_=colA[:, 2:3], identity=eyef[:])
            p1 = sig("tensor", qT.drain(), "pe")
            wait("vector", "pe", p1)
            qV.tensor_copy(out=rbufR[0:1, 0:128], in_=psM[0:1, 0:128])
            qV.tensor_copy(out=rbufB[0:1, 0:128].bitcast(F32), in_=psM[0:1, 128:256]) \
                if False else qV.tensor_copy(out=colB[0:1, 0:64], in_=psM[0:1, 128:192])
            qV.tensor_copy(out=colC[0:1, 0:64], in_=psM[0:1, 192:256])
            v2 = sig("vector", qV.drain(), "vec")
            wait("tensor", "vec", v2)
            qT.matmul(out=psM[:, 256:384], lhsT=onesM[:], rhs=rbufR[0:1, 0:128],
                      start=True, stop=True)
            qT.matmul(out=psM[:, 384:448], lhsT=onesM[:], rhs=colB[0:1, 0:64],
                      start=True, stop=True)
            qT.matmul(out=psM[:, 448:512], lhsT=onesM[:], rhs=colC[0:1, 0:64],
                      start=True, stop=True)
            p2 = sig("tensor", qT.drain(), "pe")
            wait("vector", "pe", p2)
            qV.tensor_copy(out=dst_ws[:, rt * 128:(rt + 1) * 128], in_=psM[:, 256:384])
            qV.tensor_copy(out=iws_region[:, rt * 128:rt * 128 + 64], in_=psM[:, 384:448])
            qV.tensor_copy(out=iws_region[:, rt * 128 + 64:(rt + 1) * 128], in_=psM[:, 448:512])
            sig("vector", qV.drain(), "vec")
            bank_mark(7, "vec")

    iwsF = AR2[:, 0:FQ]
    iwsO = AR2[:, FQ:FQ + D]
    row_stats_bcast(Wf_r, FQ // 128, wsA, iwsF)
    row_stats_bcast(Wo1_r, ND, wsB, iwsO)

    def quant_int8_T(srcT, n_dd, width, iws_region, dstQ):
        for dd in range(n_dd):
            for c in range((width + 2047) // 2048):
                w = min(2048, width - c * 2048)
                buf = (wldA, wldB)[(dd + c) % 2]
                wait("sync", "vec")
                wait("sync", "qG")
                tok = dma("sync", buf[:, 0:w],
                          srcT[dd * 128:(dd + 1) * 128, c * 2048:c * 2048 + w])
                wait("vector", "qS", tok)
                qV.tensor_tensor(out=tmpA[:, 0:w], in0=buf[:, 0:w],
                                 in1=iws_region[:, c * 2048:c * 2048 + w], op=OP.mult)
                qV.tensor_copy(out=IH[:, 0:w], in_=tmpA[:, 0:w])
                qV.tensor_scalar(out=WSTG[:, 0:w], in0=IH[:, 0:w], scalar1=127.0,
                                 scalar2=-127.0, op0=OP.min, op1=OP.max)
                vtok = sig("vector", qV.drain(), "vec")
                wait("gpsimd", "vec", vtok)
                dma("gpsimd", dstQ[dd * 128:(dd + 1) * 128, c * 2048:c * 2048 + w],
                    WSTG[:, 0:w])
                wait("vector", "qG")

    quant_int8_T(WfT, ND, FQ, iwsF, WFQ)
    quant_int8_T(Wo1T, ND, D, iwsO, WOQ)

    barrier()

    TBW = WR[:, 0:2048]

    def quant_mx_w(src_r, n_rt, width, dstQT):
        nch = width // 2048
        for rti in range(n_rt * nch):
            rt, ch = divmod(rti, nch)
            buf = (wldA, wldB)[rti % 2]
            wait("sync", "vec")
            tok = dma("sync", buf[:],
                      src_r[rt * 128:(rt + 1) * 128, ch * 2048:(ch + 1) * 2048])
            wait("vector", "qS", tok)
            qV.tensor_reduce(out=colA[:, 0:64],
                             in_=buf[:].rearrange("p (b k) -> p b k", k=32),
                             axis=AX, op=OP.max, apply_absolute_value=True)
            v1 = sig("vector", qV.drain(), "vec")
            wait("scalar", "vec", v1)
            qA.activation(out=colB[:, 0:64], in_=colA[:, 0:64], func=AF.Ln)
            s1 = sig("scalar", qA.drain(), "sc")
            wait("vector", "sc", s1)
            qV.tensor_scalar(out=colB[:, 0:64], in0=colB[:, 0:64], scalar1=RLN2,
                             scalar2=None, op0=OP.mult)
            qV.tensor_copy(out=colI[:, 0:64], in_=colB[:, 0:64])
            qV.tensor_tensor(out=colC[:, 0:64], in0=colI[:, 0:64], in1=colB[:, 0:64],
                             op=OP.is_gt)
            qV.tensor_tensor(out=colD[:, 0:64], in0=colI[:, 0:64], in1=colC[:, 0:64],
                             op=OP.subtract)
            qV.tensor_scalar(out=colD[:, 0:64], in0=colD[:, 0:64], scalar1=-2.0,
                             scalar2=127.0, op0=OP.add, op1=OP.min)
            qV.tensor_scalar(out=colD[:, 0:64], in0=colD[:, 0:64], scalar1=-127.0,
                             scalar2=None, op0=OP.max)
            qV.tensor_tensor(out=tmpA[:], in0=buf[:], in1=buf[:], op=OP.mult)
            v2 = sig("vector", qV.drain(), "vec")
            wait("scalar", "vec", v2)
            qA.activation(out=tmpB[:], in_=tmpA[:], func=AF.Ln)
            s2 = sig("scalar", qA.drain(), "sc")
            wait("vector", "sc", s2)
            qV.tensor_scalar(out=tmpB[:], in0=tmpB[:], scalar1=RLN2 * 0.5,
                             scalar2=-0.5, op0=OP.mult, op1=OP.add)
            qV.tensor_copy(out=IH[:], in_=tmpB[:])
            qV.tensor_tensor(out=tmpB[:].rearrange("p (b k) -> p b k", k=32),
                             in0=IH[:].rearrange("p (b k) -> p b k", k=32),
                             in1=colD[:, 0:64].unsqueeze(2).broadcast_to((128, 64, 32)),
                             op=OP.max)
            v3 = sig("vector", qV.drain(), "vec")
            wait("scalar", "vec", v3)
            qA.activation(out=tmpA[:], in_=tmpB[:], func=AF.Exp, scale=-LN2C, bias=scal[:, 10:11])
            qA.activation(out=colC[:, 0:64], in_=colD[:, 0:64], func=AF.Exp,
                          scale=LN2C, bias=scal[:, 12:13])
            s3 = sig("scalar", qA.drain(), "sc")
            wait("vector", "sc", s3)
            qV.tensor_tensor(out=tmpA[:], in0=buf[:], in1=tmpA[:], op=OP.mult)
            qV.tensor_copy(out=IH[:], in_=tmpA[:])
            v4 = sig("vector", qV.drain(), "vec")
            wait("scalar", "vec", v4)
            qA.activation(out=tmpB[:], in_=tmpB[:], func=AF.Exp, scale=LN2C, bias=scal[:, 11:12])
            s4 = sig("scalar", qA.drain(), "sc")
            wait("vector", "sc", s4)
            qV.tensor_scalar(out=colB[:, 0:64], in0=colC[:, 0:64], scalar1=-1.0,
                             scalar2=None, op0=OP.mult)
            qV.tensor_tensor(out=tmpA[:], in0=IH[:], in1=tmpB[:], op=OP.mult)
            qV.tensor_tensor(out=tmpA[:].rearrange("p (b k) -> p b k", k=32),
                             in0=tmpA[:].rearrange("p (b k) -> p b k", k=32),
                             in1=colC[:, 0:64].unsqueeze(2).broadcast_to((128, 64, 32)),
                             op=OP.min)
            qV.tensor_tensor(out=WSTG[:].rearrange("p (b k) -> p b k", k=32),
                             in0=tmpA[:].rearrange("p (b k) -> p b k", k=32),
                             in1=colB[:, 0:64].unsqueeze(2).broadcast_to((128, 64, 32)),
                             op=OP.max)
            vtok = sig("vector", qV.drain(), "vec")
            wait("tensor", "vec", vtok)
            for cg in range(4):
                wait("tensor", "gp")
                for c in range(cg * 4, cg * 4 + 4):
                    tr = qT.transpose(out=psT[:, (c % 4) * 128:((c % 4) + 1) * 128],
                                      in_=WSTG[:, c * 128:(c + 1) * 128],
                                      identity=eyeb[:])
                qT.transpose(out=psT[:, 512:640], in_=eyeb[:], identity=eyeb[:])
                ptk = sig("tensor", qT.drain(), "pe")
                wait("vector", "pe", ptk)
                if cg == 0:
                    wait("vector", "qG")
                for c in range(cg * 4, cg * 4 + 4):
                    qV.tensor_copy(out=TBW[:, c * 128:(c + 1) * 128],
                                   in_=psT[:, (c % 4) * 128:((c % 4) + 1) * 128])
                sig("vector", qV.drain(), "gp")
            wait("gpsimd", "gp")
            dma("gpsimd", dstQT[ch * 2048:(ch + 1) * 2048, rt * 128:(rt + 1) * 128]
                .rearrange("(c p) j -> p c j", p=128),
                TBW[:].rearrange("p (c j) -> p c j", j=128))
            wait("vector", "pe")

    quant_mx_w(Wg_r, NF, D, WGQT)
    quant_mx_w(Wu_r, NF, D, WUQT)
    quant_mx_w(Wd_r, ND, DFF, WDQT)

    barrier()

    def mx_act_quant(src_f32, w, dst_bf16, nblk, dbg=None):
        kb = 32
        qV.tensor_reduce(out=colA[:, 0:nblk],
                         in_=src_f32.rearrange("p (b k) -> p b k", k=kb),
                         axis=AX, op=OP.max, apply_absolute_value=True)
        v1 = sig("vector", qV.drain(), "vec")
        wait("scalar", "vec", v1)
        qA.activation(out=colB[:, 0:nblk], in_=colA[:, 0:nblk], func=AF.Ln)
        s1 = sig("scalar", qA.drain(), "sc")
        wait("vector", "sc", s1)
        if dbg is not None:
            qV.tensor_copy(out=wsB[:, dbg:dbg + 8], in_=colA[:, 0:nblk])
            qV.tensor_copy(out=wsB[:, dbg + 8:dbg + 16], in_=colB[:, 0:nblk])
        qV.tensor_scalar(out=colB[:, 0:nblk], in0=colB[:, 0:nblk], scalar1=RLN2,
                         scalar2=None, op0=OP.mult)
        qV.tensor_copy(out=colI[:, 0:nblk], in_=colB[:, 0:nblk])
        qV.tensor_tensor(out=colC[:, 0:nblk], in0=colI[:, 0:nblk], in1=colB[:, 0:nblk],
                         op=OP.is_gt)
        qV.tensor_tensor(out=colD[:, 0:nblk], in0=colI[:, 0:nblk], in1=colC[:, 0:nblk],
                         op=OP.subtract)
        qV.tensor_scalar(out=colD[:, 0:nblk], in0=colD[:, 0:nblk], scalar1=-2.0,
                         scalar2=127.0, op0=OP.add, op1=OP.min)
        qV.tensor_scalar(out=colD[:, 0:nblk], in0=colD[:, 0:nblk], scalar1=-127.0,
                         scalar2=None, op0=OP.max)
        if dbg is not None:
            qV.drain()
            qV.tensor_copy(out=wsB[:, dbg + 16:dbg + 24], in_=colB[:, 0:nblk])
            qV.tensor_copy(out=wsB[:, dbg + 24:dbg + 32], in_=colI[:, 0:nblk])
            qV.tensor_copy(out=wsB[:, dbg + 32:dbg + 40], in_=colD[:, 0:nblk])
        v2 = sig("vector", qV.drain(), "vec")
        wait("scalar", "vec", v2)
        qA.activation(out=colB[:, 0:nblk], in_=colD[:, 0:nblk], func=AF.Exp,
                      scale=-LN2C, bias=scal[:, 14:15])
        qA.activation(out=colC[:, 0:nblk], in_=colD[:, 0:nblk], func=AF.Exp,
                      scale=LN2C, bias=scal[:, 13:14])
        s2 = sig("scalar", qA.drain(), "sc")
        wait("vector", "sc", s2)
        if dbg is not None:
            qV.tensor_copy(out=wsB[:, dbg + 40:dbg + 48], in_=colB[:, 0:nblk])
            qV.tensor_copy(out=wsB[:, dbg + 48:dbg + 56], in_=colC[:, 0:nblk])
        qV.tensor_tensor(out=tmpA[:, 0:w].rearrange("p (b k) -> p b k", k=kb),
                         in0=src_f32.rearrange("p (b k) -> p b k", k=kb),
                         in1=colB[:, 0:nblk].unsqueeze(2).broadcast_to((128, nblk, kb)),
                         op=OP.mult)
        qV.tensor_scalar(out=tmpA[:, 0:w], in0=tmpA[:, 0:w], scalar1=7.5 / 64.0,
                         scalar2=-7.5 / 64.0, op0=OP.min, op1=OP.max)
        qV.tensor_copy(out=tmpF8[:, 0:w], in_=tmpA[:, 0:w])
        qV.drain()
        qV.tensor_tensor(out=dst_bf16.rearrange("p (b k) -> p b k", k=kb),
                         in0=tmpF8[:, 0:w].rearrange("p (b k) -> p b k", k=kb),
                         in1=colC[:, 0:nblk].unsqueeze(2).broadcast_to((128, nblk, kb)),
                         op=OP.mult)
        return sig("vector", qV.drain(), "vec")

    q_read = [cnt["vec"]]
    for half in range(2):
        pe_g = cnt["pe"]
        for th in range(8):
            tt = half * 8 + th
            wait("sync", "vec")
            tok = dma("sync", xtile[:], X1[tt * 128:(tt + 1) * 128, :])
            wait("vector", "qS", tok)
            if th == 0:
                wait("vector", "pe", pe_g)
            for c in range(4):
                qV.bn_stats(out=tmpB[:, c * 6:c * 6 + 6], in_=xtile[:, c * 512:(c + 1) * 512])
            qV.drain()
            qV.bn_aggr(out=colE[:, 0:2], in_=tmpB[:, 0:24])
            qV.tensor_scalar(out=colE[:, 2:3], in0=colE[:, 1:2], scalar1=EPS_LN,
                             scalar2=None, op0=OP.add)
            qV.reciprocal(out=colE[:, 3:4], in_=colE[:, 2:3])
            v1 = sig("vector", qV.drain(), "vec")
            wait("scalar", "vec", v1)
            qA.activation(out=colE[:, 4:5], in_=colE[:, 3:4], func=AF.Sqrt)
            s1 = sig("scalar", qA.drain(), "sc")
            wait("vector", "sc", s1)
            qV.tensor_scalar(out=tmpB[:], in0=xtile[:], scalar1=colE[:, 0:1],
                             scalar2=colE[:, 4:5], op0=OP.subtract, op1=OP.mult)
            qV.drain()
            vtok = mx_act_quant(tmpB[:], 2048, WSTG[:], 64)
            wait("sync", "vec", vtok)
            for c in range(ND):
                dma("sync", AR2[:, c * 1024 + th * 128: c * 1024 + (th + 1) * 128],
                    WSTG[:, c * 128:(c + 1) * 128], transpose=True)
        t_half = cnt["qS"]
        wait("tensor", "qS", t_half)
        for fc in range(32):
            pe_now = cnt["pe"]
            wait("sync", "pe", pe_now)
            wait("scalar", "pe", pe_now)
            for dd in range(ND):
                qe = ("sync", "scalar")[dd % 2]
                dma(qe, WR[:, dd * 256:(dd + 1) * 256],
                    WGQT[dd * 128:(dd + 1) * 128, fc * 256:(fc + 1) * 256])
                dma(qe, WR[:, 4096 + dd * 256: 4096 + (dd + 1) * 256],
                    WUQT[dd * 128:(dd + 1) * 128, fc * 256:(fc + 1) * 256])
            wait("tensor", "qS", cnt["qS"])
            wait("tensor", "qA", cnt["qA"])
            for th in range(8):
                tt = half * 8 + th
                bg, bu = th % 2, 2 + th % 2
                bank_wait(bg)
                bank_wait(bu)
                for dd in range(ND):
                    qT.matmul(out=psum[bg][:, 0:256],
                              lhsT=AR2[:, dd * 1024 + th * 128: dd * 1024 + (th + 1) * 128],
                              rhs=WR[:, dd * 256:(dd + 1) * 256],
                              start=(dd == 0), stop=(dd == ND - 1))
                    mmr = qT.matmul(out=psum[bu][:, 0:256],
                                    lhsT=AR2[:, dd * 1024 + th * 128: dd * 1024 + (th + 1) * 128],
                                    rhs=WR[:, 4096 + dd * 256: 4096 + (dd + 1) * 256],
                                    start=(dd == 0), stop=(dd == ND - 1))
                ptok = sig("tensor", mmr, "pe")
                wait("scalar", "pe", ptok)
                if th == 0:
                    wait("scalar", "vec", q_read[0])
                sgs = tmpB[:, th * 256:(th + 1) * 256]
                qA.activation(out=sgs, in_=psum[bg][:, 0:256], func=AF.Silu)
                stok = sig("scalar", qA.drain(), "sc")
                bank_mark(bg, "sc")
                wait("vector", "sc", stok)
                qV.tensor_tensor(out=sgs, in0=sgs, in1=psum[bu][:, 0:256], op=OP.mult)
                sig("vector", qV.drain(), "vec")
                bank_mark(bu, "vec")
            wait("vector", "qG")
            q_read[0] = mx_act_quant(tmpB[:], 2048, WSTG[:], 64)
            wait("gpsimd", "vec", q_read[0])
            dma("gpsimd",
                QINT[half * 1024:(half + 1) * 1024, fc * 256:(fc + 1) * 256]
                .rearrange("(t p) c -> p t c", p=128),
                WSTG[:].rearrange("p (t c) -> p t c", c=256))

    if DEV_TAIL == "qx":
        wait("gpsimd", "qS")
        for c in range(ND):
            dma("gpsimd", GSP[c * 128:(c + 1) * 128, 0:1024], AR2[:, c * 1024:(c + 1) * 1024])
        wait("gpsimd", "qG")

    barrier()

    for tt in range(NT):
        wait("sync", "vec")
        tok = dma("sync", xtile[:], X1[tt * 128:(tt + 1) * 128, :])
        wait("vector", "qS", tok)
        for c in range(4):
            qV.bn_stats(out=tmpB[:, c * 6:c * 6 + 6], in_=xtile[:, c * 512:(c + 1) * 512])
        qV.drain()
        qV.bn_aggr(out=colE[:, 0:2], in_=tmpB[:, 0:24])
        qV.tensor_scalar(out=colE[:, 2:3], in0=colE[:, 1:2], scalar1=EPS_LN,
                         scalar2=None, op0=OP.add)
        qV.reciprocal(out=colE[:, 3:4], in_=colE[:, 2:3])
        v1 = sig("vector", qV.drain(), "vec")
        wait("scalar", "vec", v1)
        qA.activation(out=colE[:, 4:5], in_=colE[:, 3:4], func=AF.Sqrt)
        s1 = sig("scalar", qA.drain(), "sc")
        wait("vector", "sc", s1)
        qV.tensor_reduce(out=colD[:, 0:1], in_=xtile[:], axis=AX, op=OP.max)
        qV.tensor_reduce(out=colD[:, 1:2], in_=xtile[:], axis=AX, op=OP.min)
        qV.drain()
        qV.tensor_tensor(out=colD[:, 2:3], in0=colD[:, 0:1], in1=colE[:, 0:1], op=OP.subtract)
        qV.tensor_tensor(out=colD[:, 3:4], in0=colE[:, 0:1], in1=colD[:, 1:2], op=OP.subtract)
        qV.tensor_tensor(out=colD[:, 4:5], in0=colD[:, 2:3], in1=colD[:, 3:4], op=OP.max)
        qV.tensor_tensor(out=colD[:, 5:6], in0=colD[:, 4:5], in1=colE[:, 4:5], op=OP.mult)
        qV.tensor_scalar(out=stat[:, tt * 8 + 4:tt * 8 + 5], in0=colD[:, 5:6],
                         scalar1=1.0 / 127.0, scalar2=EPS_I8, op0=OP.mult, op1=OP.max)
        qV.drain()
        qV.reciprocal(out=colD[:, 6:7], in_=stat[:, tt * 8 + 4:tt * 8 + 5])
        qV.tensor_tensor(out=colD[:, 7:8], in0=colE[:, 4:5], in1=colD[:, 6:7], op=OP.mult)
        qV.drain()
        qV.tensor_scalar(out=tmpB[:], in0=xtile[:], scalar1=colE[:, 0:1],
                         scalar2=colD[:, 7:8], op0=OP.subtract, op1=OP.mult)
        qV.tensor_copy(out=IH[:], in_=tmpB[:])
        qV.tensor_scalar(out=WSTG[:], in0=IH[:], scalar1=127.0, scalar2=-127.0,
                         op0=OP.min, op1=OP.max)
        vtok = sig("vector", qV.drain(), "vec")
        wait("sync", "vec", vtok)
        for c in range(ND):
            dma("sync", AR1[:, c * S + tt * 128: c * S + (tt + 1) * 128],
                WSTG[:, c * 128:(c + 1) * 128], transpose=True)

    barrier()

    ebuf = WR[:, 0:8192]
    kT_g = AR2[:, 0:2048]
    qT_h = AR2[:, 2048:4096]
    WFS = AR2[:, 4096:4096 + 16 * 128]

    def l1_proj(fslice, dst_kind, rms_rope, gain_col, out_cb):
        wait("sync", "pe")
        for dd in range(ND):
            dma("sync", WFS[:, dd * 128:(dd + 1) * 128],
                WFQ[dd * 128:(dd + 1) * 128, fslice:fslice + 128])
        wtok = cnt["qS"]
        wait("tensor", "qS", wtok)
        for ttg in range(4):
            for ti in range(4):
                tt = ttg * 4 + ti
                if ti == 0:
                    bank_wait(4)
                for dd in range(ND):
                    mmr = qT.matmul(out=psum[4][:, ti * 128:(ti + 1) * 128],
                                    lhsT=AR1[:, dd * S + tt * 128: dd * S + (tt + 1) * 128],
                                    rhs=WFS[:, dd * 128:(dd + 1) * 128],
                                    start=(dd == 0), stop=(dd == ND - 1))
            ptok = sig("tensor", mmr, "pe")
            wait("vector", "pe", ptok)
            for ti in range(4):
                tt = ttg * 4 + ti
                pslot = psum[4][:, ti * 128:(ti + 1) * 128]
                fslot = tmpA[:, (ti % 2) * 1024:(ti % 2) * 1024 + 128]
                qV.scalar_tensor_tensor(out=fslot, in0=pslot,
                                        scalar=stat[:, tt * 8 + 4:tt * 8 + 5],
                                        in1=wsA[:, fslice:fslice + 128],
                                        op0=OP.mult, op1=OP.mult)
                if rms_rope:
                    qV.drain()
                    qV.scalar_tensor_tensor(out=tmpB[:, 0:128], in0=fslot, scalar=1.0,
                                            in1=fslot, op0=OP.mult, op1=OP.mult,
                                            accum_out=colA[:, 0:1])
                    qV.tensor_scalar(out=colA[:, 1:2], in0=colA[:, 0:1],
                                     scalar1=1.0 / 128.0, scalar2=F32EPS,
                                     op0=OP.mult, op1=OP.add)
                    qV.reciprocal(out=colA[:, 2:3], in_=colA[:, 1:2])
                    vv = sig("vector", qV.drain(), "vec")
                    wait("scalar", "vec", vv)
                    qA.activation(out=colA[:, 3:4], in_=colA[:, 2:3], func=AF.Sqrt)
                    ss = sig("scalar", qA.drain(), "sc")
                    wait("vector", "sc", ss)
                    if gain_col is not None:
                        qV.tensor_tensor(out=colA[:, 4:5], in0=colA[:, 3:4],
                                         in1=gain_col, op=OP.mult)
                    else:
                        qV.tensor_copy(out=colA[:, 4:5], in_=colA[:, 3:4])
                    qV.drain()
                    qV.tensor_scalar(out=fslot, in0=fslot, scalar1=colA[:, 4:5],
                                     scalar2=None, op0=OP.mult)
                    cosb = ropeC[:, tt * 64:(tt + 1) * 64]
                    sinb = ropeS[:, tt * 64:(tt + 1) * 64]
                    qV.tensor_tensor(out=tmpB[:, 0:64], in0=fslot[:, 0:64], in1=cosb, op=OP.mult)
                    qV.tensor_tensor(out=tmpB[:, 64:128], in0=fslot[:, 64:128], in1=sinb, op=OP.mult)
                    qV.tensor_tensor(out=tmpB[:, 128:192], in0=fslot[:, 0:64], in1=sinb, op=OP.mult)
                    qV.tensor_tensor(out=tmpB[:, 192:256], in0=fslot[:, 64:128], in1=cosb, op=OP.mult)
                    qV.tensor_tensor(out=WSTG[:, 0:64], in0=tmpB[:, 0:64],
                                     in1=tmpB[:, 64:128], op=OP.add)
                    qV.tensor_tensor(out=WSTG[:, 64:128], in0=tmpB[:, 192:256],
                                     in1=tmpB[:, 128:192], op=OP.subtract)
                else:
                    qV.tensor_copy(out=WSTG[:, 0:128], in_=fslot)
                vtok = sig("vector", qV.drain(), "vec")
                out_cb(tt, WSTG[:, 0:128], vtok)
            bank_mark(4, "vec")

    kv_gain = None
    for kv in range(NKV):
        def k_cb(tt, slot, vtok):
            wait("sync", "vec", vtok)
            dma("sync", kT_g[:, tt * 128:(tt + 1) * 128], slot, transpose=True)
        l1_proj(D + kv * 128, "k", True, None, k_cb)
        def v_cb(tt, slot, vtok):
            wait("gpsimd", "vec", vtok)
            g1 = dma("gpsimd", YSP[tt * 128:(tt + 1) * 128, 0:128], slot)
            wait("vector", "qG", g1)
        def v_cb2(tt, slot, vtok):
            qV.tensor_copy(out=vext[:, tt * 132:tt * 132 + 128], in_=slot)
            sig("vector", qV.drain(), "vec")
        l1_proj(D + KVD + kv * 128, "v", False, None, v_cb2)
        for qh in range(kv * 4, kv * 4 + 4):
            def q_cb(tt, slot, vtok):
                wait("sync", "vec", vtok)
                dma("sync", qT_h[:, tt * 128:(tt + 1) * 128], slot, transpose=True)
            l1_proj(qh * 128, "q", True, gainb[:, qh:qh + 1], q_cb)
            wait("tensor", "qS")
            wait("tensor", "vec")
            for tqc in range(4):
                ntk = 4 * tqc + 4
                sc_base = cnt["sc"]
                for tkt in range(ntk):
                    b = tkt % 2
                    bank_wait(b)
                    mm = qT.matmul(out=psum[b][:], lhsT=kT_g[:, tkt * 128:(tkt + 1) * 128],
                                   rhs=qT_h[:, tqc * 512:(tqc + 1) * 512],
                                   start=True, stop=True)
                    ptok = sig("tensor", mm, "pe")
                    wait("scalar", "pe", ptok)
                    qA.activation(out=ebuf[:, tkt * 512:(tkt + 1) * 512], in_=psum[b][:],
                                  func=AF.Exp, scale=ISQHD)
                    stok = sig("scalar", qA.drain(), "sc")
                    bank_mark(b, "sc")
                    if tkt >= 4 * tqc:
                        a = tkt - 4 * tqc
                        wait("vector", "sc", stok)
                        qV.tensor_tensor(out=ebuf[:, tkt * 512:(tkt + 1) * 512],
                                         in0=ebuf[:, tkt * 512:(tkt + 1) * 512],
                                         in1=dgmask[:, a * 512:(a + 1) * 512], op=OP.mult)
                        sig("vector", qV.drain(), "vec")
                wait("tensor", "sc")
                wait("tensor", "vec")
                for ti in range(4):
                    Tg = tqc * 4 + ti
                    b2 = 2 + ti % 2
                    bank_wait(b2)
                    for tkt in range(Tg + 1):
                        mm = qT.matmul(out=psum[b2][:, 0:129],
                                       lhsT=ebuf[:, tkt * 512 + ti * 128: tkt * 512 + (ti + 1) * 128],
                                       rhs=vext[:, tkt * 132:tkt * 132 + 129],
                                       start=(tkt == 0), stop=(tkt == Tg))
                    ptok = sig("tensor", mm, "pe")
                    wait("vector", "pe", ptok)
                    qV.reciprocal(out=colA[:, 8:9], in_=psum[b2][:, 128:129])
                    qV.drain()
                    yslot = tmpA[:, 1536 + (ti % 2) * 128: 1536 + (ti % 2) * 128 + 128]
                    qV.tensor_scalar(out=yslot, in0=psum[b2][:, 0:128],
                                     scalar1=colA[:, 8:9], scalar2=None, op0=OP.mult)
                    vtok = sig("vector", qV.drain(), "vec")
                    bank_mark(b2, "vec")
                    wait("gpsimd", "vec", vtok)
                    g1 = dma("gpsimd", YSP[Tg * 128:(Tg + 1) * 128, qh * 128:(qh + 1) * 128],
                             yslot.bitcast(F32) if False else yslot)
                    wait("vector", "qG", g1)

    barrier()

    YB = WR[:, 0:4096]
    for tt in range(NT):
        wait("sync", "vec")
        tok = dma("sync", YB[:, 0:2048], YSP[tt * 128:(tt + 1) * 128, :])
        wait("vector", "qS", tok)
        qV.tensor_reduce(out=colA[:, 0:1], in_=YB[:, 0:2048], axis=AX, op=OP.max,
                         apply_absolute_value=True)
        qV.tensor_scalar(out=stat[:, tt * 8 + 5:tt * 8 + 6], in0=colA[:, 0:1],
                         scalar1=1.0 / 127.0, scalar2=EPS_I8, op0=OP.mult, op1=OP.max)
        qV.drain()
        qV.reciprocal(out=colA[:, 1:2], in_=stat[:, tt * 8 + 5:tt * 8 + 6])
        qV.drain()
        qV.tensor_scalar(out=tmpB[:], in0=YB[:, 0:2048], scalar1=colA[:, 1:2],
                         scalar2=None, op0=OP.mult)
        qV.tensor_copy(out=IH[:], in_=tmpB[:])
        qV.tensor_scalar(out=WSTG[:], in0=IH[:], scalar1=127.0, scalar2=-127.0,
                         op0=OP.min, op1=OP.max)
        vtok = sig("vector", qV.drain(), "vec")
        wait("sync", "vec", vtok)
        for c in range(ND):
            dma("sync", AR1[:, c * S + tt * 128: c * S + (tt + 1) * 128],
                WSTG[:, c * 128:(c + 1) * 128], transpose=True)

    b4_tok = [0, 0]
    for foc in range(4):
        for ttg in range(4):
            for dd in range(ND):
                slot = WR[:, 4096 + (dd % 2) * 512: 4096 + (dd % 2) * 512 + 512]
                wait("sync", "pe")
                rtok = dma("sync", slot, WOQ[dd * 128:(dd + 1) * 128,
                                             foc * 512:(foc + 1) * 512])
                wait("tensor", "qS", rtok)
                for ti in range(4):
                    tt = ttg * 4 + ti
                    if dd == 0:
                        bank_wait(ti)
                    mmr = qT.matmul(out=psum[ti][:],
                                    lhsT=AR1[:, dd * S + tt * 128: dd * S + (tt + 1) * 128],
                                    rhs=slot, start=(dd == 0), stop=(dd == ND - 1))
                    if ti == 3:
                        b4_tok[dd % 2] = sig("tensor", mmr, "pe")
            for ti in range(4):
                tt = ttg * 4 + ti
                wait("vector", "pe")
                oslot = tmpB[:, (ti % 4) * 512:(ti % 4) * 512 + 512]
                qV.scalar_tensor_tensor(out=oslot, in0=psum[ti][:],
                                        scalar=stat[:, tt * 8 + 5:tt * 8 + 6],
                                        in1=wsB[:, foc * 512:(foc + 1) * 512],
                                        op0=OP.mult, op1=OP.mult)
                vtok = sig("vector", qV.drain(), "vec")
                bank_mark(ti, "vec")
                wait("gpsimd", "vec", vtok)
                dma("gpsimd", ATT1[tt * 128:(tt + 1) * 128, foc * 512:(foc + 1) * 512], oslot)
            wait("vector", "qG")

    barrier()

    for grp in range(4):
        pe_g = cnt["pe"]
        for ti in range(4):
            tt = grp * 4 + ti
            qe = ("sync", "scalar")[ti % 2]
            wait(qe, "pe", pe_g)
            for ddg in range(NF):
                dma(qe, AR1[:, ddg * 512 + ti * 128: ddg * 512 + (ti + 1) * 128],
                    QINT[tt * 128:(tt + 1) * 128, ddg * 128:(ddg + 1) * 128],
                    transpose=True)
        t_grpS, t_grpA = cnt["qS"], cnt["qA"]
        slot_tok = [0, 0]
        for foc in range(4):
            wait("tensor", "qS", t_grpS)
            wait("tensor", "qA", t_grpA)
            for ddg in range(NF):
                slot = WR[:, 8192 + (ddg % 2) * 512: 8192 + (ddg % 2) * 512 + 512]
                qe = ("sync", "scalar")[ddg % 2]
                wait(qe, "pe", slot_tok[ddg % 2])
                dtok = dma(qe, slot, WDQT[ddg * 128:(ddg + 1) * 128,
                                          foc * 512:(foc + 1) * 512])
                wait("tensor", "qS" if qe == "sync" else "qA", dtok)
                for ti in range(4):
                    if ddg == 0:
                        bank_wait(ti)
                    mmr = qT.matmul(out=psum[ti][:],
                                    lhsT=AR1[:, ddg * 512 + ti * 128: ddg * 512 + (ti + 1) * 128],
                                    rhs=slot, start=(ddg == 0), stop=(ddg == NF - 1))
                    if ti == 3:
                        slot_tok[ddg % 2] = sig("tensor", mmr, "pe")
            for ti in range(4):
                tt = grp * 4 + ti
                xtok = dma("sync", wldA[:, 0:512],
                           X1[tt * 128:(tt + 1) * 128, foc * 512:(foc + 1) * 512])
                atok = dma("sync", wldA[:, 512:1024],
                           ATT1[tt * 128:(tt + 1) * 128, foc * 512:(foc + 1) * 512])
                wait("vector", "pe")
                wait("vector", "qS", atok)
                oslot = tmpB[:, (ti % 4) * 512:(ti % 4) * 512 + 512]
                qV.tensor_tensor(out=oslot, in0=psum[ti][:], in1=wldA[:, 0:512], op=OP.add)
                qV.tensor_tensor(out=oslot, in0=oslot, in1=wldA[:, 512:1024], op=OP.add)
                vtok = sig("vector", qV.drain(), "vec")
                bank_mark(ti, "vec")
                wait("gpsimd", "vec", vtok)
                dma("gpsimd", out_h[tt * 128:(tt + 1) * 128, foc * 512:(foc + 1) * 512],
                    oslot)
                wait("sync", "qG")
            wait("vector", "qG")

    wait("gpsimd", "qG")


    src_map = {"x": x_in, "gsp": GSP, "x1": X1, "ffn0": FFN0, "att1": ATT1,
               "qts": None}
    if DEV_TAIL == "gsp":
        for tt in range(NT):
            dma("sync", out_h[tt * 128:(tt + 1) * 128, :],
                GSP[tt * 128:(tt + 1) * 128, 0:2048])
    elif DEV_TAIL in ("x1", "ffn0", "att1"):
        t = src_map[DEV_TAIL]
        for tt in range(NT):
            dma("sync", out_h[tt * 128:(tt + 1) * 128, :], t[tt * 128:(tt + 1) * 128, :])
    elif DEV_TAIL in ("qts", "kts", "vts"):
        _src = {"qts": QTs, "kts": KTs, "vts": VTs}[DEV_TAIL]
        for tt in range(NT):
            dma("gpsimd", out_h[tt * 128:(tt + 1) * 128, 0:2048],
                _src[tt * 128:(tt + 1) * 128, 0:2048])
        wait("gpsimd", "qG")
    elif DEV_TAIL in ("wgq", "wuq", "wdq"):
        _srcq = {"wgq": WGQT, "wuq": WUQT, "wdq": WDQT}[DEV_TAIL]
        for tt in range(NT):
            dma("gpsimd", out_h[tt * 128:(tt + 1) * 128, :], _srcq[tt * 128:(tt + 1) * 128, 0:2048])
        wait("gpsimd", "qG")
    elif DEV_TAIL == "qx":
        for tt in range(NT):
            dma("gpsimd", out_h[tt * 128:(tt + 1) * 128, 0:1024],
                GSP[tt * 128:(tt + 1) * 128, 0:1024])
        wait("gpsimd", "qG")
    elif DEV_TAIL == "gup":
        dma("gpsimd", out_h[0:128, 0:2048], GSP[0:128, 0:2048])
        dma("gpsimd", out_h[128:256, 0:1024], GSP[128:256, 0:1024])
        dma("gpsimd", out_h[128:256, 1024:1152], GSP[128:256, 1024:1152])
        wait("gpsimd", "qG")
    elif DEV_TAIL == "qint":
        for tt in range(NT):
            dma("gpsimd", out_h[tt * 128:(tt + 1) * 128, :], QINT[tt * 128:(tt + 1) * 128, 0:2048])
        wait("gpsimd", "qG")
    elif DEV_TAIL == "w2q":
        for tt in range(NT):
            dma("gpsimd", out_h[tt * 128:(tt + 1) * 128, :], W2Q[tt * 128:(tt + 1) * 128, :])
        wait("gpsimd", "qG")
    elif DEV_TAIL == "xq2b":
        for tt in range(8):
            wait("sync", "vec")
            wait("sync", "qG")
            tok = dma("sync", wldA[:], GSP[tt * 128:(tt + 1) * 128, 0:2048])
            wait("vector", "qS", tok)
            qV.tensor_scalar(out=tmpA[:], in0=wldA[:], scalar1=stat[:, tt * 8 + 2:tt * 8 + 3],
                             scalar2=stat[:, tt * 8 + 1:tt * 8 + 2], op0=OP.subtract, op1=OP.mult)
            qV.tensor_copy(out=IH[:], in_=tmpA[:])
            qV.tensor_scalar(out=tmpB[:], in0=IH[:], scalar1=127.0, scalar2=-128.0,
                             op0=OP.min, op1=OP.max)
            vtok = sig("vector", qV.drain(), "vec")
            wait("gpsimd", "vec", vtok)
            g1 = dma("gpsimd", out_h[tt * 128:(tt + 1) * 128, :], tmpB[:])
            g2 = dma("gpsimd", out_h[1024 + tt * 128:1024 + (tt + 1) * 128, :], wldA[:])
            wait("vector", "qG", g2)
    elif DEV_TAIL == "xq2":
        for tt in range(NT):
            wait("sync", "vec")
            wait("sync", "qG")
            tok = dma("sync", wldA[:], GSP[tt * 128:(tt + 1) * 128, 0:2048])
            wait("vector", "qS", tok)
            qV.tensor_scalar(out=tmpA[:], in0=wldA[:], scalar1=stat[:, tt * 8 + 2:tt * 8 + 3],
                             scalar2=stat[:, tt * 8 + 1:tt * 8 + 2], op0=OP.subtract, op1=OP.mult)
            qV.tensor_copy(out=IH[:], in_=tmpA[:])
            qV.tensor_scalar(out=tmpB[:], in0=IH[:], scalar1=127.0, scalar2=-128.0,
                             op0=OP.min, op1=OP.max)
            vtok = sig("vector", qV.drain(), "vec")
            wait("gpsimd", "vec", vtok)
            g2 = dma("gpsimd", out_h[tt * 128:(tt + 1) * 128, :], tmpB[:])
            wait("vector", "qG", g2)
    elif DEV_TAIL == "xq1t":
        for ddx in range(ND):
            dma("gpsimd", out_h[ddx * 128:(ddx + 1) * 128, :], AR1[:, ddx * S:(ddx + 1) * S])
        wait("gpsimd", "qG")
    elif DEV_TAIL == "srow":
        dma("gpsimd", out_h[0:1, 0:S], srow[:])
        dma("gpsimd", out_h[2:3, 0:S], srow2[:])
        dma("gpsimd", out_h[4:132, 0:NT * 8], stat[:])
        dma("gpsimd", out_h[140:268, 0:NT], gmax[:])
        dma("gpsimd", out_h[270:398, 0:NT], gmin[:])
        wait("gpsimd", "qG")
    elif DEV_TAIL == "ye":
        dma("gpsimd", out_h[0:128, 0:2048], AR2[:, 12288:14336])
        dma("gpsimd", out_h[128:256, 0:2048], AR2[:, 14336:16384])
        wait("gpsimd", "qG")
    elif DEV_TAIL in ("none", "ar2g", "grp0"):
        pass
    else:
        for tt in range(NT):
            dma("sync", out_h[tt * 128:(tt + 1) * 128, :], x_in[tt * 128:(tt + 1) * 128, :])
    wait("sync", "qS")

    blk = nc.Block().__enter__()
    for ename in ENGS:
        getattr(blk, ename)(lambda e, _q=Q[ename]: _q.replay(e))
    blk.__exit__(None, None, None)
    for cm in reversed(ctxs):
        cm.__exit__(None, None, None)
    return nc


def _stage(inputs):
    A = {k: np.ascontiguousarray(v) for k, v in inputs.items()}
    wf = np.concatenate([A["L1_attn_w_q"], A["L1_attn_w_k"], A["L1_attn_w_v"]], axis=0)
    common = dict(
        WqT=np.ascontiguousarray(A["L0_attn_Wq"].T), bq=A["L0_attn_bq"].reshape(1, D),
        WkT=np.ascontiguousarray(A["L0_attn_Wk"].T), bk=A["L0_attn_bk"].reshape(1, D),
        WvT=np.ascontiguousarray(A["L0_attn_Wv"].T), bv=A["L0_attn_bv"].reshape(1, D),
        WoT=np.ascontiguousarray(A["L0_attn_Wo"].T), bo=A["L0_attn_bo"].reshape(1, D),
        W1T=np.ascontiguousarray(A["L0_ffn_w1"].T), b1=A["L0_ffn_b1"].reshape(1, DFF),
        W2T=np.ascontiguousarray(A["L0_ffn_w2"].T), b2=A["L0_ffn_b2"].reshape(1, D),
        WfT=np.ascontiguousarray(wf.T), Wf_r=wf,
        Wo1T=np.ascontiguousarray(A["L1_attn_w_o"].T), Wo1_r=A["L1_attn_w_o"],
        qgain=A["L1_attn_q_gain"].reshape(1, NH),
        Wg_r=A["L1_ffn_w_gate"], Wu_r=A["L1_ffn_w_up"], Wd_r=A["L1_ffn_w_down"],
    )
    xs = A["L0_attn_x"]
    return [dict(common, x=np.ascontiguousarray(xs[i])) for i in range(8)]


def kernel(**inputs):
    nc = build()
    in_maps = _stage(inputs)
    res = run_bass_kernel_spmd(nc, in_maps, core_ids=list(range(8)))
    out = np.stack([np.asarray(r["out"]) for r in res.results], axis=0)
    return out.astype(np.float32)
